# version 01
# baseline (speedup 1.0000x reference)
import numpy as np
import ml_dtypes
import contextlib
import concourse.bass as bass
import concourse.mybir as mybir
from concourse.bass_utils import run_bass_kernel_spmd

F32 = mybir.dt.float32
BF16 = mybir.dt.bfloat16
I32 = mybir.dt.int32
AF = mybir.ActivationFunctionType
ALU = mybir.AluOpType
AX = mybir.AxisListType.X

B, S, C, H, HD = 4, 4096, 1024, 16, 64
TOK = 2048
NT = TOK // 128
QKW = 2048

OFF_NAMES = []
for _l in range(3):
    OFF_NAMES += [f"qk{_l}_mine", f"v{_l}_mine",
                  f"qq{_l}_ev", f"qq{_l}_od", f"kk{_l}_ev", f"kk{_l}_od",
                  f"v{_l}_ev", f"v{_l}_od",
                  f"att{_l}_mine", f"att{_l}_mine_myhalf", f"att{_l}_part_myhalf"]
OI = {n: i for i, n in enumerate(OFF_NAMES)}


def _ap(t, off, dims):
    return bass.AP(t, off, [list(d) for d in dims])


class Ctx:
    def __init__(self, nc):
        self.nc = nc
        self.sems = {}
        self.cnt = {}

    def s(self, name):
        if name not in self.sems:
            self.sems[name] = self.nc.ctx.enter_context(self.nc.semaphore(name))
            self.cnt[name] = 0
        return self.sems[name]

    def c(self, name):
        self.s(name)
        return self.cnt[name]

    def add(self, name, amt):
        self.s(name)
        self.cnt[name] += amt
        return self.cnt[name]


def build_program(upto=4):
    import os
    KDBG = set(os.environ.get("KDBG", "").split(",")) - {""}
    nc = bass.Bass("TRN2", target_bir_lowering=False)
    cx = Ctx(nc)
    din = {}
    _uq = [0]

    def un(nm):
        _uq[0] += 1
        return f"{nm}_{_uq[0]}"

    def dbg_dump(name, sb, parts, free, dtype):
        if name not in KDBG:
            return
        dt_ = nc.dram_tensor("dbg_" + name, [parts, free], dtype, kind="ExternalOutput")
        with nc.Block() as blk:
            @blk.sync
            def _(s):
                s.dma_start(out=dt_[:, :], in_=sb[0:parts, 0:free]).then_inc(cx.s("ST"), 16)
                s.wait_ge(cx.s("ST"), cx.add("ST", 16))

    def dram_in(name, shape, dtype=F32):
        din[name] = nc.dram_tensor(name, list(shape), dtype, kind="ExternalInput")
        return din[name]

    x_in = dram_in("x_own", [TOK, C])
    out_t = nc.dram_tensor("out", [TOK, C], F32, kind="ExternalOutput")

    dram_in("ident", [128, 128], BF16)
    dram_in("masks01", [128, 4 * 512], BF16)
    dram_in("offs_in", [1, 64], I32)

    dram_in("w_qk1", [C, 128])
    dram_in("w_qk2", [C, 256])
    dram_in("w_v0", [C, C])
    dram_in("w_o0", [C, C])
    dram_in("bo0", [1, C])
    dram_in("ln1_w", [1, C])
    dram_in("ln2_w", [1, C])
    dram_in("w_q1f", [C, C])
    dram_in("w_kv", [C, 256])
    dram_in("w_kup", [256, C])
    dram_in("w_vup", [256, C])
    dram_in("w_o2", [C, C])
    dram_in("w_r", [C, 8])
    dram_in("r_bias", [1, 8])
    dram_in("sw1", [C, 2048])
    dram_in("sw2", [2048, C])
    dram_in("sw3", [C, 2048])
    dram_in("ew1", [8, C, 2048])
    dram_in("ew2", [8, 2048, C])
    dram_in("ew3", [8, C, 2048])
    dram_in("ccss", [128, 2 * TOK], BF16)
    dram_in("pswap", [128, 128], BF16)
    dram_in("w_q2a", [C, C])
    dram_in("w_k2a", [C, C])
    dram_in("w_v2a", [C, C])
    dram_in("w_o3", [C, C])
    dram_in("bo3", [1, C])
    dram_in("w_gate", [C, 8])
    dram_in("gate_b", [1, 8])
    dram_in("rms_w", [1, 8])
    dram_in("fw1", [8, C, 2048])
    dram_in("fb1", [8, 2048])
    dram_in("fw2", [8, 2048, C])
    dram_in("fb2", [8, C])

    XQK0 = nc.dram_tensor("XQK0", [8 * 1024, QKW], BF16, kind="Internal", addr_space="Shared")
    XV0 = nc.dram_tensor("XV0", [8 * 2048, 1040], BF16, kind="Internal", addr_space="Shared")
    XATT0 = nc.dram_tensor("XATT0", [8 * 512, 4096], BF16, kind="Internal", addr_space="Shared")
    XQK1 = nc.dram_tensor("XQK1", [8 * 2048, QKW], BF16, kind="Internal", addr_space="Shared")
    XV1 = nc.dram_tensor("XV1", [8 * 2048, 1040], BF16, kind="Internal", addr_space="Shared")
    XATT1 = nc.dram_tensor("XATT1", [8 * 512, 4096], BF16, kind="Internal", addr_space="Shared")
    XQK2 = nc.dram_tensor("XQK2", [8 * 2048, QKW], BF16, kind="Internal", addr_space="Shared")
    XV2 = nc.dram_tensor("XV2", [8 * 2048, 1040], BF16, kind="Internal", addr_space="Shared")
    XATT2 = nc.dram_tensor("XATT2", [8 * 512, 4096], BF16, kind="Internal", addr_space="Shared")
    xspill = nc.dram_tensor("xspill", [TOK, C], F32, kind="Internal")
    bars = [(nc.dram_tensor(f"bar_i{i}", [1, 2], F32),
             nc.dram_tensor(f"bar_o{i}", [1, 2], F32)) for i in range(8)]
    nbar = [0]

    es = nc.ctx
    resid = es.enter_context(nc.sbuf_tensor(un("resid"), [128, NT * C], F32))
    ident = es.enter_context(nc.sbuf_tensor(un("ident_sb"), [128, 128], BF16))
    offs = es.enter_context(nc.sbuf_tensor(un("offs_sb"), [1, 64], I32))
    ones_f = es.enter_context(nc.sbuf_tensor(un("ones_f"), [1, 128], F32))
    eps_ln = es.enter_context(nc.sbuf_tensor(un("eps_ln"), [128, 1], F32))
    eps_rms = es.enter_context(nc.sbuf_tensor(un("eps_rms"), [128, 1], F32))
    eps_g = es.enter_context(nc.sbuf_tensor(un("eps_g"), [128, 1], F32))
    stat = es.enter_context(nc.sbuf_tensor(un("stat"), [128, 16], F32))
    barz = es.enter_context(nc.sbuf_tensor(un("barz"), [1, 2], F32))

    LD, GD, ST, CC = "LD", "GD", "ST", "CC"

    with nc.Block() as blk:
        @blk.sync
        def _(s):
            s.dma_start(
                out=_ap(resid, 0, [[NT * C, 128], [C, NT], [1, C]]),
                in_=_ap(x_in, 0, [[C, 128], [128 * C, NT], [1, C]]),
            ).then_inc(cx.s(LD), 16)
            cx.add(LD, 16)
            for nm, dst in (("ident", ident), ("offs_in", offs)):
                s.dma_start(out=dst[:, :], in_=din[nm][:, :]).then_inc(cx.s(LD), 16)
                cx.add(LD, 16)

        @blk.vector
        def _(v):
            v.memset(ones_f[:, :], 1.0)
            v.memset(eps_ln[:, :], 1.0e-5)
            v.memset(eps_rms[:, :], 1.0e-6)
            v.memset(eps_g[:, :], float(np.finfo(np.float32).eps))
            v.memset(barz[:, :], 0.0)

    def barrier(g):
        i = nbar[0]
        nbar[0] += 1
        bi, bo = bars[i]
        g.dma_start(out=bi[:, :], in_=barz[:, :]).then_inc(cx.s(ST), 16)
        g.wait_ge(cx.s(ST), cx.add(ST, 16))
        g.collective_compute("AllReduce", ALU.add, replica_groups=[list(range(8))],
                             ins=[bi.ap().opt()], outs=[bo.ap().opt()]).then_inc(cx.s(CC), 1)
        g.wait_ge(cx.s(CC), cx.add(CC, 1))

    def fence(eng, ins, nm):
        ins.then_inc(cx.s(nm), 1)
        eng.wait_ge(cx.s(nm), cx.add(nm, 1))

    def ln_act_tile(a, src, t, sqscr, sv0, eps_tile):
        b = 8 * (t % 2)
        a.activation(sqscr, src, AF.Square,
                     accum_out=stat[:, b + 1:b + 2]).then_inc(cx.s("SA"), 1)
        a.wait_ge(cx.s("SV"), sv0 + t + 1)
        fence(a, a.activation(stat[:, b + 5:b + 6], stat[:, b + 3:b + 4], AF.Ln,
                              bias=eps_tile), "FA")
        a.activation(stat[:, b + 4:b + 5], stat[:, b + 5:b + 6], AF.Exp,
                     scale=-0.5).then_inc(cx.s("SR"), 1)

    def ln_dve_stats(v, src, t, sa0):
        b = 8 * (t % 2)
        v.wait_ge(cx.s("SA"), sa0 + t + 1)
        fence(v, v.reduce_sum(stat[:, b:b + 1], src, AX), "FV")
        fence(v, v.tensor_scalar_mul(stat[:, b + 2:b + 3], stat[:, b:b + 1], 1.0 / C), "FV")
        fence(v, v.tensor_scalar_mul(stat[:, b + 5:b + 6], stat[:, b + 1:b + 2], 1.0 / C),
              "FV")
        fence(v, v.tensor_tensor(stat[:, b + 3:b + 4], stat[:, b + 2:b + 3],
                                 stat[:, b + 2:b + 3], ALU.mult), "FV")
        v.tensor_sub(stat[:, b + 3:b + 4], stat[:, b + 5:b + 6],
                     stat[:, b + 3:b + 4]).then_inc(cx.s("SV"), 1)

    def phase_l0attn_pre(ph):
        import os
        ksub = int(os.environ.get("KSUB", "3"))
        afm = ph.enter_context(nc.sbuf_tensor(un("afm"), [128, 8 * 2048], BF16))
        abf = ph.enter_context(nc.sbuf_tensor(un("abf"), [128, 2 * C], BF16))
        sqs = ph.enter_context(nc.sbuf_tensor(un("sqs"), [128, C], BF16))
        wqk1 = ph.enter_context(nc.sbuf_tensor(un("wqk1"), [128, 8 * 128], BF16))
        wqk2 = ph.enter_context(nc.sbuf_tensor(un("wqk2"), [128, 8 * 256], BF16))
        wv = ph.enter_context(nc.sbuf_tensor(un("wv"), [128, 8 * 1024], BF16))
        qk1t = ph.enter_context(nc.sbuf_tensor(un("qk1t"), [128, 2 * 128], F32))
        qk2t = ph.enter_context(nc.sbuf_tensor(un("qk2t"), [128, 2 * 256], F32))
        q12t = ph.enter_context(nc.sbuf_tensor(un("q12t"), [128, 2 * 512], BF16))
        k12t = ph.enter_context(nc.sbuf_tensor(un("k12t"), [128, 2 * 512], BF16))
        q12fm = ph.enter_context(nc.sbuf_tensor(un("q12fm"), [128, 4 * 2048], BF16))
        k12fm = ph.enter_context(nc.sbuf_tensor(un("k12fm"), [128, 4 * 2048], BF16))
        vstage = ph.enter_context(nc.sbuf_tensor(un("vstage"), [128, 16 * 1040], BF16))
        ptr = ph.enter_context(nc.psum_tensor(un("ptr"), [128, 1024], BF16))
        pqk = ph.enter_context(nc.psum_tensor(un("pqk"), [128, 2 * 512], F32))
        pv = ph.enter_context(nc.psum_tensor(un("pv"), [128, 4 * 512], F32))
        ptqk = ph.enter_context(nc.psum_tensor(un("ptqk"), [128, 1024], BF16))

        ld_t = cx.c(LD)
        sa0, sv0, sr0, sb0, sc0, sd0 = (cx.c(x) for x in ("SA", "SV", "SR", "SB", "SC", "SD"))
        with nc.Block() as blk:
            @blk.gpsimd
            def _(g):
                for wname, dst, wid in (("w_qk1", wqk1, 128), ("w_qk2", wqk2, 256),
                                        ("w_v0", wv, 1024)):
                    g.dma_start(
                        out=_ap(dst, 0, [[8 * wid, 128], [wid, 8], [1, wid]]),
                        in_=_ap(din[wname], 0, [[wid, 128], [128 * wid, 8], [1, wid]]),
                    ).then_inc(cx.s(GD), 16)
                    cx.add(GD, 16)

            @blk.scalar
            def _(a):
                a.wait_ge(cx.s(LD), ld_t)
                kser = int(os.environ.get("KSER", "0"))
                for t in range(NT):
                    if kser and t >= 1:
                        a.wait_ge(cx.s("SD"), sd0 + t)
                    ln_act_tile(a, resid[:, C * t:C * (t + 1)], t, sqs[:, :], sv0,
                                eps_ln[:, :])

            @blk.vector
            def _(v):
                v.memset(vstage[:, :], 1.0)
                for t in range(NT):
                    b = 8 * (t % 2)
                    xt = resid[:, C * t:C * (t + 1)]
                    ln_dve_stats(v, xt, t, sa0)
                    v.wait_ge(cx.s("SR"), sr0 + t + 1)
                    v.tensor_scalar(abf[:, C * (t % 2):C * (t % 2 + 1)], xt,
                                    stat[:, b + 2:b + 3], stat[:, b + 4:b + 5],
                                    ALU.subtract, ALU.mult).then_inc(cx.s("SB"), 1)
                    v.wait_ge(cx.s("SC"), sc0 + t + 1)
                    v.tensor_copy(
                        _ap(afm, 128 * t, [[8 * 2048, 128], [2048, 8], [1, 128]]),
                        _ap(ptr, 0, [[1024, 128], [128, 8], [1, 128]]),
                    ).then_inc(cx.s("SD"), 1)

            @blk.tensor
            def _(te):
                for t in range(NT):
                    pp = t % 2
                    te.wait_ge(cx.s("SB"), sb0 + t + 1)
                    if t >= 1:
                        te.wait_ge(cx.s("SD"), sd0 + t)
                    ins = None
                    for r in range(8):
                        ins = te.transpose(
                            ptr[:, 128 * r:128 * (r + 1)],
                            abf[:, C * pp + 128 * r:C * pp + 128 * (r + 1)], ident[:, :])
                    ins.then_inc(cx.s("SC"), 1)
        for nm in ("SA", "SV", "SR", "SB", "SC", "SD"):
            cx.add(nm, NT)

        if ksub < 2:
            dbg_dump("afm", afm, 128, 8 * 2048, BF16)
            return
        gd_t = cx.c(GD)
        sp0, sq0, st20 = cx.c("SP"), cx.c("SQ"), cx.c("ST2")
        sm0, sn0 = cx.c("SM"), cx.c("SN")
        with nc.Block() as blk:
            @blk.tensor
            def _(te):
                te.wait_ge(cx.s(GD), gd_t)
                for t in range(NT):
                    pp = t % 2
                    if t >= 2:
                        te.wait_ge(cx.s("SM"), sm0 + 3 * (t - 1))
                    ins = None
                    for r in range(8):
                        lt = afm[:, 2048 * r + 128 * t:2048 * r + 128 * (t + 1)]
                        ins = te.matmul(pqk[:, 512 * pp:512 * pp + 128], lt,
                                        wqk1[:, 128 * r:128 * (r + 1)],
                                        start=(r == 0), stop=(r == 7))
                    ins.then_inc(cx.s("SP"), 1)
                    ins = None
                    for r in range(8):
                        lt = afm[:, 2048 * r + 128 * t:2048 * r + 128 * (t + 1)]
                        ins = te.matmul(pqk[:, 512 * pp + 128:512 * pp + 384], lt,
                                        wqk2[:, 256 * r:256 * (r + 1)],
                                        start=(r == 0), stop=(r == 7))
                    ins.then_inc(cx.s("SP"), 1)
                    for dh in range(2):
                        ins = None
                        for r in range(8):
                            lt = afm[:, 2048 * r + 128 * t:2048 * r + 128 * (t + 1)]
                            ins = te.matmul(pv[:, 512 * (2 * pp + dh):512 * (2 * pp + dh + 1)],
                                            lt,
                                            wv[:, 1024 * r + 512 * dh:1024 * r + 512 * (dh + 1)],
                                            start=(r == 0), stop=(r == 7))
                        ins.then_inc(cx.s("SP"), 1)
                    te.wait_ge(cx.s("SQ"), sq0 + t + 1)
                    if t >= 1:
                        te.wait_ge(cx.s("SN"), sn0 + t)
                    pq = 512 * pp
                    for u in range(4):
                        te.transpose(ptqk[:, 128 * u:128 * (u + 1)],
                                     q12t[:, pq + 128 * u:pq + 128 * (u + 1)], ident[:, :])
                    ins = None
                    for u in range(4):
                        ins = te.transpose(ptqk[:, 512 + 128 * u:512 + 128 * (u + 1)],
                                           k12t[:, pq + 128 * u:pq + 128 * (u + 1)], ident[:, :])
                    ins.then_inc(cx.s("ST2"), 1)

            @blk.vector
            def _(v):
                for t in range(NT):
                    pp = t % 2
                    o1, o2, oq = 128 * pp, 256 * pp, 512 * pp
                    v.wait_ge(cx.s("SP"), sp0 + 4 * t + 2)
                    fence(v, v.tensor_copy(qk1t[:, o1:o1 + 128],
                                           pqk[:, 512 * pp:512 * pp + 128]), "FV")
                    fence(v, v.tensor_copy(qk2t[:, o2:o2 + 256],
                                           pqk[:, 512 * pp + 128:512 * pp + 384]), "FV")
                    v.engine_nop().then_inc(cx.s("SM"), 1)
                    v.tensor_tensor(
                        _ap(q12t, oq, [[2 * 512, 128], [32, 16], [8, 4], [1, 8]]),
                        _ap(qk2t, o2, [[2 * 256, 128], [8, 16], [0, 4], [1, 8]]),
                        _ap(qk1t, o1, [[2 * 128, 128], [4, 16], [1, 4], [0, 8]]),
                        ALU.mult)
                    v.tensor_tensor(
                        _ap(k12t, oq, [[2 * 512, 128], [32, 16], [8, 4], [1, 8]]),
                        _ap(qk2t, o2 + 128, [[2 * 256, 128], [8, 16], [0, 4], [1, 8]]),
                        _ap(qk1t, o1 + 64, [[2 * 128, 128], [4, 16], [1, 4], [0, 8]]),
                        ALU.mult).then_inc(cx.s("SQ"), 1)
                    for dh in range(2):
                        v.wait_ge(cx.s("SP"), sp0 + 4 * t + 3 + dh)
                        v.tensor_copy(
                            _ap(vstage, 1040 * t + 520 * dh,
                                [[16 * 1040, 128], [130, 4], [65, 2], [1, 64]]),
                            _ap(pv, 512 * (2 * pp + dh),
                                [[4 * 512, 128], [128, 4], [64, 2], [1, 64]]),
                        ).then_inc(cx.s("SM"), 1)
                    v.wait_ge(cx.s("ST2"), st20 + t + 1)
                    fence(v, v.tensor_copy(
                        _ap(q12fm, 128 * t, [[4 * 2048, 128], [2048, 4], [1, 128]]),
                        _ap(ptqk, 0, [[1024, 128], [128, 4], [1, 128]])), "FV")
                    v.tensor_copy(
                        _ap(k12fm, 128 * t, [[4 * 2048, 128], [2048, 4], [1, 128]]),
                        _ap(ptqk, 512, [[1024, 128], [128, 4], [1, 128]])).then_inc(cx.s("SN"), 1)
        cx.add("SP", 4 * NT)
        cx.add("SQ", NT)
        cx.add("ST2", NT)
        cx.add("SM", 3 * NT)
        cx.add("SN", NT)

        if ksub < 3:
            return
        sn_t = cx.c("SN")
        ld_t2 = cx.c(LD)
        with nc.Block() as blk:
            @blk.gpsimd
            def _(g):
                g.wait_ge(cx.s("SN"), sn_t)
                g.wait_ge(cx.s(LD), ld_t2)
                with g.register(un("r0")) as r0, g.register(un("r1")) as r1:
                    g.reg_load(r0, offs[0:1, OI["qk0_mine"]:OI["qk0_mine"] + 1])
                    g.dma_start(
                        out=_ap(XQK0, r0, [[QKW, 128], [128 * QKW, 4], [1, 2048]]),
                        in_=_ap(q12fm, 0, [[4 * 2048, 128], [2048, 4], [1, 2048]]),
                    ).then_inc(cx.s(ST), 16)
                    cx.add(ST, 16)
                    g.reg_add(r1, r0, 512 * QKW)
                    g.dma_start(
                        out=_ap(XQK0, r1, [[QKW, 128], [128 * QKW, 4], [1, 2048]]),
                        in_=_ap(k12fm, 0, [[4 * 2048, 128], [2048, 4], [1, 2048]]),
                    ).then_inc(cx.s(ST), 16)
                    cx.add(ST, 16)
                    g.reg_load(r0, offs[0:1, OI["v0_mine"]:OI["v0_mine"] + 1])
                    g.dma_start(
                        out=_ap(XV0, r0, [[1040, 128], [128 * 1040, 16], [1, 1040]]),
                        in_=_ap(vstage, 0, [[16 * 1040, 128], [1040, 16], [1, 1040]]),
                    ).then_inc(cx.s(ST), 16)
                    cx.add(ST, 16)
                g.wait_ge(cx.s(ST), cx.c(ST))
                barrier(g)
        dbg_dump("afm", afm, 128, 8 * 2048, BF16)
        dbg_dump("q12fm", q12fm, 128, 4 * 2048, BF16)
        dbg_dump("k12fm", k12fm, 128, 4 * 2048, BF16)
        dbg_dump("vstage", vstage, 128, 16 * 1040, BF16)

    def phase_attn_core(ph, rk, causal, scale, xqk, xv, xatt,
                        k_qq, k_kk, k_vv, k_att_mine):
        kall = ph.enter_context(nc.sbuf_tensor(un("kall"), [2 * rk, 4 * 4096], BF16))
        qall = ph.enter_context(nc.sbuf_tensor(un("qall"), [2 * rk, 4 * 4096], BF16))
        vallA = ph.enter_context(nc.sbuf_tensor(un("vallA"), [128, 16 * 520], BF16))
        vallB = ph.enter_context(nc.sbuf_tensor(un("vallB"), [128, 16 * 520], BF16))
        att = ph.enter_context(nc.sbuf_tensor(un("att"), [128, 4 * 4096], BF16))
        eT = ph.enter_context(nc.sbuf_tensor(un("eT"), [128, 2 * 512], BF16))
        zi = ph.enter_context(nc.sbuf_tensor(un("zi"), [1, 512], F32))
        zr = ph.enter_context(nc.sbuf_tensor(un("zr"), [64, 512], F32))
        pS = ph.enter_context(nc.psum_tensor(un("pS"), [128, 2 * 512], F32))
        pAV = ph.enter_context(nc.psum_tensor(un("pAV"), [128, 2 * 512], F32))
        pZ = ph.enter_context(nc.psum_tensor(un("pZ"), [64, 512], F32))
        masks = ph.enter_context(nc.sbuf_tensor(un("masks_sb"), [128, 4 * 512], BF16))
        ld_m = None
        if causal:
            with nc.Block() as blk0:
                @blk0.sync
                def _(s):
                    s.dma_start(out=masks[:, :], in_=din["masks01"][:, :]).then_inc(
                        cx.s(LD), 16)
                    cx.add(LD, 16)
            ld_m = cx.c(LD)

        iters = []
        for p in range(4):
            for hh in range(2):
                for s in range(8):
                    iters.append((p, hh, s, (4 * s + 4) if causal else 32))
        nit = len(iters)
        ss_b = [0]
        for (_, _, _, nk) in iters:
            ss_b.append(ss_b[-1] + nk)
        sdm_b = [0]
        for (_, _, s, nk) in iters:
            sdm_b.append(sdm_b[-1] + (4 if causal else 0))

        ss0, se0, sav0, sdm0 = cx.c("SS"), cx.c("SE"), cx.c("SAV"), cx.c("SDM")
        szi0, szr0, szc0, snr0 = cx.c("SZI"), cx.c("SZR"), cx.c("SZC"), cx.c("SNR")
        gd0, st0 = cx.c(GD), cx.c(ST)

        with nc.Block() as blk:
            @blk.scalar
            def _(a):
                with a.register(un("sra")) as ra:
                    for rs in (0, 1):
                        a.reg_load(ra, offs[0:1, k_qq[rs]:k_qq[rs] + 1])
                        a.dma_start(
                            out=_ap(qall, 2048 * rs,
                                    [[4 * 4096, 2 * rk], [4096, 4], [1, 2048]]),
                            in_=_ap(xqk, ra,
                                    [[QKW, 2 * rk], [QKW * 2 * rk, 4], [1, 2048]]),
                        ).then_inc(cx.s(GD), 16)
                        cx.add(GD, 16)
                        a.reg_load(ra, offs[0:1, k_kk[rs]:k_kk[rs] + 1])
                        a.dma_start(
                            out=_ap(kall, 2048 * rs,
                                    [[4 * 4096, 2 * rk], [4096, 4], [1, 2048]]),
                            in_=_ap(xqk, ra,
                                    [[QKW, 2 * rk], [QKW * 2 * rk, 4], [1, 2048]]),
                        ).then_inc(cx.s(GD), 16)
                        cx.add(GD, 16)
                        a.reg_load(ra, offs[0:1, k_vv[rs]:k_vv[rs] + 1])
                        a.dma_start(
                            out=_ap(vallA if rs == 0 else vallB, 0,
                                    [[16 * 520, 128], [520, 16], [1, 520]]),
                            in_=_ap(xv, ra,
                                    [[1040, 128], [128 * 1040, 16], [1, 520]]),
                        ).then_inc(cx.s(GD), 16)
                        cx.add(GD, 16)
                gd_all = cx.c(GD)
                for i, (p, hh, s, nk) in enumerate(iters):
                    for kc in range(nk):
                        a.wait_ge(cx.s("SS"), ss0 + ss_b[i] + kc + 1)
                        if kc >= 2:
                            a.wait_ge(cx.s("SAV"), sav0 + ss_b[i] + kc - 1)
                        else:
                            a.wait_ge(cx.s("SAV"), sav0 + ss_b[i])
                        a.activation(eT[:, 512 * (kc % 2):512 * (kc % 2 + 1)],
                                     pS[:, 512 * (kc % 2):512 * (kc % 2 + 1)],
                                     AF.Exp, scale=scale).then_inc(cx.s("SE"), 1)

            gd_all = gd0 + 96

            @blk.gpsimd
            def _(g):
                g.wait_ge(cx.s("SNR"), snr0 + 64)
                with g.register(un("ga")) as ra:
                    g.reg_load(ra, offs[0:1, k_att_mine:k_att_mine + 1])
                    g.dma_start(
                        out=_ap(xatt, ra, [[4096, 128], [128 * 4096, 4], [1, 4096]]),
                        in_=_ap(att, 0, [[4 * 4096, 128], [4096, 4], [1, 4096]]),
                    ).then_inc(cx.s(ST), 16)
                    cx.add(ST, 16)
                g.wait_ge(cx.s(ST), cx.c(ST))
                barrier(g)

            @blk.tensor
            def _(te):
                te.wait_ge(cx.s(GD), gd_all)
                for i, (p, hh, s, nk) in enumerate(iters):
                    bi = i % 2
                    if i >= 2:
                        te.wait_ge(cx.s("SNR"), snr0 + i - 1)
                    te.wait_ge(cx.s("SE"), se0 + ss_b[i])
                    for kc in range(nk):
                        if kc >= 2:
                            te.wait_ge(cx.s("SE"), se0 + ss_b[i] + kc - 1)
                        te.matmul(pS[:, 512 * (kc % 2):512 * (kc % 2 + 1)],
                                  kall[rk * hh:rk * (hh + 1),
                                       4096 * p + 128 * kc:4096 * p + 128 * (kc + 1)],
                                  qall[rk * hh:rk * (hh + 1),
                                       4096 * p + 512 * s:4096 * p + 512 * (s + 1)],
                                  start=True, stop=True).then_inc(cx.s("SS"), 1)
                        dj = kc - (nk - 4) if causal else -1
                        if causal and dj >= 0:
                            te.wait_ge(cx.s("SDM"), sdm0 + sdm_b[i] + dj + 1)
                        else:
                            te.wait_ge(cx.s("SE"), se0 + ss_b[i] + kc + 1)
                        vsrc = vallA if kc < 16 else vallB
                        vcol = 520 * (kc % 16) + 130 * p + 65 * hh
                        te.matmul(pAV[0:65, 512 * bi:512 * bi + 512],
                                  vsrc[:, vcol:vcol + 65],
                                  eT[:, 512 * (kc % 2):512 * (kc % 2 + 1)],
                                  start=(kc == 0), stop=(kc == nk - 1)).then_inc(
                            cx.s("SAV"), 1)
                    te.wait_ge(cx.s("SZI"), szi0 + i + 1)
                    te.wait_ge(cx.s("SZC"), szc0 + i)
                    te.matmul(pZ[:, :], ones_f[0:1, 0:64], zi[0:1, :],
                              start=True, stop=True).then_inc(cx.s("SZR"), 1)

            @blk.vector
            def _(v):
                if causal:
                    v.wait_ge(cx.s(LD), ld_m)
                for i, (p, hh, s, nk) in enumerate(iters):
                    bi = i % 2
                    if causal:
                        for dj in range(4):
                            kc = nk - 4 + dj
                            v.wait_ge(cx.s("SE"), se0 + ss_b[i] + kc + 1)
                            v.tensor_tensor(eT[:, 512 * (kc % 2):512 * (kc % 2 + 1)],
                                            eT[:, 512 * (kc % 2):512 * (kc % 2 + 1)],
                                            masks[:, 512 * dj:512 * (dj + 1)],
                                            ALU.mult).then_inc(cx.s("SDM"), 1)
                    v.wait_ge(cx.s("SAV"), sav0 + ss_b[i + 1])
                    v.reciprocal(zi[0:1, :], pAV[64:65, 512 * bi:512 * bi + 512]).then_inc(
                        cx.s("SZI"), 1)
                    v.wait_ge(cx.s("SZR"), szr0 + i + 1)
                    v.tensor_copy(zr[:, :], pZ[:, :]).then_inc(cx.s("SZC"), 1)
                    v.wait_ge(cx.s("SZC"), szc0 + i + 1)
                    v.tensor_tensor(att[64 * hh:64 * (hh + 1),
                                        4096 * p + 512 * s:4096 * p + 512 * (s + 1)],
                                    pAV[0:64, 512 * bi:512 * bi + 512],
                                    zr[:, :], ALU.mult).then_inc(cx.s("SNR"), 1)

        cx.add("SS", ss_b[nit])
        cx.add("SE", ss_b[nit])
        cx.add("SAV", ss_b[nit])
        cx.add("SDM", sdm_b[nit])
        for nm in ("SZI", "SZR", "SZC", "SNR"):
            cx.add(nm, nit)
        dbg_dump("att", att, 128, 4 * 4096, BF16)
        dbg_dump("qall", qall, 2 * rk, 4 * 4096, BF16)
        dbg_dump("kall", kall, 2 * rk, 4 * 4096, BF16)
        dbg_dump("vallA", vallA, 128, 16 * 520, BF16)
        dbg_dump("vallB", vallB, 128, 16 * 520, BF16)

    def phase_oproj_sandwich(ph, xatt, w_o_name, bias_name, mode,
                             k_att_mine_myhalf, k_att_part_myhalf):
        oin = ph.enter_context(nc.sbuf_tensor(un("oin"), [128, 8 * 2048], BF16))
        wo = ph.enter_context(nc.sbuf_tensor(un("wo"), [128, 8 * 1024], BF16))
        ysc = ph.enter_context(nc.sbuf_tensor(un("ysc"), [128, 2 * C], F32))
        sqs2 = ph.enter_context(nc.sbuf_tensor(un("sqs2"), [128, C], BF16))
        lnap = ph.enter_context(nc.sbuf_tensor(un("lnap"), [128, C], F32))
        brep = ph.enter_context(nc.sbuf_tensor(un("brep"), [128, C], F32))
        bsb = ph.enter_context(nc.sbuf_tensor(un("bsb"), [1, C], F32))
        py = ph.enter_context(nc.psum_tensor(un("py"), [128, 2 * 512], F32))
        pb = ph.enter_context(nc.psum_tensor(un("pb"), [128, 512], F32))

        with nc.Block() as blk:
            @blk.gpsimd
            def _(g):
                with g.register(un("ra")) as ra:
                    for half, idx in ((0, k_att_mine_myhalf), (1, k_att_part_myhalf)):
                        g.reg_load(ra, offs[0:1, idx:idx + 1])
                        g.dma_start(
                            out=_ap(oin, 4 * 2048 * half,
                                    [[8 * 2048, 128], [2048, 4], [1, 2048]]),
                            in_=_ap(xatt, ra, [[4096, 128], [128 * 4096, 4], [1, 2048]]),
                        ).then_inc(cx.s(GD), 16)
                        cx.add(GD, 16)
                g.dma_start(
                    out=_ap(wo, 0, [[8 * 1024, 128], [1024, 8], [1, 1024]]),
                    in_=_ap(din[w_o_name], 0, [[1024, 128], [128 * 1024, 8], [1, 1024]]),
                ).then_inc(cx.s(GD), 16)
                cx.add(GD, 16)

            @blk.sync
            def _(s):
                if bias_name is not None:
                    s.dma_start(out=bsb[:, :], in_=din[bias_name][:, :]).then_inc(
                        cx.s(LD), 16)
                    cx.add(LD, 16)
        gd_in = cx.c(GD)
        ld_b = cx.c(LD)

        nb = 2 if bias_name else 0
        sy0, sw0 = cx.c("SY"), cx.c("SW")
        sa0, sv0, sr0 = cx.c("SA"), cx.c("SV"), cx.c("SR")
        with nc.Block() as blk:
            @blk.tensor
            def _(te):
                te.wait_ge(cx.s(GD), gd_in)
                if bias_name is not None:
                    te.wait_ge(cx.s(LD), ld_b)
                    for dh in range(2):
                        te.matmul(pb[:, :], ones_f[0:1, 0:128],
                                  bsb[0:1, 512 * dh:512 * (dh + 1)],
                                  start=True, stop=True).then_inc(cx.s("SW"), 1)
                        te.wait_ge(cx.s("SY"), sy0 + dh + 1)
                for t in range(NT):
                    if t >= 1:
                        te.wait_ge(cx.s("SY"), sy0 + nb + 2 * t)
                    for dh in range(2):
                        ins = None
                        for j in range(8):
                            ins = te.matmul(
                                py[:, 512 * dh:512 * (dh + 1)],
                                oin[:, 2048 * j + 128 * t:2048 * j + 128 * (t + 1)],
                                wo[:, 1024 * j + 512 * dh:1024 * j + 512 * (dh + 1)],
                                start=(j == 0), stop=(j == 7))
                        ins.then_inc(cx.s("SW"), 1)

            @blk.vector
            def _(v):
                if bias_name is not None:
                    for dh in range(2):
                        v.wait_ge(cx.s("SW"), sw0 + dh + 1)
                        v.tensor_copy(brep[:, 512 * dh:512 * (dh + 1)], pb[:, :]).then_inc(
                            cx.s("SY"), 1)
                for t in range(NT):
                    pp = t % 2
                    b = 8 * pp
                    for dh in range(2):
                        v.wait_ge(cx.s("SW"), sw0 + nb + 2 * t + dh + 1)
                        dst = ysc[:, C * pp + 512 * dh:C * pp + 512 * (dh + 1)]
                        if bias_name is not None:
                            v.tensor_add(dst, py[:, 512 * dh:512 * (dh + 1)],
                                         brep[:, 512 * dh:512 * (dh + 1)]).then_inc(
                                cx.s("SY"), 1)
                        else:
                            v.tensor_copy(dst, py[:, 512 * dh:512 * (dh + 1)]).then_inc(
                                cx.s("SY"), 1)
                    yt = ysc[:, C * pp:C * (pp + 1)]
                    v.wait_ge(cx.s("SY"), sy0 + nb + 2 * t + 2)
                    if mode == "add":
                        v.tensor_add(resid[:, C * t:C * (t + 1)],
                                     resid[:, C * t:C * (t + 1)], yt)
                        continue
                    ln_dve_stats(v, yt, t, sa0)
                    v.wait_ge(cx.s("SR"), sr0 + t + 1)
                    fence(v, v.tensor_scalar(lnap[:, :], yt, stat[:, b + 2:b + 3],
                                             stat[:, b + 4:b + 5], ALU.subtract, ALU.mult),
                          "FV")
                    v.tensor_add(resid[:, C * t:C * (t + 1)],
                                 resid[:, C * t:C * (t + 1)], lnap[:, :])

            if mode == "ln_add":
                @blk.scalar
                def _(a):
                    for t in range(NT):
                        a.wait_ge(cx.s("SY"), sy0 + nb + 2 * t + 2)
                        ln_act_tile(a, ysc[:, C * (t % 2):C * (t % 2 + 1)], t, sqs2[:, :],
                                    sv0, eps_ln[:, :])

        cx.add("SW", nb + 2 * NT)
        cx.add("SY", nb + 2 * NT)
        if mode == "ln_add":
            cx.add("SA", NT)
            cx.add("SV", NT)
            cx.add("SR", NT)
        dbg_dump("oin", oin, 128, 8 * 2048, BF16)



    def phase_l0ffn_spill_u(ph):
        sqs = ph.enter_context(nc.sbuf_tensor(un("sqs0"), [128, C], BF16))
        st0 = cx.c(ST)
        with nc.Block() as blk:
            @blk.sync
            def _(s):
                s.dma_start(
                    out=_ap(xspill, 0, [[C, 128], [128 * C, NT], [1, C]]),
                    in_=_ap(resid, 0, [[NT * C, 128], [C, NT], [1, C]]),
                ).then_inc(cx.s(ST), 16)
                cx.add(ST, 16)
        emit_norm_block("ln", afm=None, abf=None, sqs=sqs, ptr=None,
                        write_stream=True, eps_tile=eps_ln,
                        pre_wait=(ST, st0 + 16))

    def phase_l0ffn_pre(ph):
        afm = ph.enter_context(nc.sbuf_tensor(un("afm1"), [128, 8 * 2048], BF16))
        abf = ph.enter_context(nc.sbuf_tensor(un("abf1"), [128, 2 * C], BF16))
        sqs = ph.enter_context(nc.sbuf_tensor(un("sqs1"), [128, C], BF16))
        ptr = ph.enter_context(nc.psum_tensor(un("ptr1"), [128, 1024], BF16))
        ln1row = ph.enter_context(nc.sbuf_tensor(un("ln1row"), [1, C], F32))
        ln1rep = ph.enter_context(nc.sbuf_tensor(un("ln1rep"), [128, C], BF16))
        wq_sb = ph.enter_context(nc.sbuf_tensor(un("wq_sb"), [128, 8 * 1024], BF16))
        wkv_sb = ph.enter_context(nc.sbuf_tensor(un("wkv_sb"), [128, 8 * 256], BF16))
        wkup_sb = ph.enter_context(nc.sbuf_tensor(un("wkup_sb"), [128, 2 * 1024], BF16))
        wvup_sb = ph.enter_context(nc.sbuf_tensor(un("wvup_sb"), [128, 2 * 1024], BF16))
        kv_sb = ph.enter_context(nc.sbuf_tensor(un("kv_sb"), [128, 2 * 2048], BF16))
        qraw = ph.enter_context(nc.sbuf_tensor(un("qraw"), [128, 2 * 512], BF16))
        tmp1 = ph.enter_context(nc.sbuf_tensor(un("tmp1"), [128, 512], F32))
        qstage = ph.enter_context(nc.sbuf_tensor(un("qstage"), [128, 4 * 2048], BF16))
        vstage = ph.enter_context(nc.sbuf_tensor(un("vstage1"), [128, 16 * 1040], BF16))
        ccss_sb = ph.enter_context(nc.sbuf_tensor(un("ccss_sb"), [128, 2 * TOK], BF16))
        psw_sb = ph.enter_context(nc.sbuf_tensor(un("psw_sb"), [128, 128], BF16))

        ld0 = cx.c(LD)
        with nc.Block() as blk:
            @blk.sync
            def _(s):
                for nm, dst in (("ln1_w", ln1row), ("ccss", ccss_sb), ("pswap", psw_sb)):
                    s.dma_start(out=dst[:, :], in_=din[nm][:, :]).then_inc(cx.s(LD), 16)
                    cx.add(LD, 16)

            @blk.gpsimd
            def _(g):
                for nm, dst, wid in (("w_q1f", wq_sb, 1024), ("w_kv", wkv_sb, 256)):
                    g.dma_start(
                        out=_ap(dst, 0, [[8 * wid, 128], [wid, 8], [1, wid]]),
                        in_=_ap(din[nm], 0, [[wid, 128], [128 * wid, 8], [1, wid]]),
                    ).then_inc(cx.s(GD), 16)
                    cx.add(GD, 16)
                for nm, dst in (("w_kup", wkup_sb), ("w_vup", wvup_sb)):
                    g.dma_start(
                        out=_ap(dst, 0, [[2 * 1024, 128], [1024, 2], [1, 1024]]),
                        in_=_ap(din[nm], 0, [[1024, 128], [128 * 1024, 2], [1, 1024]]),
                    ).then_inc(cx.s(GD), 16)
                    cx.add(GD, 16)
        emit_replicate_rows([("ln1", ln1row, ln1rep, C)], ld_wait=ld0 + 16)
        emit_norm_block("rms", afm=afm, abf=abf, sqs=sqs, ptr=ptr,
                        gain_rep=ln1rep, eps_tile=eps_rms)

        gd_w = cx.c(GD)
        sp0 = cx.c("SP")
        sm0 = cx.c("SM3")
        with nc.psum_tensor(un("pkv"), [128, 2 * 512], F32) as pkv:
            with nc.Block() as blk:
                @blk.tensor
                def _(te):
                    te.wait_ge(cx.s(GD), gd_w)
                    for i1 in range(8):
                        lc, s = i1 // 4, i1 % 4
                        pp = i1 % 2
                        if i1 >= 2:
                            te.wait_ge(cx.s("SM3"), sm0 + i1 - 1)
                        ins = None
                        for r in range(8):
                            ins = te.matmul(
                                pkv[:, 512 * pp:512 * (pp + 1)],
                                wkv_sb[:, 256 * r + 128 * lc:256 * r + 128 * (lc + 1)],
                                afm[:, 2048 * r + 512 * s:2048 * r + 512 * (s + 1)],
                                start=(r == 0), stop=(r == 7))
                        ins.then_inc(cx.s("SP"), 1)

                @blk.scalar
                def _(a):
                    for i1 in range(8):
                        lc, s = i1 // 4, i1 % 4
                        pp = i1 % 2
                        a.wait_ge(cx.s("SP"), sp0 + i1 + 1)
                        a.activation(kv_sb[:, 2048 * lc + 512 * s:2048 * lc + 512 * (s + 1)],
                                     pkv[:, 512 * pp:512 * (pp + 1)], AF.Copy,
                                     scale=0.25).then_inc(cx.s("SM3"), 1)
            cx.add("SP", 8)
            cx.add("SM3", 8)

        for tgt, nr, rowbase, stname in (("q", 8, 0, "qst"), ("k", 2, 16 * 64, "kst")):
            sp0 = cx.c("SP2")
            sq0 = cx.c("SQ2")
            st20 = cx.c("ST2")
            sn0 = cx.c("SN2")
            stb0 = cx.c(ST)
            with (nc.psum_tensor(un("pq"), [128, 2 * 512], F32) as pq,
                  nc.psum_tensor(un("psw"), [128, 2 * 512], F32) as psw):
                with nc.Block() as blk:
                    @blk.tensor
                    def _(te):
                        for i2 in range(32):
                            hc, s = i2 // 4, i2 % 4
                            pp = i2 % 2
                            if i2 >= 2:
                                te.wait_ge(cx.s("SQ2"), sq0 + i2 - 1)
                            ins = None
                            for r in range(nr):
                                if tgt == "q":
                                    lt = wq_sb[:, 1024 * r + 128 * hc:
                                               1024 * r + 128 * (hc + 1)]
                                    rt = afm[:, 2048 * r + 512 * s:2048 * r + 512 * (s + 1)]
                                else:
                                    lt = wkup_sb[:, 1024 * r + 128 * hc:
                                                 1024 * r + 128 * (hc + 1)]
                                    rt = kv_sb[:, 2048 * r + 512 * s:2048 * r + 512 * (s + 1)]
                                ins = te.matmul(pq[:, 512 * pp:512 * (pp + 1)], lt, rt,
                                                start=(r == 0), stop=(r == nr - 1))
                            ins.then_inc(cx.s("SP2"), 1)
                            te.wait_ge(cx.s("SQ2"), sq0 + i2 + 1)
                            if i2 >= 2:
                                te.wait_ge(cx.s("SN2"), sn0 + i2 - 1)
                            te.matmul(psw[:, 512 * pp:512 * (pp + 1)], psw_sb[:, :],
                                      qraw[:, 512 * pp:512 * (pp + 1)],
                                      start=True, stop=True).then_inc(cx.s("ST2"), 1)

                    @blk.vector
                    def _(v):
                        for i2 in range(32):
                            hc, s = i2 // 4, i2 % 4
                            pp = i2 % 2
                            v.wait_ge(cx.s("SP2"), sp0 + i2 + 1)
                            if i2 >= 2:
                                v.wait_ge(cx.s("ST2"), st20 + i2 - 1)
                            v.tensor_copy(qraw[:, 512 * pp:512 * (pp + 1)],
                                          pq[:, 512 * pp:512 * (pp + 1)]).then_inc(
                                cx.s("SQ2"), 1)
                            v.wait_ge(cx.s("ST2"), st20 + i2 + 1)
                            if i2 == 16:
                                v.wait_ge(cx.s(ST), stb0 + 16)
                            qsl = qstage[:, 2048 * (hc % 4) + 512 * s:
                                          2048 * (hc % 4) + 512 * (s + 1)]
                            fence(v, v.tensor_tensor(
                                tmp1[:, :], qraw[:, 512 * pp:512 * (pp + 1)],
                                ccss_sb[:, 512 * s:512 * (s + 1)], ALU.mult), "FV")
                            fence(v, v.tensor_tensor(
                                qsl, psw[:, 512 * pp:512 * (pp + 1)],
                                ccss_sb[:, TOK + 512 * s:TOK + 512 * (s + 1)],
                                ALU.mult), "FV")
                            v.tensor_add(qsl, qsl, tmp1[:, :]).then_inc(cx.s("SN2"), 1)

                    @blk.gpsimd
                    def _(g):
                        with g.register(un("rg")) as rg:
                            for half in range(2):
                                g.wait_ge(cx.s("SN2"), sn0 + 16 * (half + 1))
                                g.reg_load(rg, offs[0:1, OI["qk1_mine"]:
                                                 OI["qk1_mine"] + 1])
                                g.reg_add(rg, rg, (rowbase + 512 * half) * QKW)
                                g.dma_start(
                                    out=_ap(XQK1, rg,
                                            [[QKW, 128], [128 * QKW, 4], [1, 2048]]),
                                    in_=_ap(qstage, 0,
                                            [[4 * 2048, 128], [2048, 4], [1, 2048]]),
                                ).then_inc(cx.s(ST), 16)
                                cx.add(ST, 16)
                cx.add("SP2", 32)
                cx.add("SQ2", 32)
                cx.add("ST2", 32)
                cx.add("SN2", 32)

        sp0 = cx.c("SP4")
        sm0 = cx.c("SM4")
        with nc.psum_tensor(un("pv1"), [128, 4 * 512], F32) as pv:
            with nc.Block() as blk:
                @blk.tensor
                def _(te):
                    for i4 in range(32):
                        t, dh = i4 // 2, i4 % 2
                        s4 = i4 % 4
                        if i4 >= 4:
                            te.wait_ge(cx.s("SM4"), sm0 + i4 - 3)
                        ins = None
                        for lc in range(2):
                            ins = te.matmul(
                                pv[:, 512 * s4:512 * (s4 + 1)],
                                kv_sb[:, 2048 * lc + 128 * t:2048 * lc + 128 * (t + 1)],
                                wvup_sb[:, 1024 * lc + 512 * dh:1024 * lc + 512 * (dh + 1)],
                                start=(lc == 0), stop=(lc == 1))
                        ins.then_inc(cx.s("SP4"), 1)

                @blk.vector
                def _(v):
                    v.memset(vstage[:, :], 1.0)
                    for i4 in range(32):
                        t, dh = i4 // 2, i4 % 2
                        s4 = i4 % 4
                        v.wait_ge(cx.s("SP4"), sp0 + i4 + 1)
                        v.tensor_copy(
                            _ap(vstage, 1040 * t + 520 * dh,
                                [[16 * 1040, 128], [130, 4], [65, 2], [1, 64]]),
                            _ap(pv, 512 * s4, [[4 * 512, 128], [128, 4], [64, 2], [1, 64]]),
                        ).then_inc(cx.s("SM4"), 1)

                @blk.gpsimd
                def _(g):
                    g.wait_ge(cx.s("SM4"), sm0 + 32)
                    with g.register(un("rg")) as rg:
                        g.reg_load(rg, offs[0:1, OI["v1_mine"]:OI["v1_mine"] + 1])
                        g.dma_start(
                            out=_ap(XV1, rg, [[1040, 128], [128 * 1040, 16], [1, 1040]]),
                            in_=_ap(vstage, 0, [[16 * 1040, 128], [1040, 16], [1, 1040]]),
                        ).then_inc(cx.s(ST), 16)
                        cx.add(ST, 16)
                    g.wait_ge(cx.s(ST), cx.c(ST))
                    barrier(g)
            cx.add("SP4", 32)
            cx.add("SM4", 32)

    def phase_l0ffn_moe(ph):
        afm = ph.enter_context(nc.sbuf_tensor(un("afm2"), [128, 8 * 2048], BF16))
        abf = ph.enter_context(nc.sbuf_tensor(un("abf2"), [128, 2 * C], BF16))
        sqs = ph.enter_context(nc.sbuf_tensor(un("sqs2b"), [128, C], BF16))
        ptr = ph.enter_context(nc.psum_tensor(un("ptr2"), [128, 1024], BF16))
        ln2row = ph.enter_context(nc.sbuf_tensor(un("ln2row"), [1, C], F32))
        ln2rep = ph.enter_context(nc.sbuf_tensor(un("ln2rep"), [128, C], BF16))
        rbrow = ph.enter_context(nc.sbuf_tensor(un("rbrow"), [1, 8], F32))
        rbrep = ph.enter_context(nc.sbuf_tensor(un("rbrep"), [128, 8], F32))
        wr_sb = ph.enter_context(nc.sbuf_tensor(un("wr_sb"), [128, 8 * 8], BF16))
        coef = ph.enter_context(nc.sbuf_tensor(un("coef"), [128, NT * 10], F32))
        lgt = ph.enter_context(nc.sbuf_tensor(un("lgt"), [128, 2 * 8], F32))
        probs = ph.enter_context(nc.sbuf_tensor(un("probs"), [128, 2 * 8], F32))
        rsc = ph.enter_context(nc.sbuf_tensor(un("rsc"), [128, 8], F32))
        g_sb = ph.enter_context(nc.sbuf_tensor(un("g_sb"), [128, 16 * 2048], BF16))
        s1 = ph.enter_context(nc.sbuf_tensor(un("s1"), [128, 2 * 512], BF16))
        w1_sb = ph.enter_context(nc.sbuf_tensor(un("w1_sb"), [128, 2 * 8 * 128], BF16))
        w3_sb = ph.enter_context(nc.sbuf_tensor(un("w3_sb"), [128, 2 * 8 * 128], BF16))
        w2_sb = ph.enter_context(nc.sbuf_tensor(un("w2_sb"), [128, 8 * 1024], BF16))

        ld0 = cx.c(LD)
        with nc.Block() as blk:
            @blk.sync
            def _(s):
                for nm, dst in (("ln2_w", ln2row), ("r_bias", rbrow)):
                    s.dma_start(out=dst[:, :], in_=din[nm][:, :]).then_inc(cx.s(LD), 16)
                    cx.add(LD, 16)

            @blk.gpsimd
            def _(g):
                g.dma_start(
                    out=_ap(wr_sb, 0, [[64, 128], [8, 8], [1, 8]]),
                    in_=_ap(din["w_r"], 0, [[8, 128], [1024, 8], [1, 8]]),
                ).then_inc(cx.s(GD), 16)
                cx.add(GD, 16)
        emit_replicate_rows([("ln2", ln2row, ln2rep, C), ("rb", rbrow, rbrep, 8)],
                            ld_wait=ld0 + 32)
        emit_norm_block("rms", afm=afm, abf=abf, sqs=sqs, ptr=ptr,
                        gain_rep=ln2rep, eps_tile=eps_rms)

        gd_r = cx.c(GD)
        sp0, sy0 = cx.c("SP5"), cx.c("SY5")
        sg0, sh0, sy60 = cx.c("SG5"), cx.c("SH5"), cx.c("SY6")
        with nc.psum_tensor(un("pR"), [128, 512], F32) as pR:
            with nc.Block() as blk:
                @blk.tensor
                def _(te):
                    te.wait_ge(cx.s(GD), gd_r)
                    for t in range(NT):
                        if t >= 2:
                            te.wait_ge(cx.s("SY5"), sy0 + t - 1)
                        ins = None
                        for r in range(8):
                            ins = te.matmul(pR[:, 8 * (t % 2):8 * (t % 2) + 8],
                                            afm[:, 2048 * r + 128 * t:
                                                2048 * r + 128 * (t + 1)],
                                            wr_sb[:, 8 * r:8 * (r + 1)],
                                            start=(r == 0), stop=(r == 7))
                        ins.then_inc(cx.s("SP5"), 1)

                @blk.scalar
                def _(a):
                    for t in range(NT):
                        pp = t % 2
                        a.wait_ge(cx.s("SG5"), sg0 + t + 1)
                        if t >= 2:
                            a.wait_ge(cx.s("SY6"), sy60 + t - 1)
                        a.activation(probs[:, 8 * pp:8 * (pp + 1)],
                                     lgt[:, 8 * pp:8 * (pp + 1)],
                                     AF.Sigmoid).then_inc(cx.s("SH5"), 1)

                @blk.vector
                def _(v):
                    for t in range(NT):
                        pp = t % 2
                        lg = lgt[:, 8 * pp:8 * (pp + 1)]
                        pr = probs[:, 8 * pp:8 * (pp + 1)]
                        v.wait_ge(cx.s("SP5"), sp0 + t + 1)
                        fence(v, v.tensor_add(lg, pR[:, 8 * pp:8 * (pp + 1)],
                                              rbrep[:, :]), "FV")
                        v.engine_nop().then_inc(cx.s("SY5"), 1)
                        fence(v, v.tensor_scalar_min(lg, lg, 50.0), "FV")
                        fence(v, v.tensor_scalar_max(lg, lg, -50.0), "FV")
                        v.engine_nop().then_inc(cx.s("SG5"), 1)
                        v.wait_ge(cx.s("SH5"), sh0 + t + 1)
                        fence(v, v.reduce_max(rsc[:, 0:1], pr, AX), "FV")
                        fence(v, v.tensor_scalar(rsc[:, 1:2], rsc[:, 0:1], 1e-8,
                                                 1.0 - 1e-8, ALU.max, ALU.min), "FV")
                        fence(v, v.tensor_scalar(rsc[:, 2:3], rsc[:, 1:2],
                                                 0.5 + 1e-8, 0.5, ALU.add, ALU.max), "FV")
                        fence(v, v.tensor_scalar_min(rsc[:, 2:3], rsc[:, 2:3], 2.0), "FV")
                        fence(v, v.reciprocal(rsc[:, 3:4], rsc[:, 2:3]), "FV")
                        fence(v, v.tensor_tensor(rsc[:, 4:5], rsc[:, 1:2], rsc[:, 3:4],
                                                 ALU.mult), "FV")
                        fence(v, v.tensor_scalar_mul(coef[:, 10 * t + 8:10 * t + 9],
                                                     rsc[:, 3:4], 0.5), "FV")
                        fence(v, v.memset(rsc[:, 5:6], 0.0), "FV")
                        for e in range(8):
                            fence(v, v.tensor_tensor(rsc[:, 7:8], pr[:, e:e + 1],
                                                     rsc[:, 0:1], ALU.is_equal), "FV")
                            fence(v, v.tensor_scalar(rsc[:, 6:7], rsc[:, 5:6], -1.0,
                                                     1.0, ALU.mult, ALU.add), "FV")
                            fence(v, v.tensor_tensor(rsc[:, 6:7], rsc[:, 7:8],
                                                     rsc[:, 6:7], ALU.mult), "FV")
                            fence(v, v.tensor_tensor(coef[:, 10 * t + e:10 * t + e + 1],
                                                     rsc[:, 6:7], rsc[:, 4:5],
                                                     ALU.mult), "FV")
                            fence(v, v.tensor_tensor(rsc[:, 5:6], rsc[:, 5:6],
                                                     rsc[:, 7:8], ALU.max), "FV")
                        v.engine_nop().then_inc(cx.s("SY6"), 1)
            cx.add("SP5", NT)
            cx.add("SY5", NT)
            cx.add("SG5", NT)
            cx.add("SH5", NT)
            cx.add("SY6", NT)

        experts = [("sw1", 0, "sw3", 0, "sw2", 0, 8)]
        for e in range(8):
            experts.append(("ew1", e * C * 2048, "ew3", e * C * 2048,
                            "ew2", e * 2048 * C, e))
        gw0 = cx.c("GW")
        sp60 = cx.c("SP6")
        sm60 = cx.c("SM6")
        se60 = cx.c("SE6")
        sw20 = cx.c("SW2")
        sy20 = cx.c("SY2")
        with (nc.psum_tensor(un("pA"), [128, 2 * 512], F32) as pA,
              nc.psum_tensor(un("pB"), [128, 2 * 512], F32) as pB,
              nc.psum_tensor(un("pY"), [128, 2 * 512], F32) as pY):
            with nc.Block() as blk:
                @blk.gpsimd
                def _(g):
                    for ei, (n1, o1, n3, o3, n2, o2, cc_) in enumerate(experts):
                        hb = ei % 2
                        for hc in range(16):
                            if hc >= 1 or ei >= 1:
                                pass
                            g.wait_ge(cx.s("SP6"),
                                      sp60 + max(0, (ei * 16 + hc - 1) * 4))
                            for nm, off, dst in ((n1, o1, w1_sb), (n3, o3, w3_sb)):
                                g.dma_start(
                                    out=_ap(dst, 1024 * (hc % 2),
                                            [[2 * 1024, 128], [128, 8], [1, 128]]),
                                    in_=_ap(din[nm], off + 128 * hc,
                                            [[2048, 128], [128 * 2048, 8], [1, 128]]),
                                ).then_inc(cx.s("GW"), 16)
                                cx.add("GW", 16)
                        for wh in range(2):
                            g.wait_ge(cx.s("SW2"), sw20 + 64 * ei + 32 * wh)
                            g.dma_start(
                                out=_ap(w2_sb, 0, [[8 * 1024, 128], [1024, 8], [1, 1024]]),
                                in_=_ap(din[n2], o2 + 8 * wh * 128 * 1024,
                                        [[1024, 128], [128 * 1024, 8], [1, 1024]]),
                            ).then_inc(cx.s("GW"), 16)
                            cx.add("GW", 16)

                @blk.tensor
                def _(te):
                    for ei, (n1, o1, n3, o3, n2, o2, cc_) in enumerate(experts):
                        for hc in range(16):
                            te.wait_ge(cx.s("GW"), gw0 + 544 * ei + 32 * (hc + 1))
                            for s in range(4):
                                i3 = 4 * hc + s
                                pp = i3 % 2
                                if i3 >= 2 or ei >= 1:
                                    te.wait_ge(cx.s("SM6"), sm60 + 64 * ei + i3 - 1)
                                ins = None
                                for r in range(8):
                                    ins = te.matmul(
                                        pA[:, 512 * pp:512 * (pp + 1)],
                                        w1_sb[:, 1024 * (hc % 2) + 128 * r:
                                              1024 * (hc % 2) + 128 * (r + 1)],
                                        afm[:, 2048 * r + 512 * s:2048 * r + 512 * (s + 1)],
                                        start=(r == 0), stop=(r == 7))
                                for r in range(8):
                                    ins = te.matmul(
                                        pB[:, 512 * pp:512 * (pp + 1)],
                                        w3_sb[:, 1024 * (hc % 2) + 128 * r:
                                              1024 * (hc % 2) + 128 * (r + 1)],
                                        afm[:, 2048 * r + 512 * s:2048 * r + 512 * (s + 1)],
                                        start=(r == 0), stop=(r == 7))
                                ins.then_inc(cx.s("SP6"), 1)
                        te.wait_ge(cx.s("SM6"), sm60 + 64 * (ei + 1))
                        for wh in range(2):
                            te.wait_ge(cx.s("GW"), gw0 + 544 * ei + 512 + 16 * (wh + 1))
                            for i4 in range(32):
                                t, dh = i4 // 2, i4 % 2
                                pp = i4 % 2
                                idx = 64 * ei + 32 * wh + i4
                                if idx >= 2:
                                    te.wait_ge(cx.s("SY2"), sy20 + idx - 1)
                                ins = None
                                for h8 in range(8):
                                    hc = 8 * wh + h8
                                    ins = te.matmul(
                                        pY[:, 512 * pp:512 * (pp + 1)],
                                        g_sb[:, 2048 * hc + 128 * t:
                                             2048 * hc + 128 * (t + 1)],
                                        w2_sb[:, 1024 * h8 + 512 * dh:
                                              1024 * h8 + 512 * (dh + 1)],
                                        start=(h8 == 0), stop=(h8 == 7))
                                ins.then_inc(cx.s("SW2"), 1)

                @blk.scalar
                def _(a):
                    for ei in range(len(experts)):
                        for i3 in range(64):
                            pp = i3 % 2
                            a.wait_ge(cx.s("SP6"), sp60 + 64 * ei + i3 + 1)
                            if i3 >= 2 or ei >= 1:
                                a.wait_ge(cx.s("SM6"), sm60 + 64 * ei + i3 - 1)
                            a.activation(s1[:, 512 * pp:512 * (pp + 1)],
                                         pA[:, 512 * pp:512 * (pp + 1)],
                                         AF.Silu).then_inc(cx.s("SE6"), 1)

                @blk.vector
                def _(v):
                    for ei, (n1, o1, n3, o3, n2, o2, cc_) in enumerate(experts):
                        if ei >= 1:
                            v.wait_ge(cx.s("SW2"), sw20 + 64 * ei)
                        for i3 in range(64):
                            hc, s = i3 // 4, i3 % 4
                            pp = i3 % 2
                            v.wait_ge(cx.s("SE6"), se60 + 64 * ei + i3 + 1)
                            v.tensor_tensor(
                                g_sb[:, 2048 * hc + 512 * s:2048 * hc + 512 * (s + 1)],
                                s1[:, 512 * pp:512 * (pp + 1)],
                                pB[:, 512 * pp:512 * (pp + 1)],
                                ALU.mult).then_inc(cx.s("SM6"), 1)
                        for wh in range(2):
                            for i4 in range(32):
                                t, dh = i4 // 2, i4 % 2
                                pp = i4 % 2
                                idx = 64 * ei + 32 * wh + i4
                                v.wait_ge(cx.s("SW2"), sw20 + idx + 1)
                                dst = resid[:, C * t + 512 * dh:C * t + 512 * (dh + 1)]
                                v.scalar_tensor_tensor(
                                    dst, pY[:, 512 * pp:512 * (pp + 1)],
                                    coef[:, 10 * t + cc_:10 * t + cc_ + 1], dst,
                                    ALU.mult, ALU.add).then_inc(cx.s("SY2"), 1)
            ne = len(experts)
            cx.add("SP6", 64 * ne)
            cx.add("SE6", 64 * ne)
            cx.add("SM6", 64 * ne)
            cx.add("SW2", 64 * ne)
            cx.add("SY2", 64 * ne)

    def phase_l0ffn_tail(ph):
        xsp = ph.enter_context(nc.sbuf_tensor(un("xsp"), [128, NT * C], F32))
        sqs = ph.enter_context(nc.sbuf_tensor(un("sqs3"), [128, C], BF16))
        lnap = ph.enter_context(nc.sbuf_tensor(un("lnap3"), [128, C], F32))
        ld0 = cx.c(LD)
        with nc.Block() as blk:
            @blk.sync
            def _(s):
                s.dma_start(
                    out=_ap(xsp, 0, [[NT * C, 128], [C, NT], [1, C]]),
                    in_=_ap(xspill, 0, [[C, 128], [128 * C, NT], [1, C]]),
                ).then_inc(cx.s(LD), 16)
                cx.add(LD, 16)
        sa0, sv0, sr0 = cx.c("SA"), cx.c("SV"), cx.c("SR")
        with nc.Block() as blk:
            @blk.scalar
            def _(a):
                for t in range(NT):
                    ln_act_tile(a, resid[:, C * t:C * (t + 1)], t, sqs[:, :], sv0,
                                eps_ln[:, :])

            @blk.vector
            def _(v):
                v.wait_ge(cx.s(LD), ld0 + 16)
                for t in range(NT):
                    b = 8 * (t % 2)
                    yt = resid[:, C * t:C * (t + 1)]
                    ln_dve_stats(v, yt, t, sa0)
                    v.wait_ge(cx.s("SR"), sr0 + t + 1)
                    fence(v, v.tensor_scalar(lnap[:, :], yt, stat[:, b + 2:b + 3],
                                             stat[:, b + 4:b + 5], ALU.subtract,
                                             ALU.mult), "FV")
                    v.tensor_add(yt, xsp[:, C * t:C * (t + 1)], lnap[:, :])
        cx.add("SA", NT)
        cx.add("SV", NT)
        cx.add("SR", NT)


    def phase_l1attn_pre(ph):
        afm = ph.enter_context(nc.sbuf_tensor(un("afm3"), [128, 8 * 2048], BF16))
        abf = ph.enter_context(nc.sbuf_tensor(un("abf3"), [128, 2 * C], BF16))
        sqs = ph.enter_context(nc.sbuf_tensor(un("sqs4"), [128, C], BF16))
        ptr = ph.enter_context(nc.psum_tensor(un("ptr3"), [128, 1024], BF16))
        wa_sb = ph.enter_context(nc.sbuf_tensor(un("wa_sb"), [128, 2 * 8 * 1024], BF16))
        qstage = ph.enter_context(nc.sbuf_tensor(un("qstage3"), [128, 4 * 2048], BF16))
        vstage = ph.enter_context(nc.sbuf_tensor(un("vstage3"), [128, 16 * 1040], BF16))

        emit_norm_block("ln", afm=afm, abf=abf, sqs=sqs, ptr=ptr, eps_tile=eps_ln)

        gw0 = cx.c("GW")
        with nc.Block() as blk:
            @blk.gpsimd
            def _(g):
                for wi, nm in enumerate(("w_q2a", "w_k2a")):
                    g.dma_start(
                        out=_ap(wa_sb, 8192 * (wi % 2),
                                [[2 * 8192, 128], [1024, 8], [1, 1024]]),
                        in_=_ap(din[nm], 0, [[1024, 128], [128 * 1024, 8], [1, 1024]]),
                    ).then_inc(cx.s("GW"), 16)
                    cx.add("GW", 16)

            sp70 = cx.c("SP7")
            sq70 = cx.c("SQ7")
            st70 = cx.c(ST)
            sm70 = cx.c("SM7")
            with (nc.psum_tensor(un("pq3"), [128, 2 * 512], F32) as pq3,
                  nc.psum_tensor(un("pv3"), [128, 4 * 512], F32) as pv3):
                @blk.tensor
                def _(te):
                    for wi in range(2):
                        te.wait_ge(cx.s("GW"), gw0 + 16 * (wi + 1))
                        for i2 in range(32):
                            hc, s = i2 // 4, i2 % 4
                            pp = i2 % 2
                            gi = 32 * wi + i2
                            if gi >= 2:
                                te.wait_ge(cx.s("SQ7"), sq70 + gi - 1)
                            ins = None
                            for r in range(8):
                                ins = te.matmul(
                                    pq3[:, 512 * pp:512 * (pp + 1)],
                                    wa_sb[:, 8192 * (wi % 2) + 1024 * r + 128 * hc:
                                          8192 * (wi % 2) + 1024 * r + 128 * (hc + 1)],
                                    afm[:, 2048 * r + 512 * s:2048 * r + 512 * (s + 1)],
                                    start=(r == 0), stop=(r == 7))
                            ins.then_inc(cx.s("SP7"), 1)
                    te.wait_ge(cx.s("GW"), gw0 + 48)
                    for i4 in range(32):
                        t, dh = i4 // 2, i4 % 2
                        s4 = i4 % 4
                        if i4 >= 4:
                            te.wait_ge(cx.s("SM7"), sm70 + i4 - 3)
                        ins = None
                        for r in range(8):
                            ins = te.matmul(
                                pv3[:, 512 * s4:512 * (s4 + 1)],
                                afm[:, 2048 * r + 128 * t:2048 * r + 128 * (t + 1)],
                                wa_sb[:, 8192 * 0 + 1024 * r + 512 * dh:
                                      8192 * 0 + 1024 * r + 512 * (dh + 1)],
                                start=(r == 0), stop=(r == 7))
                        ins.then_inc(cx.s("SP7"), 1)

                @blk.vector
                def _(v):
                    v.memset(vstage[:, :], 1.0)
                    for wi in range(2):
                        for i2 in range(32):
                            hc, s = i2 // 4, i2 % 4
                            pp = i2 % 2
                            gi = 32 * wi + i2
                            v.wait_ge(cx.s("SP7"), sp70 + gi + 1)
                            if gi >= 16 and i2 < 16:
                                pass
                            if i2 == 0 and wi >= 0:
                                pass
                            if i2 == 16 or (wi == 1 and i2 == 0):
                                v.wait_ge(cx.s(ST), st70 + 16 * (2 * wi + (i2 // 16)))
                            v.tensor_copy(
                                qstage[:, 2048 * (hc % 4) + 512 * s:
                                       2048 * (hc % 4) + 512 * (s + 1)],
                                pq3[:, 512 * pp:512 * (pp + 1)]).then_inc(cx.s("SQ7"), 1)
                    for i4 in range(32):
                        t, dh = i4 // 2, i4 % 2
                        s4 = i4 % 4
                        v.wait_ge(cx.s("SP7"), sp70 + 64 + i4 + 1)
                        v.tensor_copy(
                            _ap(vstage, 1040 * t + 520 * dh,
                                [[16 * 1040, 128], [130, 4], [65, 2], [1, 64]]),
                            _ap(pv3, 512 * s4, [[4 * 512, 128], [128, 4], [64, 2], [1, 64]]),
                        ).then_inc(cx.s("SM7"), 1)

                @blk.gpsimd
                def _(g):
                    with g.register(un("rg")) as rg:
                        for wi in range(2):
                            if wi == 1:
                                g.wait_ge(cx.s("SP7"), sp70 + 32)
                                g.dma_start(
                                    out=_ap(wa_sb, 0,
                                            [[2 * 8192, 128], [1024, 8], [1, 1024]]),
                                    in_=_ap(din["w_v2a"], 0,
                                            [[1024, 128], [128 * 1024, 8], [1, 1024]]),
                                ).then_inc(cx.s("GW"), 16)
                                cx.add("GW", 16)
                            rowbase = 0 if wi == 0 else 16 * 64
                            for half in range(2):
                                g.wait_ge(cx.s("SQ7"), sq70 + 32 * wi + 16 * (half + 1))
                                g.reg_load(rg, offs[0:1, OI["qk2_mine"]:
                                                 OI["qk2_mine"] + 1])
                                g.reg_add(rg, rg, (rowbase + 512 * half) * QKW)
                                g.dma_start(
                                    out=_ap(XQK2, rg,
                                            [[QKW, 128], [128 * QKW, 4], [1, 2048]]),
                                    in_=_ap(qstage, 0,
                                            [[4 * 2048, 128], [2048, 4], [1, 2048]]),
                                ).then_inc(cx.s(ST), 16)
                                cx.add(ST, 16)
                        g.wait_ge(cx.s("SM7"), sm70 + 32)
                        g.reg_load(rg, offs[0:1, OI["v2_mine"]:OI["v2_mine"] + 1])
                        g.dma_start(
                            out=_ap(XV2, rg, [[1040, 128], [128 * 1040, 16], [1, 1040]]),
                            in_=_ap(vstage, 0, [[16 * 1040, 128], [1040, 16], [1, 1040]]),
                        ).then_inc(cx.s(ST), 16)
                        cx.add(ST, 16)
                        g.wait_ge(cx.s(ST), cx.c(ST))
                        barrier(g)
                cx.add("SP7", 96)
                cx.add("SQ7", 64)
                cx.add("SM7", 32)

    def phase_l1ffn(ph):
        afm = ph.enter_context(nc.sbuf_tensor(un("afm4"), [128, 8 * 2048], BF16))
        abf = ph.enter_context(nc.sbuf_tensor(un("abf4"), [128, 2 * C], BF16))
        sqs = ph.enter_context(nc.sbuf_tensor(un("sqs5"), [128, C], BF16))
        ptr = ph.enter_context(nc.psum_tensor(un("ptr4"), [128, 1024], BF16))
        wg_sb = ph.enter_context(nc.sbuf_tensor(un("wg_sb"), [128, 8 * 8], BF16))
        gbrow = ph.enter_context(nc.sbuf_tensor(un("gbrow"), [1, 8], F32))
        gbrep = ph.enter_context(nc.sbuf_tensor(un("gbrep"), [128, 8], F32))
        rwrow = ph.enter_context(nc.sbuf_tensor(un("rwrow"), [1, 8], F32))
        rwrep = ph.enter_context(nc.sbuf_tensor(un("rwrep"), [128, 8], F32))
        b1T = ph.enter_context(nc.sbuf_tensor(un("b1T"), [128, 128], F32))
        b2row = ph.enter_context(nc.sbuf_tensor(un("b2row"), [1, C], F32))
        b2rep = ph.enter_context(nc.sbuf_tensor(un("b2rep"), [128, C], F32))
        identf = ph.enter_context(nc.sbuf_tensor(un("identf"), [128, 128], F32))
        ones8 = ph.enter_context(nc.sbuf_tensor(un("ones8"), [8, 1], F32))
        lgt2 = ph.enter_context(nc.sbuf_tensor(un("lgt2"), [128, 2 * 8], F32))
        em2 = ph.enter_context(nc.sbuf_tensor(un("em2"), [128, 2 * 8], F32))
        rsc2 = ph.enter_context(nc.sbuf_tensor(un("rsc2"), [128, 8], F32))
        normg = ph.enter_context(nc.sbuf_tensor(un("normg"), [128, NT], F32))

        emit_norm_block("ln", afm=afm, abf=abf, sqs=sqs, ptr=ptr, eps_tile=eps_ln)

        sub_es = contextlib.ExitStack()
        b1_sb = sub_es.enter_context(nc.sbuf_tensor(un("b1_sb"), [8, 2048], F32))
        b2_sb = sub_es.enter_context(nc.sbuf_tensor(un("b2_sb"), [8, 1024], F32))
        ld0 = cx.c(LD)
        with nc.Block() as blk:
            @blk.sync
            def _(s):
                for nm, dst in (("gate_b", gbrow), ("rms_w", rwrow),
                                ("fb1", b1_sb), ("fb2", b2_sb)):
                    s.dma_start(out=dst[:, :], in_=din[nm][:, :]).then_inc(cx.s(LD), 16)
                    cx.add(LD, 16)

            @blk.gpsimd
            def _(g):
                g.dma_start(
                    out=_ap(wg_sb, 0, [[64, 128], [8, 8], [1, 8]]),
                    in_=_ap(din["w_gate"], 0, [[8, 128], [1024, 8], [1, 8]]),
                ).then_inc(cx.s(GD), 16)
                cx.add(GD, 16)

            @blk.vector
            def _(v):
                v.memset(ones8[:, :], 1.0)
                v.tensor_copy(identf[:, :], ident[:, :])

        sb80, sy80 = cx.c("SB8"), cx.c("SY8")
        ld1 = cx.c(LD)
        with nc.psum_tensor(un("pT"), [128, 512], F32) as pT:
            with nc.Block() as blk:
                @blk.tensor
                def _(te):
                    te.wait_ge(cx.s(LD), ld1)
                    ins = None
                    for hc in range(16):
                        ins = te.transpose(pT[:, 8 * hc:8 * (hc + 1)],
                                           b1_sb[0:8, 128 * hc:128 * (hc + 1)],
                                           identf[0:8, 0:8])
                    ins.then_inc(cx.s("SB8"), 1)
                    te.wait_ge(cx.s("SY8"), sy80 + 1)
                    for dh in range(2):
                        te.matmul(pT[0:1, 8 * dh:8 * dh + 512] if False else
                                  pT[0:1, 0:512],
                                  ones8[:, :], b2_sb[0:8, 512 * dh:512 * (dh + 1)],
                                  start=True, stop=True).then_inc(cx.s("SB8"), 1)
                        te.wait_ge(cx.s("SY8"), sy80 + 2 + dh)

                @blk.vector
                def _(v):
                    v.wait_ge(cx.s("SB8"), sb80 + 1)
                    v.tensor_copy(b1T[:, :], pT[:, 0:128]).then_inc(cx.s("SY8"), 1)
                    for dh in range(2):
                        v.wait_ge(cx.s("SB8"), sb80 + 2 + dh)
                        v.tensor_copy(b2row[0:1, 512 * dh:512 * (dh + 1)],
                                      pT[0:1, 0:512]).then_inc(cx.s("SY8"), 1)
            cx.add("SB8", 3)
            cx.add("SY8", 3)
        emit_replicate_rows([("b2", b2row, b2rep, C), ("gb", gbrow, gbrep, 8),
                             ("rw", rwrow, rwrep, 8)], ld_wait=ld0 + 64)
        sub_es.close()
        g_sb = ph.enter_context(nc.sbuf_tensor(un("g1_sb"), [128, 16 * 1024], BF16))
        yacc = ph.enter_context(nc.sbuf_tensor(un("yacc"), [128, 8 * 1024], F32))
        w1_sb = ph.enter_context(nc.sbuf_tensor(un("w1f_sb"), [128, 2 * 8 * 128], BF16))
        w2_sb = ph.enter_context(nc.sbuf_tensor(un("w2f_sb"), [128, 8 * 1024], BF16))
        ysc2 = ph.enter_context(nc.sbuf_tensor(un("ysc2"), [128, C], F32))
        lnap2 = ph.enter_context(nc.sbuf_tensor(un("lnap2"), [128, C], F32))


        gd_g = cx.c(GD)
        spg0, syg0, svg0, srg0, swg0, sxg0, szg0 = (cx.c(x) for x in
            ("SPg", "SYg", "SVg", "SRg", "SWg", "SXg", "SZg"))
        with nc.psum_tensor(un("pG"), [128, 512], F32) as pG:
            with nc.Block() as blk:
                @blk.tensor
                def _(te):
                    te.wait_ge(cx.s(GD), gd_g)
                    for t in range(NT):
                        if t >= 2:
                            te.wait_ge(cx.s("SYg"), syg0 + t - 1)
                        ins = None
                        for r in range(8):
                            ins = te.matmul(pG[:, 8 * (t % 2):8 * (t % 2) + 8],
                                            afm[:, 2048 * r + 128 * t:
                                                2048 * r + 128 * (t + 1)],
                                            wg_sb[:, 8 * r:8 * (r + 1)],
                                            start=(r == 0), stop=(r == 7))
                        ins.then_inc(cx.s("SPg"), 1)

                @blk.scalar
                def _(a):
                    for t in range(NT):
                        pp = t % 2
                        a.wait_ge(cx.s("SVg"), svg0 + t + 1)
                        fence(a, a.activation(rsc2[:, 1:2], rsc2[:, 0:1], AF.Ln,
                                              bias=eps_g[:, :]), "FA")
                        a.activation(rsc2[:, 2:3], rsc2[:, 1:2], AF.Exp,
                                     scale=-0.5).then_inc(cx.s("SRg"), 1)
                        a.wait_ge(cx.s("SWg"), swg0 + t + 1)
                        if t >= 2:
                            a.wait_ge(cx.s("SZg"), szg0 + t - 1)
                        a.activation(em2[:, 8 * pp:8 * (pp + 1)],
                                     lgt2[:, 8 * pp:8 * (pp + 1)], AF.Exp,
                                     scale=2.0, bias=rsc2[:, 4:5]).then_inc(
                            cx.s("SXg"), 1)

                @blk.vector
                def _(v):
                    for t in range(NT):
                        pp = t % 2
                        lg = lgt2[:, 8 * pp:8 * (pp + 1)]
                        v.wait_ge(cx.s("SPg"), spg0 + t + 1)
                        fence(v, v.tensor_add(lg, pG[:, 8 * pp:8 * pp + 8],
                                              gbrep[:, :]), "FV")
                        v.engine_nop().then_inc(cx.s("SYg"), 1)
                        fence(v, v.tensor_tensor(rsc2[:, 6:7] if False else rsc2[:, 5:6],
                                                 lg[:, 0:1], lg[:, 0:1], ALU.mult), "FV")
                        fence(v, v.tensor_tensor(em2[:, 8 * pp:8 * (pp + 1)], lg, lg,
                                                 ALU.mult), "FV")
                        fence(v, v.reduce_sum(rsc2[:, 0:1], em2[:, 8 * pp:8 * (pp + 1)],
                                              AX), "FV")
                        fence(v, v.tensor_scalar_mul(rsc2[:, 0:1], rsc2[:, 0:1],
                                                     1.0 / 8), "FV")
                        v.engine_nop().then_inc(cx.s("SVg"), 1)
                        v.wait_ge(cx.s("SRg"), srg0 + t + 1)
                        fence(v, v.tensor_scalar_mul(lg, lg, rsc2[:, 2:3]), "FV")
                        fence(v, v.tensor_tensor(lg, lg, rwrep[:, :], ALU.mult), "FV")
                        fence(v, v.reduce_max(rsc2[:, 3:4], lg, AX), "FV")
                        fence(v, v.tensor_scalar_mul(rsc2[:, 4:5], rsc2[:, 3:4],
                                                     -2.0), "FV")
                        v.engine_nop().then_inc(cx.s("SWg"), 1)
                        v.wait_ge(cx.s("SXg"), sxg0 + t + 1)
                        fence(v, v.reduce_sum(rsc2[:, 5:6],
                                              em2[:, 8 * pp:8 * (pp + 1)], AX), "FV")
                        fence(v, v.tensor_scalar(rsc2[:, 6:7], rsc2[:, 5:6], 1e-6, 1.0,
                                                 ALU.mult, ALU.add), "FV")
                        fence(v, v.reciprocal(normg[:, t:t + 1], rsc2[:, 6:7]), "FV")
                        v.engine_nop().then_inc(cx.s("SZg"), 1)
            for nm in ("SPg", "SYg", "SVg", "SRg", "SWg", "SXg", "SZg"):
                cx.add(nm, NT)

        for hf in range(2):
            gw0 = cx.c("GW")
            sp60 = cx.c("SP6")
            se60 = cx.c("SE6")
            sw20 = cx.c("SW2")
            sy20 = cx.c("SY2")
            with (nc.psum_tensor(un("pH"), [128, 2 * 512], F32) as pH,
                  nc.psum_tensor(un("pY2"), [128, 2 * 512], F32) as pY2):
                with nc.Block() as blk:
                    @blk.vector
                    def _(v):
                        v.memset(yacc[:, :], 0.0)
                        for ei in range(8):
                            for wh in range(2):
                                for i4 in range(16):
                                    tt, dh = i4 // 2, i4 % 2
                                    pp = i4 % 2
                                    idx = 32 * ei + 16 * wh + i4
                                    v.wait_ge(cx.s("SW2"), sw20 + idx + 1)
                                    dst = yacc[:, 1024 * tt + 512 * dh:
                                               1024 * tt + 512 * (dh + 1)]
                                    v.tensor_add(dst, dst,
                                                 pY2[:, 512 * pp:512 * (pp + 1)]
                                                 ).then_inc(cx.s("SY2"), 1)

                    @blk.gpsimd
                    def _(g):
                        for ei in range(8):
                            for hc in range(16):
                                g.wait_ge(cx.s("SP6"),
                                          sp60 + max(0, (ei * 16 + hc - 1) * 2))
                                g.dma_start(
                                    out=_ap(w1_sb, 1024 * (hc % 2),
                                            [[2 * 1024, 128], [128, 8], [1, 128]]),
                                    in_=_ap(din["fw1"], ei * C * 2048 + 128 * hc,
                                            [[2048, 128], [128 * 2048, 8], [1, 128]]),
                                ).then_inc(cx.s("GW"), 16)
                                cx.add("GW", 16)
                            for wh in range(2):
                                g.wait_ge(cx.s("SW2"), sw20 + 32 * ei + 16 * wh)
                                g.dma_start(
                                    out=_ap(w2_sb, 0,
                                            [[8 * 1024, 128], [1024, 8], [1, 1024]]),
                                    in_=_ap(din["fw2"],
                                            ei * 2048 * C + wh * 1024 * C,
                                            [[1024, 128], [128 * 1024, 8], [1, 1024]]),
                                ).then_inc(cx.s("GW"), 16)
                                cx.add("GW", 16)

                    @blk.tensor
                    def _(te):
                        for ei in range(8):
                            for hc in range(16):
                                te.wait_ge(cx.s("GW"), gw0 + 288 * ei + 16 * (hc + 1))
                                for s2 in range(2):
                                    i3 = 2 * hc + s2
                                    pp = i3 % 2
                                    gidx = 32 * ei + i3
                                    if gidx >= 2:
                                        te.wait_ge(cx.s("SE6"), se60 + gidx - 1)
                                    s = 2 * hf + s2
                                    ins = None
                                    for r in range(8):
                                        ins = te.matmul(
                                            pH[:, 512 * pp:512 * (pp + 1)],
                                            w1_sb[:, 1024 * (hc % 2) + 128 * r:
                                                  1024 * (hc % 2) + 128 * (r + 1)],
                                            afm[:, 2048 * r + 512 * s:
                                                2048 * r + 512 * (s + 1)],
                                            start=(r == 0), stop=(r == 7))
                                    ins.then_inc(cx.s("SP6"), 1)
                            te.wait_ge(cx.s("SE6"), se60 + 32 * (ei + 1))
                            for wh in range(2):
                                te.wait_ge(cx.s("GW"), gw0 + 288 * ei + 256
                                           + 16 * (wh + 1))
                                for i4 in range(16):
                                    tt, dh = i4 // 2, i4 % 2
                                    pp = i4 % 2
                                    idx = 32 * ei + 16 * wh + i4
                                    if idx >= 2:
                                        te.wait_ge(cx.s("SY2"), sy20 + idx - 1)
                                    t = 8 * hf + tt
                                    ins = None
                                    for h8 in range(8):
                                        hc = 8 * wh + h8
                                        ins = te.matmul(
                                            pY2[:, 512 * pp:512 * (pp + 1)],
                                            g_sb[:, 1024 * hc + 128 * tt:
                                                 1024 * hc + 128 * (tt + 1)],
                                            w2_sb[:, 1024 * h8 + 512 * dh:
                                                  1024 * h8 + 512 * (dh + 1)],
                                            start=(h8 == 0), stop=(h8 == 7))
                                    ins.then_inc(cx.s("SW2"), 1)

                    @blk.scalar
                    def _(a):
                        for ei in range(8):
                            for i3 in range(32):
                                hc, s2 = i3 // 2, i3 % 2
                                pp = i3 % 2
                                gidx = 32 * ei + i3
                                a.wait_ge(cx.s("SP6"), sp60 + gidx + 1)
                                a.activation(
                                    g_sb[:, 1024 * hc + 512 * s2:
                                         1024 * hc + 512 * (s2 + 1)],
                                    pH[:, 512 * pp:512 * (pp + 1)], AF.Gelu,
                                    bias=b1T[:, 8 * hc + ei:8 * hc + ei + 1]
                                ).then_inc(cx.s("SE6"), 1)
                cx.add("SP6", 32 * 8)
                cx.add("SE6", 32 * 8)
                cx.add("SW2", 32 * 8)
                cx.add("SY2", 32 * 8)
                cx.add("GW", 0)

            sa0, sv0, sr0 = cx.c("SA"), cx.c("SV"), cx.c("SR")
            with nc.Block() as blk:
                @blk.scalar
                def _(a):
                    for tt in range(8):
                        a.wait_ge(cx.s("SFN"), cx.c("SFN") + tt + 1)
                        ln_act_tile(a, ysc2[:, :], tt, sqs[:, :], sv0, eps_ln[:, :])

                @blk.vector
                def _(v):
                    for tt in range(8):
                        t = 8 * hf + tt
                        b = 8 * (tt % 2)
                        fence(v, v.tensor_add(ysc2[:, :],
                                              yacc[:, 1024 * tt:1024 * (tt + 1)],
                                              b2rep[:, :]), "FV")
                        fence(v, v.tensor_scalar_mul(ysc2[:, :], ysc2[:, :],
                                                     normg[:, t:t + 1]), "FV")
                        v.engine_nop().then_inc(cx.s("SFN"), 1)
                        ln_dve_stats(v, ysc2[:, :], tt, sa0)
                        v.wait_ge(cx.s("SR"), sr0 + tt + 1)
                        fence(v, v.tensor_scalar(lnap2[:, :], ysc2[:, :],
                                                 stat[:, b + 2:b + 3],
                                                 stat[:, b + 4:b + 5],
                                                 ALU.subtract, ALU.mult), "FV")
                        fence(v, v.tensor_add(resid[:, C * t:C * (t + 1)],
                                              resid[:, C * t:C * (t + 1)],
                                              lnap2[:, :]), "FV")
            cx.add("SA", 8)
            cx.add("SV", 8)
            cx.add("SR", 8)
            cx.add("SFN", 8)

    def emit_replicate_rows(rows, ld_wait=None):
        with nc.psum_tensor(un("prep"), [128, 512], F32) as prep:
            w0 = cx.c("SW")
            y0 = cx.c("SY")
            with nc.Block() as blk:
                @blk.tensor
                def _(te):
                    if ld_wait is not None:
                        te.wait_ge(cx.s(LD), ld_wait)
                    i = 0
                    for (_, srow, drep, wid) in rows:
                        for dh in range((wid + 511) // 512):
                            w = min(512, wid - 512 * dh)
                            if i >= 1:
                                te.wait_ge(cx.s("SY"), y0 + i)
                            te.matmul(prep[:, 0:w], ones_f[0:1, 0:128],
                                      srow[0:1, 512 * dh:512 * dh + w],
                                      start=True, stop=True).then_inc(cx.s("SW"), 1)
                            i += 1

                @blk.vector
                def _(v):
                    i = 0
                    for (_, srow, drep, wid) in rows:
                        for dh in range((wid + 511) // 512):
                            w = min(512, wid - 512 * dh)
                            v.wait_ge(cx.s("SW"), w0 + i + 1)
                            v.tensor_copy(drep[:, 512 * dh:512 * dh + w],
                                          prep[:, 0:w]).then_inc(cx.s("SY"), 1)
                            i += 1
            n = sum((wid + 511) // 512 for (_, _, _, wid) in rows)
            cx.add("SW", n)
            cx.add("SY", n)

    def emit_norm_block(kind, afm=None, abf=None, sqs=None, ptr=None,
                        gain_rep=None, write_stream=False, eps_tile=None,
                        pre_wait=None):
        sa0, sv0, sr0, sb0, sc0, sd0 = (cx.c(x) for x in
                                        ("SA", "SV", "SR", "SB", "SC", "SD"))
        with nc.Block() as blk:
            @blk.scalar
            def _(a):
                for t in range(NT):
                    b = 8 * (t % 2)
                    a.activation(sqs[:, :], resid[:, C * t:C * (t + 1)], AF.Square,
                                 accum_out=stat[:, b + 1:b + 2]).then_inc(cx.s("SA"), 1)
                    a.wait_ge(cx.s("SV"), sv0 + t + 1)
                    fence(a, a.activation(stat[:, b + 5:b + 6], stat[:, b + 3:b + 4],
                                          AF.Ln, bias=eps_tile[:, :]), "FA")
                    a.activation(stat[:, b + 4:b + 5], stat[:, b + 5:b + 6], AF.Exp,
                                 scale=-0.5).then_inc(cx.s("SR"), 1)

            @blk.vector
            def _(v):
                if pre_wait is not None:
                    v.wait_ge(cx.s(pre_wait[0]), pre_wait[1])
                for t in range(NT):
                    b = 8 * (t % 2)
                    xt = resid[:, C * t:C * (t + 1)]
                    v.wait_ge(cx.s("SA"), sa0 + t + 1)
                    if kind == "ln":
                        fence(v, v.reduce_sum(stat[:, b:b + 1], xt, AX), "FV")
                        fence(v, v.tensor_scalar_mul(stat[:, b + 2:b + 3],
                                                     stat[:, b:b + 1], 1.0 / C), "FV")
                        fence(v, v.tensor_scalar_mul(stat[:, b + 5:b + 6],
                                                     stat[:, b + 1:b + 2], 1.0 / C), "FV")
                        fence(v, v.tensor_tensor(stat[:, b + 3:b + 4],
                                                 stat[:, b + 2:b + 3],
                                                 stat[:, b + 2:b + 3], ALU.mult), "FV")
                        v.tensor_sub(stat[:, b + 3:b + 4], stat[:, b + 5:b + 6],
                                     stat[:, b + 3:b + 4]).then_inc(cx.s("SV"), 1)
                    else:
                        fence(v, v.tensor_scalar_mul(stat[:, b + 3:b + 4],
                                                     stat[:, b + 1:b + 2], 1.0 / C), "FV")
                        v.engine_nop().then_inc(cx.s("SV"), 1)
                    v.wait_ge(cx.s("SR"), sr0 + t + 1)
                    if write_stream:
                        fence(v, v.tensor_scalar(xt, xt, stat[:, b + 2:b + 3],
                                                 stat[:, b + 4:b + 5],
                                                 ALU.subtract, ALU.mult), "FV")
                    if afm is not None:
                        dst = abf[:, C * (t % 2):C * (t % 2 + 1)]
                        if kind == "ln" and not write_stream:
                            v.tensor_scalar(dst, xt, stat[:, b + 2:b + 3],
                                            stat[:, b + 4:b + 5], ALU.subtract,
                                            ALU.mult).then_inc(cx.s("SB"), 1)
                        elif kind == "rms":
                            v.scalar_tensor_tensor(dst, xt, stat[:, b + 4:b + 5],
                                                   gain_rep[:, :], ALU.mult,
                                                   ALU.mult).then_inc(cx.s("SB"), 1)
                        else:
                            v.tensor_copy(dst, xt).then_inc(cx.s("SB"), 1)
                        v.wait_ge(cx.s("SC"), sc0 + t + 1)
                        v.tensor_copy(
                            _ap(afm, 128 * t, [[8 * 2048, 128], [2048, 8], [1, 128]]),
                            _ap(ptr, 0, [[1024, 128], [128, 8], [1, 128]]),
                        ).then_inc(cx.s("SD"), 1)

            if afm is not None:
                @blk.tensor
                def _(te):
                    for t in range(NT):
                        pp = t % 2
                        te.wait_ge(cx.s("SB"), sb0 + t + 1)
                        if t >= 1:
                            te.wait_ge(cx.s("SD"), sd0 + t)
                        ins = None
                        for r in range(8):
                            ins = te.transpose(
                                ptr[:, 128 * r:128 * (r + 1)],
                                abf[:, C * pp + 128 * r:C * pp + 128 * (r + 1)],
                                ident[:, :])
                        ins.then_inc(cx.s("SC"), 1)
        for nm in ("SA", "SV", "SR"):
            cx.add(nm, NT)
        if afm is not None:
            for nm in ("SB", "SC", "SD"):
                cx.add(nm, NT)

    def phase_out():
        ld_f = cx.c(LD)
        with nc.Block() as blk:
            @blk.sync
            def _(s):
                s.wait_ge(cx.s(LD), ld_f)
                s.dma_start(
                    out=_ap(out_t, 0, [[C, 128], [128 * C, NT], [1, C]]),
                    in_=_ap(resid, 0, [[NT * C, 128], [C, NT], [1, C]]),
                ).then_inc(cx.s(ST), 16)
                s.wait_ge(cx.s(ST), cx.add(ST, 16))

    stage = os.environ.get("KSTAGE", "full")
    start = int(os.environ.get("KSTART", "0"))

    def chain_l0attn():
        with contextlib.ExitStack() as ph:
            phase_l0attn_pre(ph)
        if stage == "pre":
            return
        with contextlib.ExitStack() as ph:
            phase_attn_core(ph, rk=32, causal=True, scale=float(32 ** -0.5),
                            xqk=XQK0, xv=XV0, xatt=XATT0,
                            k_qq=(OI["qq0_ev"], OI["qq0_od"]),
                            k_kk=(OI["kk0_ev"], OI["kk0_od"]),
                            k_vv=(OI["v0_ev"], OI["v0_od"]),
                            k_att_mine=OI["att0_mine"])
        if stage == "attn":
            return
        with contextlib.ExitStack() as ph:
            phase_oproj_sandwich(ph, XATT0, "w_o0", "bo0", "ln_add",
                                 OI["att0_mine_myhalf"], OI["att0_part_myhalf"])

    def chain_l0ffn():
        with contextlib.ExitStack() as ph:
            phase_l0ffn_spill_u(ph)
        with contextlib.ExitStack() as ph:
            phase_l0ffn_pre(ph)
        with contextlib.ExitStack() as ph:
            phase_attn_core(ph, rk=64, causal=True, scale=0.125,
                            xqk=XQK1, xv=XV1, xatt=XATT1,
                            k_qq=(OI["qq1_ev"], OI["qq1_od"]),
                            k_kk=(OI["kk1_ev"], OI["kk1_od"]),
                            k_vv=(OI["v1_ev"], OI["v1_od"]),
                            k_att_mine=OI["att1_mine"])
        with contextlib.ExitStack() as ph:
            phase_oproj_sandwich(ph, XATT1, "w_o2", None, "add",
                                 OI["att1_mine_myhalf"], OI["att1_part_myhalf"])
        with contextlib.ExitStack() as ph:
            phase_l0ffn_moe(ph)
        with contextlib.ExitStack() as ph:
            phase_l0ffn_tail(ph)

    def chain_l1attn():
        with contextlib.ExitStack() as ph:
            phase_l1attn_pre(ph)
        with contextlib.ExitStack() as ph:
            phase_attn_core(ph, rk=64, causal=False, scale=0.125,
                            xqk=XQK2, xv=XV2, xatt=XATT2,
                            k_qq=(OI["qq2_ev"], OI["qq2_od"]),
                            k_kk=(OI["kk2_ev"], OI["kk2_od"]),
                            k_vv=(OI["v2_ev"], OI["v2_od"]),
                            k_att_mine=OI["att2_mine"])
        with contextlib.ExitStack() as ph:
            phase_oproj_sandwich(ph, XATT2, "w_o3", "bo3", "ln_add",
                                 OI["att2_mine_myhalf"], OI["att2_part_myhalf"])

    def chain_l1ffn():
        with contextlib.ExitStack() as ph:
            phase_l1ffn(ph)

    chains = [chain_l0attn, chain_l0ffn, chain_l1attn, chain_l1ffn]
    for i in range(start, min(upto, 4)):
        chains[i]()
    phase_out()

    return nc



def _perm_idx(par, per):
    mine = list(range(8 * par, 8 * par + 8))
    other = list(range(8 * (1 - par), 8 * (1 - par) + 8))
    return [g * per + i for g in (mine + other) for i in range(per)]


def host_inputs(inputs):
    f32 = np.float32
    x = np.asarray(inputs["L0_attn_x"], f32)
    ident = np.eye(128, dtype=ml_dtypes.bfloat16)
    masks = np.zeros((128, 4 * 512), dtype=ml_dtypes.bfloat16)
    dk = np.arange(128)[:, None]
    dq = np.arange(512)[None, :]
    for j in range(4):
        masks[:, 512 * j:512 * (j + 1)] = (dq >= 128 * j + dk).astype(ml_dtypes.bfloat16)

    per_core = []
    for c in range(8):
        par = c % 2
        p4 = _perm_idx(par, 4)
        p8 = _perm_idx(par, 8)
        p64 = _perm_idx(par, 64)
        d = {}
        d["x_own"] = np.ascontiguousarray(x[c // 2, 2048 * par:2048 * (par + 1), :])
        d["ident"] = ident
        d["masks01"] = masks
        wq1t = np.asarray(inputs["L0_attn_Wq1"], f32).T[:, p4]
        wk1t = np.asarray(inputs["L0_attn_Wk1"], f32).T[:, p4]
        d["w_qk1"] = np.ascontiguousarray(np.concatenate([wq1t, wk1t], 1))
        wq2t = np.asarray(inputs["L0_attn_Wq2"], f32).T[:, p8]
        wk2t = np.asarray(inputs["L0_attn_Wk2"], f32).T[:, p8]
        d["w_qk2"] = np.ascontiguousarray(np.concatenate([wq2t, wk2t], 1))
        d["w_v0"] = np.ascontiguousarray(np.asarray(inputs["L0_attn_Wv"], f32).T[:, p64])
        d["w_o0"] = np.ascontiguousarray(np.asarray(inputs["L0_attn_Wo"], f32).T[p64, :])
        d["bo0"] = np.ascontiguousarray(np.asarray(inputs["L0_attn_bo"], f32)[None, :])
        order = [g for g in (list(range(8 * par, 8 * par + 8))
                             + list(range(8 * (1 - par), 8 * (1 - par) + 8)))]
        d["ln1_w"] = np.ascontiguousarray(np.asarray(inputs["L0_ffn_ln1_w"], f32)[None, :])
        d["ln2_w"] = np.ascontiguousarray(np.asarray(inputs["L0_ffn_ln2_w"], f32)[None, :])
        d["w_q1f"] = np.ascontiguousarray(np.asarray(inputs["L0_ffn_Wq"], f32)[:, p64])
        d["w_kv"] = np.ascontiguousarray(np.asarray(inputs["L0_ffn_Wkv"], f32))
        ku = np.asarray(inputs["L0_ffn_Wk_up"], f32)
        vu = np.asarray(inputs["L0_ffn_Wv_up"], f32)
        d["w_kup"] = np.ascontiguousarray(np.concatenate([ku[g] for g in order], 1))
        d["w_vup"] = np.ascontiguousarray(np.concatenate([vu[g] for g in order], 1))
        d["w_o2"] = np.ascontiguousarray(np.asarray(inputs["L0_ffn_Wo"], f32)[p64, :])
        d["w_r"] = np.ascontiguousarray(np.asarray(inputs["L0_ffn_Wr"], f32))
        d["r_bias"] = np.ascontiguousarray(np.asarray(inputs["L0_ffn_r_bias"], f32)[None, :])
        for nm, src_ in (("sw1", "L0_ffn_sw1"), ("sw2", "L0_ffn_sw2"), ("sw3", "L0_ffn_sw3"),
                         ("ew1", "L0_ffn_ew1"), ("ew2", "L0_ffn_ew2"), ("ew3", "L0_ffn_ew3")):
            d[nm] = np.ascontiguousarray(np.asarray(inputs[src_], f32))
        pos = np.arange(2048 * par, 2048 * (par + 1), dtype=np.float64)
        inv = 1.0 / (100000.0 ** (np.arange(0, 64, 2, dtype=np.float64) / 64))
        fr = inv[:, None] * pos[None, :]
        co = np.cos(fr).astype(f32)
        si = np.sin(fr).astype(f32)
        CCb = np.concatenate([co, co, co, co], 0)
        SSb = np.concatenate([-si, si, -si, si], 0)
        d["ccss"] = np.ascontiguousarray(
            np.concatenate([CCb, SSb], 1).astype(ml_dtypes.bfloat16))
        psw = np.zeros((128, 128), dtype=ml_dtypes.bfloat16)
        for m in range(128):
            k_ = m + 32 if (m % 64) < 32 else m - 32
            psw[k_, m] = 1.0
        d["pswap"] = psw
        wkv2 = np.asarray(inputs["L1_attn_Wkv"], f32)
        d["w_q2a"] = np.ascontiguousarray(np.asarray(inputs["L1_attn_Wq"], f32)[:, p64])
        d["w_k2a"] = np.ascontiguousarray(wkv2[:, :C][:, p64])
        d["w_v2a"] = np.ascontiguousarray(wkv2[:, C:][:, p64])
        d["w_o3"] = np.ascontiguousarray(np.asarray(inputs["L1_attn_Wo"], f32)[p64, :])
        d["bo3"] = np.ascontiguousarray(np.asarray(inputs["L1_attn_bo"], f32)[None, :])
        d["w_gate"] = np.ascontiguousarray(np.asarray(inputs["L1_ffn_gate_w"], f32).T)
        d["gate_b"] = np.ascontiguousarray(np.asarray(inputs["L1_ffn_gate_b"], f32)[None, :])
        d["rms_w"] = np.ascontiguousarray(np.asarray(inputs["L1_ffn_rms_w"], f32)[None, :])
        d["fw1"] = np.ascontiguousarray(np.asarray(inputs["L1_ffn_w1"], f32))
        d["fb1"] = np.ascontiguousarray(np.asarray(inputs["L1_ffn_b1"], f32))
        d["fw2"] = np.ascontiguousarray(np.asarray(inputs["L1_ffn_w2"], f32))
        d["fb2"] = np.ascontiguousarray(np.asarray(inputs["L1_ffn_b2"], f32))
        off = np.zeros((1, 64), np.int32)
        ev, od = (c // 2) * 2, (c // 2) * 2 + 1
        for l, rk, R in (("0", 32, 1024), ("1", 64, 2048), ("2", 64, 2048)):
            o_ev = 0 if par == 0 else 8 * rk
            o_od = 0 if par == 1 else 8 * rk
            off[0, OI[f"qk{l}_mine"]] = c * R * QKW
            off[0, OI[f"v{l}_mine"]] = c * 2048 * 1040
            off[0, OI[f"qq{l}_ev"]] = (ev * R + o_ev) * QKW
            off[0, OI[f"qq{l}_od"]] = (od * R + o_od) * QKW
            off[0, OI[f"kk{l}_ev"]] = (ev * R + 16 * rk + o_ev) * QKW
            off[0, OI[f"kk{l}_od"]] = (od * R + 16 * rk + o_od) * QKW
            off[0, OI[f"v{l}_ev"]] = ev * 2048 * 1040 + (0 if par == 0 else 520)
            off[0, OI[f"v{l}_od"]] = od * 2048 * 1040 + (0 if par == 1 else 520)
            off[0, OI[f"att{l}_mine"]] = c * 512 * 4096
            off[0, OI[f"att{l}_mine_myhalf"]] = c * 512 * 4096 + par * 2048
            off[0, OI[f"att{l}_part_myhalf"]] = (c ^ 1) * 512 * 4096 + par * 2048
        d["offs_in"] = off
        per_core.append(d)
    return per_core


LAST_RES = None


def kernel(**inputs):
    upto = int(inputs.pop("_upto", 4))
    in_maps = host_inputs(inputs)
    nc = build_program(upto)
    res = run_bass_kernel_spmd(nc, in_maps, core_ids=list(range(8)))
    global LAST_RES
    LAST_RES = res
    out = np.empty((B, S, C), np.float32)
    for c in range(8):
        par = c % 2
        out[c // 2, 2048 * par:2048 * (par + 1), :] = res.results[c]["out"]
    return out


# version 02
# speedup vs baseline: 1.5014x; 1.5014x over previous
import numpy as np
import ml_dtypes
import contextlib
import concourse.bass as bass
import concourse.mybir as mybir
from concourse.bass_utils import run_bass_kernel_spmd

F32 = mybir.dt.float32
BF16 = mybir.dt.bfloat16
I32 = mybir.dt.int32
AF = mybir.ActivationFunctionType
ALU = mybir.AluOpType
AX = mybir.AxisListType.X

B, S, C, H, HD = 4, 4096, 1024, 16, 64
TOK = 2048
NT = TOK // 128
QKW = 2048

OFF_NAMES = []
for _l in range(3):
    OFF_NAMES += [f"qk{_l}_mine", f"v{_l}_mine",
                  f"qq{_l}_ev", f"qq{_l}_od", f"kk{_l}_ev", f"kk{_l}_od",
                  f"v{_l}_ev", f"v{_l}_od",
                  f"att{_l}_mine", f"att{_l}_mine_myhalf", f"att{_l}_part_myhalf"]
OI = {n: i for i, n in enumerate(OFF_NAMES)}


def _ap(t, off, dims):
    return bass.AP(t, off, [list(d) for d in dims])


class Ctx:
    def __init__(self, nc):
        self.nc = nc
        self.sems = {}
        self.cnt = {}

    def s(self, name):
        if name not in self.sems:
            self.sems[name] = self.nc.ctx.enter_context(self.nc.semaphore(name))
            self.cnt[name] = 0
        return self.sems[name]

    def c(self, name):
        self.s(name)
        return self.cnt[name]

    def add(self, name, amt):
        self.s(name)
        self.cnt[name] += amt
        return self.cnt[name]


def build_program(upto=4):
    import os
    KDBG = set(os.environ.get("KDBG", "").split(",")) - {""}
    nc = bass.Bass("TRN2", target_bir_lowering=False)
    cx = Ctx(nc)
    din = {}
    _uq = [0]

    def un(nm):
        _uq[0] += 1
        return f"{nm}_{_uq[0]}"

    def dbg_dump(name, sb, parts, free, dtype):
        if name not in KDBG:
            return
        dt_ = nc.dram_tensor("dbg_" + name, [parts, free], dtype, kind="ExternalOutput")
        with nc.Block() as blk:
            @blk.sync
            def _(s):
                s.dma_start(out=dt_[:, :], in_=sb[0:parts, 0:free]).then_inc(cx.s("ST"), 16)
                s.wait_ge(cx.s("ST"), cx.add("ST", 16))

    def dram_in(name, shape, dtype=F32):
        din[name] = nc.dram_tensor(name, list(shape), dtype, kind="ExternalInput")
        return din[name]

    x_in = dram_in("x_own", [TOK, C])
    out_t = nc.dram_tensor("out", [TOK, C], F32, kind="ExternalOutput")

    dram_in("ident", [128, 128], BF16)
    dram_in("masks01", [128, 4 * 512], BF16)
    dram_in("offs_in", [1, 64], I32)

    dram_in("w_qk1", [C, 128])
    dram_in("w_qk2", [C, 256])
    dram_in("w_v0", [C, C])
    dram_in("w_o0", [C, C])
    dram_in("bo0", [1, C])
    dram_in("ln1_w", [1, C])
    dram_in("ln2_w", [1, C])
    dram_in("w_q1f", [C, C])
    dram_in("w_kv", [C, 256])
    dram_in("w_kup", [256, C])
    dram_in("w_vup", [256, C])
    dram_in("w_o2", [C, C])
    dram_in("w_r", [C, 8])
    dram_in("r_bias", [1, 8])
    dram_in("sw1", [C, 2048])
    dram_in("sw2", [2048, C])
    dram_in("sw3", [C, 2048])
    dram_in("ew1", [8, C, 2048])
    dram_in("ew2", [8, 2048, C])
    dram_in("ew3", [8, C, 2048])
    dram_in("ccss", [128, 2 * TOK], BF16)
    dram_in("pswap", [128, 128], BF16)
    dram_in("w_q2a", [C, C])
    dram_in("w_k2a", [C, C])
    dram_in("w_v2a", [C, C])
    dram_in("w_o3", [C, C])
    dram_in("bo3", [1, C])
    dram_in("w_gate", [C, 8])
    dram_in("gate_b", [1, 8])
    dram_in("rms_w", [1, 8])
    dram_in("fw1", [8, C, 2048])
    dram_in("fb1", [8, 2048])
    dram_in("fw2", [8, 2048, C])
    dram_in("fb2", [8, C])

    XQK0 = nc.dram_tensor("XQK0", [8 * 1024, QKW], BF16, kind="Internal", addr_space="Shared")
    XV0 = nc.dram_tensor("XV0", [8 * 2048, 1040], BF16, kind="Internal", addr_space="Shared")
    XATT0 = nc.dram_tensor("XATT0", [8 * 512, 4096], BF16, kind="Internal", addr_space="Shared")
    XQK1 = nc.dram_tensor("XQK1", [8 * 2048, QKW], BF16, kind="Internal", addr_space="Shared")
    XV1 = nc.dram_tensor("XV1", [8 * 2048, 1040], BF16, kind="Internal", addr_space="Shared")
    XATT1 = nc.dram_tensor("XATT1", [8 * 512, 4096], BF16, kind="Internal", addr_space="Shared")
    XQK2 = nc.dram_tensor("XQK2", [8 * 2048, QKW], BF16, kind="Internal", addr_space="Shared")
    XV2 = nc.dram_tensor("XV2", [8 * 2048, 1040], BF16, kind="Internal", addr_space="Shared")
    XATT2 = nc.dram_tensor("XATT2", [8 * 512, 4096], BF16, kind="Internal", addr_space="Shared")
    xspill = nc.dram_tensor("xspill", [TOK, C], F32, kind="Internal")
    bars = [(nc.dram_tensor(f"bar_i{i}", [1, 2], F32),
             nc.dram_tensor(f"bar_o{i}", [1, 2], F32)) for i in range(8)]
    nbar = [0]

    es = nc.ctx
    resid = es.enter_context(nc.sbuf_tensor(un("resid"), [128, NT * C], F32))
    ident = es.enter_context(nc.sbuf_tensor(un("ident_sb"), [128, 128], BF16))
    offs = es.enter_context(nc.sbuf_tensor(un("offs_sb"), [1, 64], I32))
    ones_f = es.enter_context(nc.sbuf_tensor(un("ones_f"), [1, 128], F32))
    eps_ln = es.enter_context(nc.sbuf_tensor(un("eps_ln"), [128, 1], F32))
    eps_rms = es.enter_context(nc.sbuf_tensor(un("eps_rms"), [128, 1], F32))
    eps_g = es.enter_context(nc.sbuf_tensor(un("eps_g"), [128, 1], F32))
    stat = es.enter_context(nc.sbuf_tensor(un("stat"), [128, 16], F32))
    barz = es.enter_context(nc.sbuf_tensor(un("barz"), [1, 2], F32))

    LD, GD, ST, CC = "LD", "GD", "ST", "CC"

    with nc.Block() as blk:
        @blk.sync
        def _(s):
            s.dma_start(
                out=_ap(resid, 0, [[NT * C, 128], [C, NT], [1, C]]),
                in_=_ap(x_in, 0, [[C, 128], [128 * C, NT], [1, C]]),
            ).then_inc(cx.s(LD), 16)
            cx.add(LD, 16)
            for nm, dst in (("ident", ident), ("offs_in", offs)):
                s.dma_start(out=dst[:, :], in_=din[nm][:, :]).then_inc(cx.s(LD), 16)
                cx.add(LD, 16)

        @blk.vector
        def _(v):
            v.memset(ones_f[:, :], 1.0)
            v.memset(eps_ln[:, :], 1.0e-5)
            v.memset(eps_rms[:, :], 1.0e-6)
            v.memset(eps_g[:, :], float(np.finfo(np.float32).eps))
            v.memset(barz[:, :], 0.0)

    def barrier(g):
        i = nbar[0]
        nbar[0] += 1
        bi, bo = bars[i]
        g.dma_start(out=bi[:, :], in_=barz[:, :]).then_inc(cx.s(ST), 16)
        g.wait_ge(cx.s(ST), cx.add(ST, 16))
        g.collective_compute("AllReduce", ALU.add, replica_groups=[list(range(8))],
                             ins=[bi.ap().opt()], outs=[bo.ap().opt()]).then_inc(cx.s(CC), 1)
        g.wait_ge(cx.s(CC), cx.add(CC, 1))

    def fence(eng, ins, nm):
        ins.then_inc(cx.s(nm), 1)
        eng.wait_ge(cx.s(nm), cx.add(nm, 1))

    def ln_act_tile(a, src, t, sqscr, sv0, eps_tile):
        b = 8 * (t % 2)
        a.activation(sqscr, src, AF.Square,
                     accum_out=stat[:, b + 1:b + 2]).then_inc(cx.s("SA"), 1)
        a.wait_ge(cx.s("SV"), sv0 + t + 1)
        fence(a, a.activation(stat[:, b + 5:b + 6], stat[:, b + 3:b + 4], AF.Ln,
                              bias=eps_tile), "FA")
        a.activation(stat[:, b + 4:b + 5], stat[:, b + 5:b + 6], AF.Exp,
                     scale=-0.5).then_inc(cx.s("SR"), 1)

    def ln_dve_stats(v, src, t, sa0):
        b = 8 * (t % 2)
        v.wait_ge(cx.s("SA"), sa0 + t + 1)
        fence(v, v.reduce_sum(stat[:, b:b + 1], src, AX), "FV")
        fence(v, v.tensor_scalar_mul(stat[:, b + 2:b + 3], stat[:, b:b + 1], 1.0 / C), "FV")
        fence(v, v.tensor_scalar_mul(stat[:, b + 5:b + 6], stat[:, b + 1:b + 2], 1.0 / C),
              "FV")
        fence(v, v.tensor_tensor(stat[:, b + 3:b + 4], stat[:, b + 2:b + 3],
                                 stat[:, b + 2:b + 3], ALU.mult), "FV")
        v.tensor_sub(stat[:, b + 3:b + 4], stat[:, b + 5:b + 6],
                     stat[:, b + 3:b + 4]).then_inc(cx.s("SV"), 1)

    def phase_l0attn_pre(ph):
        import os
        ksub = int(os.environ.get("KSUB", "3"))
        afm = ph.enter_context(nc.sbuf_tensor(un("afm"), [128, 8 * 2048], BF16))
        abf = ph.enter_context(nc.sbuf_tensor(un("abf"), [128, 2 * C], BF16))
        sqs = ph.enter_context(nc.sbuf_tensor(un("sqs"), [128, C], BF16))
        wqk1 = ph.enter_context(nc.sbuf_tensor(un("wqk1"), [128, 8 * 128], BF16))
        wqk2 = ph.enter_context(nc.sbuf_tensor(un("wqk2"), [128, 8 * 256], BF16))
        wv = ph.enter_context(nc.sbuf_tensor(un("wv"), [128, 8 * 1024], BF16))
        qk1t = ph.enter_context(nc.sbuf_tensor(un("qk1t"), [128, 2 * 128], F32))
        qk2t = ph.enter_context(nc.sbuf_tensor(un("qk2t"), [128, 2 * 256], F32))
        q12t = ph.enter_context(nc.sbuf_tensor(un("q12t"), [128, 2 * 512], BF16))
        k12t = ph.enter_context(nc.sbuf_tensor(un("k12t"), [128, 2 * 512], BF16))
        q12fm = ph.enter_context(nc.sbuf_tensor(un("q12fm"), [128, 4 * 2048], BF16))
        k12fm = ph.enter_context(nc.sbuf_tensor(un("k12fm"), [128, 4 * 2048], BF16))
        vstage = ph.enter_context(nc.sbuf_tensor(un("vstage"), [128, 16 * 1040], BF16))
        ptr = ph.enter_context(nc.psum_tensor(un("ptr"), [128, 1024], BF16))
        pqk = ph.enter_context(nc.psum_tensor(un("pqk"), [128, 2 * 512], F32))
        pv = ph.enter_context(nc.psum_tensor(un("pv"), [128, 4 * 512], F32))
        ptqk = ph.enter_context(nc.psum_tensor(un("ptqk"), [128, 1024], BF16))

        ld_t = cx.c(LD)
        sa0, sv0, sr0, sb0, sc0, sd0 = (cx.c(x) for x in ("SA", "SV", "SR", "SB", "SC", "SD"))
        with nc.Block() as blk:
            @blk.gpsimd
            def _(g):
                for wname, dst, wid in (("w_qk1", wqk1, 128), ("w_qk2", wqk2, 256),
                                        ("w_v0", wv, 1024)):
                    g.dma_start(
                        out=_ap(dst, 0, [[8 * wid, 128], [wid, 8], [1, wid]]),
                        in_=_ap(din[wname], 0, [[wid, 128], [128 * wid, 8], [1, wid]]),
                    ).then_inc(cx.s(GD), 16)
                    cx.add(GD, 16)

            @blk.scalar
            def _(a):
                a.wait_ge(cx.s(LD), ld_t)
                kser = int(os.environ.get("KSER", "0"))
                for t in range(NT):
                    if kser and t >= 1:
                        a.wait_ge(cx.s("SD"), sd0 + t)
                    ln_act_tile(a, resid[:, C * t:C * (t + 1)], t, sqs[:, :], sv0,
                                eps_ln[:, :])

            @blk.vector
            def _(v):
                v.memset(vstage[:, :], 1.0)
                for t in range(NT):
                    b = 8 * (t % 2)
                    xt = resid[:, C * t:C * (t + 1)]
                    ln_dve_stats(v, xt, t, sa0)
                    v.wait_ge(cx.s("SR"), sr0 + t + 1)
                    v.tensor_scalar(abf[:, C * (t % 2):C * (t % 2 + 1)], xt,
                                    stat[:, b + 2:b + 3], stat[:, b + 4:b + 5],
                                    ALU.subtract, ALU.mult).then_inc(cx.s("SB"), 1)
                    v.wait_ge(cx.s("SC"), sc0 + t + 1)
                    v.tensor_copy(
                        _ap(afm, 128 * t, [[8 * 2048, 128], [2048, 8], [1, 128]]),
                        _ap(ptr, 0, [[1024, 128], [128, 8], [1, 128]]),
                    ).then_inc(cx.s("SD"), 1)

            @blk.tensor
            def _(te):
                for t in range(NT):
                    pp = t % 2
                    te.wait_ge(cx.s("SB"), sb0 + t + 1)
                    if t >= 1:
                        te.wait_ge(cx.s("SD"), sd0 + t)
                    ins = None
                    for r in range(8):
                        ins = te.transpose(
                            ptr[:, 128 * r:128 * (r + 1)],
                            abf[:, C * pp + 128 * r:C * pp + 128 * (r + 1)], ident[:, :])
                    ins.then_inc(cx.s("SC"), 1)
        for nm in ("SA", "SV", "SR", "SB", "SC", "SD"):
            cx.add(nm, NT)

        if ksub < 2:
            dbg_dump("afm", afm, 128, 8 * 2048, BF16)
            return
        gd_t = cx.c(GD)
        sp0, sq0, st20 = cx.c("SP"), cx.c("SQ"), cx.c("ST2")
        sm0, sn0 = cx.c("SM"), cx.c("SN")
        with nc.Block() as blk:
            @blk.tensor
            def _(te):
                te.wait_ge(cx.s(GD), gd_t)
                for t in range(NT):
                    pp = t % 2
                    if t >= 2:
                        te.wait_ge(cx.s("SM"), sm0 + 3 * (t - 1))
                    ins = None
                    for r in range(8):
                        lt = afm[:, 2048 * r + 128 * t:2048 * r + 128 * (t + 1)]
                        ins = te.matmul(pqk[:, 512 * pp:512 * pp + 128], lt,
                                        wqk1[:, 128 * r:128 * (r + 1)],
                                        start=(r == 0), stop=(r == 7))
                    ins.then_inc(cx.s("SP"), 1)
                    ins = None
                    for r in range(8):
                        lt = afm[:, 2048 * r + 128 * t:2048 * r + 128 * (t + 1)]
                        ins = te.matmul(pqk[:, 512 * pp + 128:512 * pp + 384], lt,
                                        wqk2[:, 256 * r:256 * (r + 1)],
                                        start=(r == 0), stop=(r == 7))
                    ins.then_inc(cx.s("SP"), 1)
                    for dh in range(2):
                        ins = None
                        for r in range(8):
                            lt = afm[:, 2048 * r + 128 * t:2048 * r + 128 * (t + 1)]
                            ins = te.matmul(pv[:, 512 * (2 * pp + dh):512 * (2 * pp + dh + 1)],
                                            lt,
                                            wv[:, 1024 * r + 512 * dh:1024 * r + 512 * (dh + 1)],
                                            start=(r == 0), stop=(r == 7))
                        ins.then_inc(cx.s("SP"), 1)
                    te.wait_ge(cx.s("SQ"), sq0 + t + 1)
                    if t >= 1:
                        te.wait_ge(cx.s("SN"), sn0 + t)
                    pq = 512 * pp
                    for u in range(4):
                        te.transpose(ptqk[:, 128 * u:128 * (u + 1)],
                                     q12t[:, pq + 128 * u:pq + 128 * (u + 1)], ident[:, :])
                    ins = None
                    for u in range(4):
                        ins = te.transpose(ptqk[:, 512 + 128 * u:512 + 128 * (u + 1)],
                                           k12t[:, pq + 128 * u:pq + 128 * (u + 1)], ident[:, :])
                    ins.then_inc(cx.s("ST2"), 1)

            @blk.vector
            def _(v):
                for t in range(NT):
                    pp = t % 2
                    o1, o2, oq = 128 * pp, 256 * pp, 512 * pp
                    v.wait_ge(cx.s("SP"), sp0 + 4 * t + 2)
                    fence(v, v.tensor_copy(qk1t[:, o1:o1 + 128],
                                           pqk[:, 512 * pp:512 * pp + 128]), "FV")
                    fence(v, v.tensor_copy(qk2t[:, o2:o2 + 256],
                                           pqk[:, 512 * pp + 128:512 * pp + 384]), "FV")
                    v.engine_nop().then_inc(cx.s("SM"), 1)
                    v.tensor_tensor(
                        _ap(q12t, oq, [[2 * 512, 128], [32, 16], [8, 4], [1, 8]]),
                        _ap(qk2t, o2, [[2 * 256, 128], [8, 16], [0, 4], [1, 8]]),
                        _ap(qk1t, o1, [[2 * 128, 128], [4, 16], [1, 4], [0, 8]]),
                        ALU.mult)
                    v.tensor_tensor(
                        _ap(k12t, oq, [[2 * 512, 128], [32, 16], [8, 4], [1, 8]]),
                        _ap(qk2t, o2 + 128, [[2 * 256, 128], [8, 16], [0, 4], [1, 8]]),
                        _ap(qk1t, o1 + 64, [[2 * 128, 128], [4, 16], [1, 4], [0, 8]]),
                        ALU.mult).then_inc(cx.s("SQ"), 1)
                    for dh in range(2):
                        v.wait_ge(cx.s("SP"), sp0 + 4 * t + 3 + dh)
                        v.tensor_copy(
                            _ap(vstage, 1040 * t + 520 * dh,
                                [[16 * 1040, 128], [130, 4], [65, 2], [1, 64]]),
                            _ap(pv, 512 * (2 * pp + dh),
                                [[4 * 512, 128], [128, 4], [64, 2], [1, 64]]),
                        ).then_inc(cx.s("SM"), 1)
                    v.wait_ge(cx.s("ST2"), st20 + t + 1)
                    fence(v, v.tensor_copy(
                        _ap(q12fm, 128 * t, [[4 * 2048, 128], [2048, 4], [1, 128]]),
                        _ap(ptqk, 0, [[1024, 128], [128, 4], [1, 128]])), "FV")
                    v.tensor_copy(
                        _ap(k12fm, 128 * t, [[4 * 2048, 128], [2048, 4], [1, 128]]),
                        _ap(ptqk, 512, [[1024, 128], [128, 4], [1, 128]])).then_inc(cx.s("SN"), 1)
        cx.add("SP", 4 * NT)
        cx.add("SQ", NT)
        cx.add("ST2", NT)
        cx.add("SM", 3 * NT)
        cx.add("SN", NT)

        if ksub < 3:
            return
        sn_t = cx.c("SN")
        ld_t2 = cx.c(LD)
        with nc.Block() as blk:
            @blk.gpsimd
            def _(g):
                g.wait_ge(cx.s("SN"), sn_t)
                g.wait_ge(cx.s(LD), ld_t2)
                with g.register(un("r0")) as r0, g.register(un("r1")) as r1:
                    g.reg_load(r0, offs[0:1, OI["qk0_mine"]:OI["qk0_mine"] + 1])
                    g.dma_start(
                        out=_ap(XQK0, r0, [[QKW, 128], [128 * QKW, 4], [1, 2048]]),
                        in_=_ap(q12fm, 0, [[4 * 2048, 128], [2048, 4], [1, 2048]]),
                    ).then_inc(cx.s(ST), 16)
                    cx.add(ST, 16)
                    g.reg_add(r1, r0, 512 * QKW)
                    g.dma_start(
                        out=_ap(XQK0, r1, [[QKW, 128], [128 * QKW, 4], [1, 2048]]),
                        in_=_ap(k12fm, 0, [[4 * 2048, 128], [2048, 4], [1, 2048]]),
                    ).then_inc(cx.s(ST), 16)
                    cx.add(ST, 16)
                    g.reg_load(r0, offs[0:1, OI["v0_mine"]:OI["v0_mine"] + 1])
                    g.dma_start(
                        out=_ap(XV0, r0, [[1040, 128], [128 * 1040, 16], [1, 1040]]),
                        in_=_ap(vstage, 0, [[16 * 1040, 128], [1040, 16], [1, 1040]]),
                    ).then_inc(cx.s(ST), 16)
                    cx.add(ST, 16)
                g.wait_ge(cx.s(ST), cx.c(ST))
                barrier(g)
        dbg_dump("afm", afm, 128, 8 * 2048, BF16)
        dbg_dump("q12fm", q12fm, 128, 4 * 2048, BF16)
        dbg_dump("k12fm", k12fm, 128, 4 * 2048, BF16)
        dbg_dump("vstage", vstage, 128, 16 * 1040, BF16)

    def phase_attn_core(ph, rk, causal, scale, xqk, xv, xatt,
                        k_qq, k_kk, k_vv, k_att_mine):
        kall = ph.enter_context(nc.sbuf_tensor(un("kall"), [2 * rk, 4 * 4096], BF16))
        qall = ph.enter_context(nc.sbuf_tensor(un("qall"), [2 * rk, 4 * 4096], BF16))
        vallA = ph.enter_context(nc.sbuf_tensor(un("vallA"), [128, 16 * 520], BF16))
        vallB = ph.enter_context(nc.sbuf_tensor(un("vallB"), [128, 16 * 520], BF16))
        att = ph.enter_context(nc.sbuf_tensor(un("att"), [128, 4 * 4096], BF16))
        eTa = ph.enter_context(nc.sbuf_tensor(un("eTa"), [128, 2 * 512], BF16))
        eTb = ph.enter_context(nc.sbuf_tensor(un("eTb"), [128, 2 * 512], BF16))
        zi = ph.enter_context(nc.sbuf_tensor(un("zi"), [1, 2 * 512], F32))
        zr = ph.enter_context(nc.sbuf_tensor(un("zr"), [64, 512], F32))
        pSa = ph.enter_context(nc.psum_tensor(un("pSa"), [128, 2 * 512], F32))
        pSb = ph.enter_context(nc.psum_tensor(un("pSb"), [128, 2 * 512], F32))
        pAVa = ph.enter_context(nc.psum_tensor(un("pAVa"), [128, 512], F32))
        pAVb = ph.enter_context(nc.psum_tensor(un("pAVb"), [128, 512], F32))
        pZA = ph.enter_context(nc.psum_tensor(un("pZA"), [64, 512], F32))
        pZB = ph.enter_context(nc.psum_tensor(un("pZB"), [64, 512], F32))
        masks = ph.enter_context(nc.sbuf_tensor(un("masks_sb"), [128, 4 * 512], BF16))
        ld_m = None
        if causal:
            with nc.Block() as blk0:
                @blk0.sync
                def _(s):
                    s.dma_start(out=masks[:, :], in_=din["masks01"][:, :]).then_inc(
                        cx.s(LD), 16)
                    cx.add(LD, 16)
            ld_m = cx.c(LD)

        iters = []
        for p in range(4):
            for s in range(8):
                iters.append((p, s, (4 * s + 4) if causal else 32))
        nit = len(iters)
        ssb = [0]
        for (_, _, nk) in iters:
            ssb.append(ssb[-1] + nk)
        sdb = [0]
        for (_, s, nk) in iters:
            sdb.append(sdb[-1] + (4 if causal else 0))

        ss0, se0, sav0, sdm0 = cx.c("SS"), cx.c("SE"), cx.c("SAV"), cx.c("SDM")
        szi0, szr0, szc0, snr0 = cx.c("SZI"), cx.c("SZR"), cx.c("SZC"), cx.c("SNR")
        gd0, st0 = cx.c(GD), cx.c(ST)
        gd_all = gd0 + 96
        pS = (pSa, pSb)
        pAV = (pAVa, pAVb)
        eT = (eTa, eTb)
        pZ2 = (pZA, pZB)

        with nc.Block() as blk:
            @blk.scalar
            def _(a):
                with a.register(un("sra")) as ra:
                    for rs in (0, 1):
                        a.reg_load(ra, offs[0:1, k_qq[rs]:k_qq[rs] + 1])
                        a.dma_start(
                            out=_ap(qall, 2048 * rs,
                                    [[4 * 4096, 2 * rk], [4096, 4], [1, 2048]]),
                            in_=_ap(xqk, ra,
                                    [[QKW, 2 * rk], [QKW * 2 * rk, 4], [1, 2048]]),
                        ).then_inc(cx.s(GD), 16)
                        cx.add(GD, 16)
                        a.reg_load(ra, offs[0:1, k_kk[rs]:k_kk[rs] + 1])
                        a.dma_start(
                            out=_ap(kall, 2048 * rs,
                                    [[4 * 4096, 2 * rk], [4096, 4], [1, 2048]]),
                            in_=_ap(xqk, ra,
                                    [[QKW, 2 * rk], [QKW * 2 * rk, 4], [1, 2048]]),
                        ).then_inc(cx.s(GD), 16)
                        cx.add(GD, 16)
                        a.reg_load(ra, offs[0:1, k_vv[rs]:k_vv[rs] + 1])
                        a.dma_start(
                            out=_ap(vallA if rs == 0 else vallB, 0,
                                    [[16 * 520, 128], [520, 16], [1, 520]]),
                            in_=_ap(xv, ra,
                                    [[1040, 128], [128 * 1040, 16], [1, 520]]),
                        ).then_inc(cx.s(GD), 16)
                        cx.add(GD, 16)
                for i, (p, s, nk) in enumerate(iters):
                    for kc in range(nk):
                        for hh in range(2):
                            a.wait_ge(cx.s("SS"), ss0 + 2 * ssb[i] + 2 * kc + hh + 1)
                            if kc >= 2:
                                a.wait_ge(cx.s("SAV"), sav0 + 2 * (ssb[i] + kc - 1))
                            else:
                                a.wait_ge(cx.s("SAV"), sav0 + 2 * ssb[i])
                            a.activation(eT[hh][:, 512 * (kc % 2):512 * (kc % 2 + 1)],
                                         pS[hh][:, 512 * (kc % 2):512 * (kc % 2 + 1)],
                                         AF.Exp, scale=scale).then_inc(cx.s("SE"), 1)

            @blk.gpsimd
            def _(g):
                g.wait_ge(cx.s("SNR"), snr0 + 2 * nit)
                with g.register(un("ga")) as ra:
                    g.reg_load(ra, offs[0:1, k_att_mine:k_att_mine + 1])
                    g.dma_start(
                        out=_ap(xatt, ra, [[4096, 128], [128 * 4096, 4], [1, 4096]]),
                        in_=_ap(att, 0, [[4 * 4096, 128], [4096, 4], [1, 4096]]),
                    ).then_inc(cx.s(ST), 16)
                    cx.add(ST, 16)
                g.wait_ge(cx.s(ST), cx.c(ST))
                barrier(g)

            @blk.tensor
            def _(te):
                te.wait_ge(cx.s(GD), gd_all)

                def emit_av(i, p, nk, kc):
                    dj = kc - (nk - 4) if causal else -1
                    if kc == 0:
                        te.wait_ge(cx.s("SNR"), snr0 + 2 * i)
                    if causal and dj >= 0:
                        te.wait_ge(cx.s("SDM"), sdm0 + 2 * (sdb[i] + dj + 1))
                    else:
                        te.wait_ge(cx.s("SE"), se0 + 2 * (ssb[i] + kc + 1))
                    for hh in range(2):
                        vsrc = vallA if kc < 16 else vallB
                        vcol = 520 * (kc % 16) + 130 * p + 65 * hh
                        te.matmul(pAV[hh][0:65, :],
                                  vsrc[:, vcol:vcol + 65],
                                  eT[hh][:, 512 * (kc % 2):512 * (kc % 2 + 1)],
                                  start=(kc == 0), stop=(kc == nk - 1)).then_inc(
                            cx.s("SAV"), 1)

                def emit_zrep(i):
                    for hh in range(2):
                        te.wait_ge(cx.s("SZI"), szi0 + 2 * i + hh + 1)
                        if i >= 1:
                            te.wait_ge(cx.s("SZC"), szc0 + 2 * (i - 1) + hh + 1)
                        te.matmul(pZ2[hh][:, :], ones_f[0:1, 0:64],
                                  zi[0:1, 512 * hh:512 * (hh + 1)],
                                  start=True, stop=True).then_inc(cx.s("SZR"), 1)

                for i, (p, s, nk) in enumerate(iters):
                    te.wait_ge(cx.s("SE"), se0 + 2 * ssb[i])
                    for kc in range(nk):
                        if kc >= 2:
                            te.wait_ge(cx.s("SE"), se0 + 2 * (ssb[i] + kc - 1))
                        for hh in range(2):
                            te.matmul(pS[hh][:, 512 * (kc % 2):512 * (kc % 2 + 1)],
                                      kall[rk * hh:rk * (hh + 1),
                                           4096 * p + 128 * kc:4096 * p + 128 * (kc + 1)],
                                      qall[rk * hh:rk * (hh + 1),
                                           4096 * p + 512 * s:4096 * p + 512 * (s + 1)],
                                      start=True, stop=True).then_inc(cx.s("SS"), 1)
                        if kc == 1 and i >= 1:
                            emit_zrep(i - 1)
                        if kc >= 1:
                            emit_av(i, p, nk, kc - 1)
                    emit_av(i, p, nk, nk - 1)
                emit_zrep(nit - 1)

            @blk.vector
            def _(v):
                if causal:
                    v.wait_ge(cx.s(LD), ld_m)
                for i, (p, s, nk) in enumerate(iters):
                    if causal:
                        for dj in range(4):
                            kc = nk - 4 + dj
                            for hh in range(2):
                                v.wait_ge(cx.s("SE"), se0 + 2 * ssb[i] + 2 * kc + hh + 1)
                                v.tensor_tensor(
                                    eT[hh][:, 512 * (kc % 2):512 * (kc % 2 + 1)],
                                    eT[hh][:, 512 * (kc % 2):512 * (kc % 2 + 1)],
                                    masks[:, 512 * dj:512 * (dj + 1)],
                                    ALU.mult).then_inc(cx.s("SDM"), 1)
                    v.wait_ge(cx.s("SAV"), sav0 + 2 * ssb[i + 1])
                    for hh in range(2):
                        v.reciprocal(zi[0:1, 512 * hh:512 * (hh + 1)],
                                     pAV[hh][64:65, :]).then_inc(cx.s("SZI"), 1)
                    for hh in range(2):
                        v.wait_ge(cx.s("SZR"), szr0 + 2 * i + hh + 1)
                        if hh == 1:
                            v.wait_ge(cx.s("SNR"), snr0 + 2 * i + 1)
                        v.tensor_copy(zr[:, :], pZ2[hh][:, :]).then_inc(cx.s("SZC"), 1)
                        v.wait_ge(cx.s("SZC"), szc0 + 2 * i + hh + 1)
                        v.tensor_tensor(att[64 * hh:64 * (hh + 1),
                                            4096 * p + 512 * s:4096 * p + 512 * (s + 1)],
                                        pAV[hh][0:64, :],
                                        zr[:, :], ALU.mult).then_inc(cx.s("SNR"), 1)

        cx.add("SS", 2 * ssb[nit])
        cx.add("SE", 2 * ssb[nit])
        cx.add("SAV", 2 * ssb[nit])
        cx.add("SDM", 2 * sdb[nit])
        for nm in ("SZI", "SZR", "SZC", "SNR"):
            cx.add(nm, 2 * nit)
        dbg_dump("att", att, 128, 4 * 4096, BF16)
        dbg_dump("qall", qall, 2 * rk, 4 * 4096, BF16)
        dbg_dump("kall", kall, 2 * rk, 4 * 4096, BF16)
        dbg_dump("vallA", vallA, 128, 16 * 520, BF16)
        dbg_dump("vallB", vallB, 128, 16 * 520, BF16)

    def phase_oproj_sandwich(ph, xatt, w_o_name, bias_name, mode,
                             k_att_mine_myhalf, k_att_part_myhalf):
        oin = ph.enter_context(nc.sbuf_tensor(un("oin"), [128, 8 * 2048], BF16))
        wo = ph.enter_context(nc.sbuf_tensor(un("wo"), [128, 8 * 1024], BF16))
        ysc = ph.enter_context(nc.sbuf_tensor(un("ysc"), [128, 2 * C], F32))
        sqs2 = ph.enter_context(nc.sbuf_tensor(un("sqs2"), [128, C], BF16))
        lnap = ph.enter_context(nc.sbuf_tensor(un("lnap"), [128, C], F32))
        brep = ph.enter_context(nc.sbuf_tensor(un("brep"), [128, C], F32))
        bsb = ph.enter_context(nc.sbuf_tensor(un("bsb"), [1, C], F32))
        py = ph.enter_context(nc.psum_tensor(un("py"), [128, 2 * 512], F32))
        pb = ph.enter_context(nc.psum_tensor(un("pb"), [128, 512], F32))

        with nc.Block() as blk:
            @blk.gpsimd
            def _(g):
                with g.register(un("ra")) as ra:
                    for half, idx in ((0, k_att_mine_myhalf), (1, k_att_part_myhalf)):
                        g.reg_load(ra, offs[0:1, idx:idx + 1])
                        g.dma_start(
                            out=_ap(oin, 4 * 2048 * half,
                                    [[8 * 2048, 128], [2048, 4], [1, 2048]]),
                            in_=_ap(xatt, ra, [[4096, 128], [128 * 4096, 4], [1, 2048]]),
                        ).then_inc(cx.s(GD), 16)
                        cx.add(GD, 16)
                g.dma_start(
                    out=_ap(wo, 0, [[8 * 1024, 128], [1024, 8], [1, 1024]]),
                    in_=_ap(din[w_o_name], 0, [[1024, 128], [128 * 1024, 8], [1, 1024]]),
                ).then_inc(cx.s(GD), 16)
                cx.add(GD, 16)

            @blk.sync
            def _(s):
                if bias_name is not None:
                    s.dma_start(out=bsb[:, :], in_=din[bias_name][:, :]).then_inc(
                        cx.s(LD), 16)
                    cx.add(LD, 16)
        gd_in = cx.c(GD)
        ld_b = cx.c(LD)

        nb = 2 if bias_name else 0
        sy0, sw0 = cx.c("SY"), cx.c("SW")
        sa0, sv0, sr0 = cx.c("SA"), cx.c("SV"), cx.c("SR")
        with nc.Block() as blk:
            @blk.tensor
            def _(te):
                te.wait_ge(cx.s(GD), gd_in)
                if bias_name is not None:
                    te.wait_ge(cx.s(LD), ld_b)
                    for dh in range(2):
                        te.matmul(pb[:, :], ones_f[0:1, 0:128],
                                  bsb[0:1, 512 * dh:512 * (dh + 1)],
                                  start=True, stop=True).then_inc(cx.s("SW"), 1)
                        te.wait_ge(cx.s("SY"), sy0 + dh + 1)
                for t in range(NT):
                    if t >= 1:
                        te.wait_ge(cx.s("SY"), sy0 + nb + 2 * t)
                    for dh in range(2):
                        ins = None
                        for j in range(8):
                            ins = te.matmul(
                                py[:, 512 * dh:512 * (dh + 1)],
                                oin[:, 2048 * j + 128 * t:2048 * j + 128 * (t + 1)],
                                wo[:, 1024 * j + 512 * dh:1024 * j + 512 * (dh + 1)],
                                start=(j == 0), stop=(j == 7))
                        ins.then_inc(cx.s("SW"), 1)

            @blk.vector
            def _(v):
                if bias_name is not None:
                    for dh in range(2):
                        v.wait_ge(cx.s("SW"), sw0 + dh + 1)
                        v.tensor_copy(brep[:, 512 * dh:512 * (dh + 1)], pb[:, :]).then_inc(
                            cx.s("SY"), 1)
                for t in range(NT):
                    pp = t % 2
                    b = 8 * pp
                    for dh in range(2):
                        v.wait_ge(cx.s("SW"), sw0 + nb + 2 * t + dh + 1)
                        dst = ysc[:, C * pp + 512 * dh:C * pp + 512 * (dh + 1)]
                        if bias_name is not None:
                            v.tensor_add(dst, py[:, 512 * dh:512 * (dh + 1)],
                                         brep[:, 512 * dh:512 * (dh + 1)]).then_inc(
                                cx.s("SY"), 1)
                        else:
                            v.tensor_copy(dst, py[:, 512 * dh:512 * (dh + 1)]).then_inc(
                                cx.s("SY"), 1)
                    yt = ysc[:, C * pp:C * (pp + 1)]
                    v.wait_ge(cx.s("SY"), sy0 + nb + 2 * t + 2)
                    if mode == "add":
                        v.tensor_add(resid[:, C * t:C * (t + 1)],
                                     resid[:, C * t:C * (t + 1)], yt)
                        continue
                    ln_dve_stats(v, yt, t, sa0)
                    v.wait_ge(cx.s("SR"), sr0 + t + 1)
                    fence(v, v.tensor_scalar(lnap[:, :], yt, stat[:, b + 2:b + 3],
                                             stat[:, b + 4:b + 5], ALU.subtract, ALU.mult),
                          "FV")
                    v.tensor_add(resid[:, C * t:C * (t + 1)],
                                 resid[:, C * t:C * (t + 1)], lnap[:, :])

            if mode == "ln_add":
                @blk.scalar
                def _(a):
                    for t in range(NT):
                        a.wait_ge(cx.s("SY"), sy0 + nb + 2 * t + 2)
                        ln_act_tile(a, ysc[:, C * (t % 2):C * (t % 2 + 1)], t, sqs2[:, :],
                                    sv0, eps_ln[:, :])

        cx.add("SW", nb + 2 * NT)
        cx.add("SY", nb + 2 * NT)
        if mode == "ln_add":
            cx.add("SA", NT)
            cx.add("SV", NT)
            cx.add("SR", NT)
        dbg_dump("oin", oin, 128, 8 * 2048, BF16)



    def phase_l0ffn_spill_u(ph):
        sqs = ph.enter_context(nc.sbuf_tensor(un("sqs0"), [128, C], BF16))
        st0 = cx.c(ST)
        with nc.Block() as blk:
            @blk.sync
            def _(s):
                s.dma_start(
                    out=_ap(xspill, 0, [[C, 128], [128 * C, NT], [1, C]]),
                    in_=_ap(resid, 0, [[NT * C, 128], [C, NT], [1, C]]),
                ).then_inc(cx.s(ST), 16)
                cx.add(ST, 16)
        emit_norm_block("ln", afm=None, abf=None, sqs=sqs, ptr=None,
                        write_stream=True, eps_tile=eps_ln,
                        pre_wait=(ST, st0 + 16))

    def phase_l0ffn_pre(ph):
        afm = ph.enter_context(nc.sbuf_tensor(un("afm1"), [128, 8 * 2048], BF16))
        abf = ph.enter_context(nc.sbuf_tensor(un("abf1"), [128, 2 * C], BF16))
        sqs = ph.enter_context(nc.sbuf_tensor(un("sqs1"), [128, C], BF16))
        ptr = ph.enter_context(nc.psum_tensor(un("ptr1"), [128, 1024], BF16))
        ln1row = ph.enter_context(nc.sbuf_tensor(un("ln1row"), [1, C], F32))
        ln1rep = ph.enter_context(nc.sbuf_tensor(un("ln1rep"), [128, C], BF16))
        wq_sb = ph.enter_context(nc.sbuf_tensor(un("wq_sb"), [128, 8 * 1024], BF16))
        wkv_sb = ph.enter_context(nc.sbuf_tensor(un("wkv_sb"), [128, 8 * 256], BF16))
        wkup_sb = ph.enter_context(nc.sbuf_tensor(un("wkup_sb"), [128, 2 * 1024], BF16))
        wvup_sb = ph.enter_context(nc.sbuf_tensor(un("wvup_sb"), [128, 2 * 1024], BF16))
        kv_sb = ph.enter_context(nc.sbuf_tensor(un("kv_sb"), [128, 2 * 2048], BF16))
        qraw = ph.enter_context(nc.sbuf_tensor(un("qraw"), [128, 2 * 512], BF16))
        tmp1 = ph.enter_context(nc.sbuf_tensor(un("tmp1"), [128, 512], F32))
        qstage = ph.enter_context(nc.sbuf_tensor(un("qstage"), [128, 4 * 2048], BF16))
        vstage = ph.enter_context(nc.sbuf_tensor(un("vstage1"), [128, 16 * 1040], BF16))
        ccss_sb = ph.enter_context(nc.sbuf_tensor(un("ccss_sb"), [128, 2 * TOK], BF16))
        psw_sb = ph.enter_context(nc.sbuf_tensor(un("psw_sb"), [128, 128], BF16))

        ld0 = cx.c(LD)
        with nc.Block() as blk:
            @blk.sync
            def _(s):
                for nm, dst in (("ln1_w", ln1row), ("ccss", ccss_sb), ("pswap", psw_sb)):
                    s.dma_start(out=dst[:, :], in_=din[nm][:, :]).then_inc(cx.s(LD), 16)
                    cx.add(LD, 16)

            @blk.gpsimd
            def _(g):
                for nm, dst, wid in (("w_q1f", wq_sb, 1024), ("w_kv", wkv_sb, 256)):
                    g.dma_start(
                        out=_ap(dst, 0, [[8 * wid, 128], [wid, 8], [1, wid]]),
                        in_=_ap(din[nm], 0, [[wid, 128], [128 * wid, 8], [1, wid]]),
                    ).then_inc(cx.s(GD), 16)
                    cx.add(GD, 16)
                for nm, dst in (("w_kup", wkup_sb), ("w_vup", wvup_sb)):
                    g.dma_start(
                        out=_ap(dst, 0, [[2 * 1024, 128], [1024, 2], [1, 1024]]),
                        in_=_ap(din[nm], 0, [[1024, 128], [128 * 1024, 2], [1, 1024]]),
                    ).then_inc(cx.s(GD), 16)
                    cx.add(GD, 16)
        emit_replicate_rows([("ln1", ln1row, ln1rep, C)], ld_wait=ld0 + 16)
        emit_norm_block("rms", afm=afm, abf=abf, sqs=sqs, ptr=ptr,
                        gain_rep=ln1rep, eps_tile=eps_rms)

        gd_w = cx.c(GD)
        sp0 = cx.c("SP")
        sm0 = cx.c("SM3")
        with nc.psum_tensor(un("pkv"), [128, 2 * 512], F32) as pkv:
            with nc.Block() as blk:
                @blk.tensor
                def _(te):
                    te.wait_ge(cx.s(GD), gd_w)
                    for i1 in range(8):
                        lc, s = i1 // 4, i1 % 4
                        pp = i1 % 2
                        if i1 >= 2:
                            te.wait_ge(cx.s("SM3"), sm0 + i1 - 1)
                        ins = None
                        for r in range(8):
                            ins = te.matmul(
                                pkv[:, 512 * pp:512 * (pp + 1)],
                                wkv_sb[:, 256 * r + 128 * lc:256 * r + 128 * (lc + 1)],
                                afm[:, 2048 * r + 512 * s:2048 * r + 512 * (s + 1)],
                                start=(r == 0), stop=(r == 7))
                        ins.then_inc(cx.s("SP"), 1)

                @blk.scalar
                def _(a):
                    for i1 in range(8):
                        lc, s = i1 // 4, i1 % 4
                        pp = i1 % 2
                        a.wait_ge(cx.s("SP"), sp0 + i1 + 1)
                        a.activation(kv_sb[:, 2048 * lc + 512 * s:2048 * lc + 512 * (s + 1)],
                                     pkv[:, 512 * pp:512 * (pp + 1)], AF.Copy,
                                     scale=0.25).then_inc(cx.s("SM3"), 1)
            cx.add("SP", 8)
            cx.add("SM3", 8)

        for tgt, nr, rowbase, stname in (("q", 8, 0, "qst"), ("k", 2, 16 * 64, "kst")):
            sp0 = cx.c("SP2")
            sq0 = cx.c("SQ2")
            st20 = cx.c("ST2")
            sn0 = cx.c("SN2")
            stb0 = cx.c(ST)
            with (nc.psum_tensor(un("pq"), [128, 2 * 512], F32) as pq,
                  nc.psum_tensor(un("psw"), [128, 2 * 512], F32) as psw):
                with nc.Block() as blk:
                    @blk.tensor
                    def _(te):
                        for i2 in range(32):
                            hc, s = i2 // 4, i2 % 4
                            pp = i2 % 2
                            if i2 >= 2:
                                te.wait_ge(cx.s("SQ2"), sq0 + i2 - 1)
                            ins = None
                            for r in range(nr):
                                if tgt == "q":
                                    lt = wq_sb[:, 1024 * r + 128 * hc:
                                               1024 * r + 128 * (hc + 1)]
                                    rt = afm[:, 2048 * r + 512 * s:2048 * r + 512 * (s + 1)]
                                else:
                                    lt = wkup_sb[:, 1024 * r + 128 * hc:
                                                 1024 * r + 128 * (hc + 1)]
                                    rt = kv_sb[:, 2048 * r + 512 * s:2048 * r + 512 * (s + 1)]
                                ins = te.matmul(pq[:, 512 * pp:512 * (pp + 1)], lt, rt,
                                                start=(r == 0), stop=(r == nr - 1))
                            ins.then_inc(cx.s("SP2"), 1)
                            te.wait_ge(cx.s("SQ2"), sq0 + i2 + 1)
                            if i2 >= 2:
                                te.wait_ge(cx.s("SN2"), sn0 + i2 - 1)
                            te.matmul(psw[:, 512 * pp:512 * (pp + 1)], psw_sb[:, :],
                                      qraw[:, 512 * pp:512 * (pp + 1)],
                                      start=True, stop=True).then_inc(cx.s("ST2"), 1)

                    @blk.vector
                    def _(v):
                        for i2 in range(32):
                            hc, s = i2 // 4, i2 % 4
                            pp = i2 % 2
                            v.wait_ge(cx.s("SP2"), sp0 + i2 + 1)
                            if i2 >= 2:
                                v.wait_ge(cx.s("ST2"), st20 + i2 - 1)
                            v.tensor_copy(qraw[:, 512 * pp:512 * (pp + 1)],
                                          pq[:, 512 * pp:512 * (pp + 1)]).then_inc(
                                cx.s("SQ2"), 1)
                            v.wait_ge(cx.s("ST2"), st20 + i2 + 1)
                            if i2 == 16:
                                v.wait_ge(cx.s(ST), stb0 + 16)
                            qsl = qstage[:, 2048 * (hc % 4) + 512 * s:
                                          2048 * (hc % 4) + 512 * (s + 1)]
                            fence(v, v.tensor_tensor(
                                tmp1[:, :], qraw[:, 512 * pp:512 * (pp + 1)],
                                ccss_sb[:, 512 * s:512 * (s + 1)], ALU.mult), "FV")
                            fence(v, v.tensor_tensor(
                                qsl, psw[:, 512 * pp:512 * (pp + 1)],
                                ccss_sb[:, TOK + 512 * s:TOK + 512 * (s + 1)],
                                ALU.mult), "FV")
                            v.tensor_add(qsl, qsl, tmp1[:, :]).then_inc(cx.s("SN2"), 1)

                    @blk.gpsimd
                    def _(g):
                        with g.register(un("rg")) as rg:
                            for half in range(2):
                                g.wait_ge(cx.s("SN2"), sn0 + 16 * (half + 1))
                                g.reg_load(rg, offs[0:1, OI["qk1_mine"]:
                                                 OI["qk1_mine"] + 1])
                                g.reg_add(rg, rg, (rowbase + 512 * half) * QKW)
                                g.dma_start(
                                    out=_ap(XQK1, rg,
                                            [[QKW, 128], [128 * QKW, 4], [1, 2048]]),
                                    in_=_ap(qstage, 0,
                                            [[4 * 2048, 128], [2048, 4], [1, 2048]]),
                                ).then_inc(cx.s(ST), 16)
                                cx.add(ST, 16)
                cx.add("SP2", 32)
                cx.add("SQ2", 32)
                cx.add("ST2", 32)
                cx.add("SN2", 32)

        sp0 = cx.c("SP4")
        sm0 = cx.c("SM4")
        with nc.psum_tensor(un("pv1"), [128, 4 * 512], F32) as pv:
            with nc.Block() as blk:
                @blk.tensor
                def _(te):
                    for i4 in range(32):
                        t, dh = i4 // 2, i4 % 2
                        s4 = i4 % 4
                        if i4 >= 4:
                            te.wait_ge(cx.s("SM4"), sm0 + i4 - 3)
                        ins = None
                        for lc in range(2):
                            ins = te.matmul(
                                pv[:, 512 * s4:512 * (s4 + 1)],
                                kv_sb[:, 2048 * lc + 128 * t:2048 * lc + 128 * (t + 1)],
                                wvup_sb[:, 1024 * lc + 512 * dh:1024 * lc + 512 * (dh + 1)],
                                start=(lc == 0), stop=(lc == 1))
                        ins.then_inc(cx.s("SP4"), 1)

                @blk.vector
                def _(v):
                    v.memset(vstage[:, :], 1.0)
                    for i4 in range(32):
                        t, dh = i4 // 2, i4 % 2
                        s4 = i4 % 4
                        v.wait_ge(cx.s("SP4"), sp0 + i4 + 1)
                        v.tensor_copy(
                            _ap(vstage, 1040 * t + 520 * dh,
                                [[16 * 1040, 128], [130, 4], [65, 2], [1, 64]]),
                            _ap(pv, 512 * s4, [[4 * 512, 128], [128, 4], [64, 2], [1, 64]]),
                        ).then_inc(cx.s("SM4"), 1)

                @blk.gpsimd
                def _(g):
                    g.wait_ge(cx.s("SM4"), sm0 + 32)
                    with g.register(un("rg")) as rg:
                        g.reg_load(rg, offs[0:1, OI["v1_mine"]:OI["v1_mine"] + 1])
                        g.dma_start(
                            out=_ap(XV1, rg, [[1040, 128], [128 * 1040, 16], [1, 1040]]),
                            in_=_ap(vstage, 0, [[16 * 1040, 128], [1040, 16], [1, 1040]]),
                        ).then_inc(cx.s(ST), 16)
                        cx.add(ST, 16)
                    g.wait_ge(cx.s(ST), cx.c(ST))
                    barrier(g)
            cx.add("SP4", 32)
            cx.add("SM4", 32)

    def phase_l0ffn_moe(ph):
        afm = ph.enter_context(nc.sbuf_tensor(un("afm2"), [128, 8 * 2048], BF16))
        abf = ph.enter_context(nc.sbuf_tensor(un("abf2"), [128, 2 * C], BF16))
        sqs = ph.enter_context(nc.sbuf_tensor(un("sqs2b"), [128, C], BF16))
        ptr = ph.enter_context(nc.psum_tensor(un("ptr2"), [128, 1024], BF16))
        ln2row = ph.enter_context(nc.sbuf_tensor(un("ln2row"), [1, C], F32))
        ln2rep = ph.enter_context(nc.sbuf_tensor(un("ln2rep"), [128, C], BF16))
        rbrow = ph.enter_context(nc.sbuf_tensor(un("rbrow"), [1, 8], F32))
        rbrep = ph.enter_context(nc.sbuf_tensor(un("rbrep"), [128, 8], F32))
        wr_sb = ph.enter_context(nc.sbuf_tensor(un("wr_sb"), [128, 8 * 8], BF16))
        coef = ph.enter_context(nc.sbuf_tensor(un("coef"), [128, NT * 10], F32))
        lgt = ph.enter_context(nc.sbuf_tensor(un("lgt"), [128, 2 * 8], F32))
        probs = ph.enter_context(nc.sbuf_tensor(un("probs"), [128, 2 * 8], F32))
        rsc = ph.enter_context(nc.sbuf_tensor(un("rsc"), [128, 8], F32))
        g_sb = ph.enter_context(nc.sbuf_tensor(un("g_sb"), [128, 16 * 2048], BF16))
        s1 = ph.enter_context(nc.sbuf_tensor(un("s1"), [128, 2 * 512], BF16))
        w1_sb = ph.enter_context(nc.sbuf_tensor(un("w1_sb"), [128, 2 * 8 * 128], BF16))
        w3_sb = ph.enter_context(nc.sbuf_tensor(un("w3_sb"), [128, 2 * 8 * 128], BF16))
        w2_sb = ph.enter_context(nc.sbuf_tensor(un("w2_sb"), [128, 8 * 1024], BF16))

        ld0 = cx.c(LD)
        with nc.Block() as blk:
            @blk.sync
            def _(s):
                for nm, dst in (("ln2_w", ln2row), ("r_bias", rbrow)):
                    s.dma_start(out=dst[:, :], in_=din[nm][:, :]).then_inc(cx.s(LD), 16)
                    cx.add(LD, 16)

            @blk.gpsimd
            def _(g):
                g.dma_start(
                    out=_ap(wr_sb, 0, [[64, 128], [8, 8], [1, 8]]),
                    in_=_ap(din["w_r"], 0, [[8, 128], [1024, 8], [1, 8]]),
                ).then_inc(cx.s(GD), 16)
                cx.add(GD, 16)
        emit_replicate_rows([("ln2", ln2row, ln2rep, C), ("rb", rbrow, rbrep, 8)],
                            ld_wait=ld0 + 32)
        emit_norm_block("rms", afm=afm, abf=abf, sqs=sqs, ptr=ptr,
                        gain_rep=ln2rep, eps_tile=eps_rms)

        gd_r = cx.c(GD)
        sp0, sy0 = cx.c("SP5"), cx.c("SY5")
        sg0, sh0, sy60 = cx.c("SG5"), cx.c("SH5"), cx.c("SY6")
        with nc.psum_tensor(un("pR"), [128, 512], F32) as pR:
            with nc.Block() as blk:
                @blk.tensor
                def _(te):
                    te.wait_ge(cx.s(GD), gd_r)
                    for t in range(NT):
                        if t >= 2:
                            te.wait_ge(cx.s("SY5"), sy0 + t - 1)
                        ins = None
                        for r in range(8):
                            ins = te.matmul(pR[:, 8 * (t % 2):8 * (t % 2) + 8],
                                            afm[:, 2048 * r + 128 * t:
                                                2048 * r + 128 * (t + 1)],
                                            wr_sb[:, 8 * r:8 * (r + 1)],
                                            start=(r == 0), stop=(r == 7))
                        ins.then_inc(cx.s("SP5"), 1)

                @blk.scalar
                def _(a):
                    for t in range(NT):
                        pp = t % 2
                        a.wait_ge(cx.s("SG5"), sg0 + t + 1)
                        if t >= 2:
                            a.wait_ge(cx.s("SY6"), sy60 + t - 1)
                        a.activation(probs[:, 8 * pp:8 * (pp + 1)],
                                     lgt[:, 8 * pp:8 * (pp + 1)],
                                     AF.Sigmoid).then_inc(cx.s("SH5"), 1)

                @blk.vector
                def _(v):
                    for t in range(NT):
                        pp = t % 2
                        lg = lgt[:, 8 * pp:8 * (pp + 1)]
                        pr = probs[:, 8 * pp:8 * (pp + 1)]
                        v.wait_ge(cx.s("SP5"), sp0 + t + 1)
                        fence(v, v.tensor_add(lg, pR[:, 8 * pp:8 * (pp + 1)],
                                              rbrep[:, :]), "FV")
                        v.engine_nop().then_inc(cx.s("SY5"), 1)
                        fence(v, v.tensor_scalar_min(lg, lg, 50.0), "FV")
                        fence(v, v.tensor_scalar_max(lg, lg, -50.0), "FV")
                        v.engine_nop().then_inc(cx.s("SG5"), 1)
                        v.wait_ge(cx.s("SH5"), sh0 + t + 1)
                        fence(v, v.reduce_max(rsc[:, 0:1], pr, AX), "FV")
                        fence(v, v.tensor_scalar(rsc[:, 1:2], rsc[:, 0:1], 1e-8,
                                                 1.0 - 1e-8, ALU.max, ALU.min), "FV")
                        fence(v, v.tensor_scalar(rsc[:, 2:3], rsc[:, 1:2],
                                                 0.5 + 1e-8, 0.5, ALU.add, ALU.max), "FV")
                        fence(v, v.tensor_scalar_min(rsc[:, 2:3], rsc[:, 2:3], 2.0), "FV")
                        fence(v, v.reciprocal(rsc[:, 3:4], rsc[:, 2:3]), "FV")
                        fence(v, v.tensor_tensor(rsc[:, 4:5], rsc[:, 1:2], rsc[:, 3:4],
                                                 ALU.mult), "FV")
                        fence(v, v.tensor_scalar_mul(coef[:, 10 * t + 8:10 * t + 9],
                                                     rsc[:, 3:4], 0.5), "FV")
                        fence(v, v.memset(rsc[:, 5:6], 0.0), "FV")
                        for e in range(8):
                            fence(v, v.tensor_tensor(rsc[:, 7:8], pr[:, e:e + 1],
                                                     rsc[:, 0:1], ALU.is_equal), "FV")
                            fence(v, v.tensor_scalar(rsc[:, 6:7], rsc[:, 5:6], -1.0,
                                                     1.0, ALU.mult, ALU.add), "FV")
                            fence(v, v.tensor_tensor(rsc[:, 6:7], rsc[:, 7:8],
                                                     rsc[:, 6:7], ALU.mult), "FV")
                            fence(v, v.tensor_tensor(coef[:, 10 * t + e:10 * t + e + 1],
                                                     rsc[:, 6:7], rsc[:, 4:5],
                                                     ALU.mult), "FV")
                            fence(v, v.tensor_tensor(rsc[:, 5:6], rsc[:, 5:6],
                                                     rsc[:, 7:8], ALU.max), "FV")
                        v.engine_nop().then_inc(cx.s("SY6"), 1)
            cx.add("SP5", NT)
            cx.add("SY5", NT)
            cx.add("SG5", NT)
            cx.add("SH5", NT)
            cx.add("SY6", NT)

        experts = [("sw1", 0, "sw3", 0, "sw2", 0, 8)]
        for e in range(8):
            experts.append(("ew1", e * C * 2048, "ew3", e * C * 2048,
                            "ew2", e * 2048 * C, e))
        gw0 = cx.c("GW")
        sp60 = cx.c("SP6")
        sm60 = cx.c("SM6")
        se60 = cx.c("SE6")
        sw20 = cx.c("SW2")
        sy20 = cx.c("SY2")
        with (nc.psum_tensor(un("pA"), [128, 2 * 512], F32) as pA,
              nc.psum_tensor(un("pB"), [128, 2 * 512], F32) as pB,
              nc.psum_tensor(un("pY"), [128, 2 * 512], F32) as pY):
            with nc.Block() as blk:
                @blk.gpsimd
                def _(g):
                    for ei, (n1, o1, n3, o3, n2, o2, cc_) in enumerate(experts):
                        hb = ei % 2
                        for hc in range(16):
                            if hc >= 1 or ei >= 1:
                                pass
                            g.wait_ge(cx.s("SP6"),
                                      sp60 + max(0, (ei * 16 + hc - 1) * 4))
                            for nm, off, dst in ((n1, o1, w1_sb), (n3, o3, w3_sb)):
                                g.dma_start(
                                    out=_ap(dst, 1024 * (hc % 2),
                                            [[2 * 1024, 128], [128, 8], [1, 128]]),
                                    in_=_ap(din[nm], off + 128 * hc,
                                            [[2048, 128], [128 * 2048, 8], [1, 128]]),
                                ).then_inc(cx.s("GW"), 16)
                                cx.add("GW", 16)
                        for wh in range(2):
                            g.wait_ge(cx.s("SW2"), sw20 + 64 * ei + 32 * wh)
                            g.dma_start(
                                out=_ap(w2_sb, 0, [[8 * 1024, 128], [1024, 8], [1, 1024]]),
                                in_=_ap(din[n2], o2 + 8 * wh * 128 * 1024,
                                        [[1024, 128], [128 * 1024, 8], [1, 1024]]),
                            ).then_inc(cx.s("GW"), 16)
                            cx.add("GW", 16)

                @blk.tensor
                def _(te):
                    for ei, (n1, o1, n3, o3, n2, o2, cc_) in enumerate(experts):
                        for hc in range(16):
                            te.wait_ge(cx.s("GW"), gw0 + 544 * ei + 32 * (hc + 1))
                            for s in range(4):
                                i3 = 4 * hc + s
                                pp = i3 % 2
                                if i3 >= 2 or ei >= 1:
                                    te.wait_ge(cx.s("SM6"), sm60 + 64 * ei + i3 - 1)
                                ins = None
                                for r in range(8):
                                    ins = te.matmul(
                                        pA[:, 512 * pp:512 * (pp + 1)],
                                        w1_sb[:, 1024 * (hc % 2) + 128 * r:
                                              1024 * (hc % 2) + 128 * (r + 1)],
                                        afm[:, 2048 * r + 512 * s:2048 * r + 512 * (s + 1)],
                                        start=(r == 0), stop=(r == 7))
                                for r in range(8):
                                    ins = te.matmul(
                                        pB[:, 512 * pp:512 * (pp + 1)],
                                        w3_sb[:, 1024 * (hc % 2) + 128 * r:
                                              1024 * (hc % 2) + 128 * (r + 1)],
                                        afm[:, 2048 * r + 512 * s:2048 * r + 512 * (s + 1)],
                                        start=(r == 0), stop=(r == 7))
                                ins.then_inc(cx.s("SP6"), 1)
                        te.wait_ge(cx.s("SM6"), sm60 + 64 * (ei + 1))
                        for wh in range(2):
                            te.wait_ge(cx.s("GW"), gw0 + 544 * ei + 512 + 16 * (wh + 1))
                            for i4 in range(32):
                                t, dh = i4 // 2, i4 % 2
                                pp = i4 % 2
                                idx = 64 * ei + 32 * wh + i4
                                if idx >= 2:
                                    te.wait_ge(cx.s("SY2"), sy20 + idx - 1)
                                ins = None
                                for h8 in range(8):
                                    hc = 8 * wh + h8
                                    ins = te.matmul(
                                        pY[:, 512 * pp:512 * (pp + 1)],
                                        g_sb[:, 2048 * hc + 128 * t:
                                             2048 * hc + 128 * (t + 1)],
                                        w2_sb[:, 1024 * h8 + 512 * dh:
                                              1024 * h8 + 512 * (dh + 1)],
                                        start=(h8 == 0), stop=(h8 == 7))
                                ins.then_inc(cx.s("SW2"), 1)

                @blk.scalar
                def _(a):
                    for ei in range(len(experts)):
                        for i3 in range(64):
                            pp = i3 % 2
                            a.wait_ge(cx.s("SP6"), sp60 + 64 * ei + i3 + 1)
                            if i3 >= 2 or ei >= 1:
                                a.wait_ge(cx.s("SM6"), sm60 + 64 * ei + i3 - 1)
                            a.activation(s1[:, 512 * pp:512 * (pp + 1)],
                                         pA[:, 512 * pp:512 * (pp + 1)],
                                         AF.Silu).then_inc(cx.s("SE6"), 1)

                @blk.vector
                def _(v):
                    for ei, (n1, o1, n3, o3, n2, o2, cc_) in enumerate(experts):
                        if ei >= 1:
                            v.wait_ge(cx.s("SW2"), sw20 + 64 * ei)
                        for i3 in range(64):
                            hc, s = i3 // 4, i3 % 4
                            pp = i3 % 2
                            v.wait_ge(cx.s("SE6"), se60 + 64 * ei + i3 + 1)
                            v.tensor_tensor(
                                g_sb[:, 2048 * hc + 512 * s:2048 * hc + 512 * (s + 1)],
                                s1[:, 512 * pp:512 * (pp + 1)],
                                pB[:, 512 * pp:512 * (pp + 1)],
                                ALU.mult).then_inc(cx.s("SM6"), 1)
                        for wh in range(2):
                            for i4 in range(32):
                                t, dh = i4 // 2, i4 % 2
                                pp = i4 % 2
                                idx = 64 * ei + 32 * wh + i4
                                v.wait_ge(cx.s("SW2"), sw20 + idx + 1)
                                dst = resid[:, C * t + 512 * dh:C * t + 512 * (dh + 1)]
                                v.scalar_tensor_tensor(
                                    dst, pY[:, 512 * pp:512 * (pp + 1)],
                                    coef[:, 10 * t + cc_:10 * t + cc_ + 1], dst,
                                    ALU.mult, ALU.add).then_inc(cx.s("SY2"), 1)
            ne = len(experts)
            cx.add("SP6", 64 * ne)
            cx.add("SE6", 64 * ne)
            cx.add("SM6", 64 * ne)
            cx.add("SW2", 64 * ne)
            cx.add("SY2", 64 * ne)

    def phase_l0ffn_tail(ph):
        xsp = ph.enter_context(nc.sbuf_tensor(un("xsp"), [128, NT * C], F32))
        sqs = ph.enter_context(nc.sbuf_tensor(un("sqs3"), [128, C], BF16))
        lnap = ph.enter_context(nc.sbuf_tensor(un("lnap3"), [128, C], F32))
        ld0 = cx.c(LD)
        with nc.Block() as blk:
            @blk.sync
            def _(s):
                s.dma_start(
                    out=_ap(xsp, 0, [[NT * C, 128], [C, NT], [1, C]]),
                    in_=_ap(xspill, 0, [[C, 128], [128 * C, NT], [1, C]]),
                ).then_inc(cx.s(LD), 16)
                cx.add(LD, 16)
        sa0, sv0, sr0 = cx.c("SA"), cx.c("SV"), cx.c("SR")
        with nc.Block() as blk:
            @blk.scalar
            def _(a):
                for t in range(NT):
                    ln_act_tile(a, resid[:, C * t:C * (t + 1)], t, sqs[:, :], sv0,
                                eps_ln[:, :])

            @blk.vector
            def _(v):
                v.wait_ge(cx.s(LD), ld0 + 16)
                for t in range(NT):
                    b = 8 * (t % 2)
                    yt = resid[:, C * t:C * (t + 1)]
                    ln_dve_stats(v, yt, t, sa0)
                    v.wait_ge(cx.s("SR"), sr0 + t + 1)
                    fence(v, v.tensor_scalar(lnap[:, :], yt, stat[:, b + 2:b + 3],
                                             stat[:, b + 4:b + 5], ALU.subtract,
                                             ALU.mult), "FV")
                    v.tensor_add(yt, xsp[:, C * t:C * (t + 1)], lnap[:, :])
        cx.add("SA", NT)
        cx.add("SV", NT)
        cx.add("SR", NT)


    def phase_l1attn_pre(ph):
        afm = ph.enter_context(nc.sbuf_tensor(un("afm3"), [128, 8 * 2048], BF16))
        abf = ph.enter_context(nc.sbuf_tensor(un("abf3"), [128, 2 * C], BF16))
        sqs = ph.enter_context(nc.sbuf_tensor(un("sqs4"), [128, C], BF16))
        ptr = ph.enter_context(nc.psum_tensor(un("ptr3"), [128, 1024], BF16))
        wa_sb = ph.enter_context(nc.sbuf_tensor(un("wa_sb"), [128, 2 * 8 * 1024], BF16))
        qstage = ph.enter_context(nc.sbuf_tensor(un("qstage3"), [128, 4 * 2048], BF16))
        vstage = ph.enter_context(nc.sbuf_tensor(un("vstage3"), [128, 16 * 1040], BF16))

        emit_norm_block("ln", afm=afm, abf=abf, sqs=sqs, ptr=ptr, eps_tile=eps_ln)

        gw0 = cx.c("GW")
        with nc.Block() as blk:
            @blk.gpsimd
            def _(g):
                for wi, nm in enumerate(("w_q2a", "w_k2a")):
                    g.dma_start(
                        out=_ap(wa_sb, 8192 * (wi % 2),
                                [[2 * 8192, 128], [1024, 8], [1, 1024]]),
                        in_=_ap(din[nm], 0, [[1024, 128], [128 * 1024, 8], [1, 1024]]),
                    ).then_inc(cx.s("GW"), 16)
                    cx.add("GW", 16)

            sp70 = cx.c("SP7")
            sq70 = cx.c("SQ7")
            st70 = cx.c(ST)
            sm70 = cx.c("SM7")
            with (nc.psum_tensor(un("pq3"), [128, 2 * 512], F32) as pq3,
                  nc.psum_tensor(un("pv3"), [128, 4 * 512], F32) as pv3):
                @blk.tensor
                def _(te):
                    for wi in range(2):
                        te.wait_ge(cx.s("GW"), gw0 + 16 * (wi + 1))
                        for i2 in range(32):
                            hc, s = i2 // 4, i2 % 4
                            pp = i2 % 2
                            gi = 32 * wi + i2
                            if gi >= 2:
                                te.wait_ge(cx.s("SQ7"), sq70 + gi - 1)
                            ins = None
                            for r in range(8):
                                ins = te.matmul(
                                    pq3[:, 512 * pp:512 * (pp + 1)],
                                    wa_sb[:, 8192 * (wi % 2) + 1024 * r + 128 * hc:
                                          8192 * (wi % 2) + 1024 * r + 128 * (hc + 1)],
                                    afm[:, 2048 * r + 512 * s:2048 * r + 512 * (s + 1)],
                                    start=(r == 0), stop=(r == 7))
                            ins.then_inc(cx.s("SP7"), 1)
                    te.wait_ge(cx.s("GW"), gw0 + 48)
                    for i4 in range(32):
                        t, dh = i4 // 2, i4 % 2
                        s4 = i4 % 4
                        if i4 >= 4:
                            te.wait_ge(cx.s("SM7"), sm70 + i4 - 3)
                        ins = None
                        for r in range(8):
                            ins = te.matmul(
                                pv3[:, 512 * s4:512 * (s4 + 1)],
                                afm[:, 2048 * r + 128 * t:2048 * r + 128 * (t + 1)],
                                wa_sb[:, 8192 * 0 + 1024 * r + 512 * dh:
                                      8192 * 0 + 1024 * r + 512 * (dh + 1)],
                                start=(r == 0), stop=(r == 7))
                        ins.then_inc(cx.s("SP7"), 1)

                @blk.vector
                def _(v):
                    v.memset(vstage[:, :], 1.0)
                    for wi in range(2):
                        for i2 in range(32):
                            hc, s = i2 // 4, i2 % 4
                            pp = i2 % 2
                            gi = 32 * wi + i2
                            v.wait_ge(cx.s("SP7"), sp70 + gi + 1)
                            if gi >= 16 and i2 < 16:
                                pass
                            if i2 == 0 and wi >= 0:
                                pass
                            if i2 == 16 or (wi == 1 and i2 == 0):
                                v.wait_ge(cx.s(ST), st70 + 16 * (2 * wi + (i2 // 16)))
                            v.tensor_copy(
                                qstage[:, 2048 * (hc % 4) + 512 * s:
                                       2048 * (hc % 4) + 512 * (s + 1)],
                                pq3[:, 512 * pp:512 * (pp + 1)]).then_inc(cx.s("SQ7"), 1)
                    for i4 in range(32):
                        t, dh = i4 // 2, i4 % 2
                        s4 = i4 % 4
                        v.wait_ge(cx.s("SP7"), sp70 + 64 + i4 + 1)
                        v.tensor_copy(
                            _ap(vstage, 1040 * t + 520 * dh,
                                [[16 * 1040, 128], [130, 4], [65, 2], [1, 64]]),
                            _ap(pv3, 512 * s4, [[4 * 512, 128], [128, 4], [64, 2], [1, 64]]),
                        ).then_inc(cx.s("SM7"), 1)

                @blk.gpsimd
                def _(g):
                    with g.register(un("rg")) as rg:
                        for wi in range(2):
                            if wi == 1:
                                g.wait_ge(cx.s("SP7"), sp70 + 32)
                                g.dma_start(
                                    out=_ap(wa_sb, 0,
                                            [[2 * 8192, 128], [1024, 8], [1, 1024]]),
                                    in_=_ap(din["w_v2a"], 0,
                                            [[1024, 128], [128 * 1024, 8], [1, 1024]]),
                                ).then_inc(cx.s("GW"), 16)
                                cx.add("GW", 16)
                            rowbase = 0 if wi == 0 else 16 * 64
                            for half in range(2):
                                g.wait_ge(cx.s("SQ7"), sq70 + 32 * wi + 16 * (half + 1))
                                g.reg_load(rg, offs[0:1, OI["qk2_mine"]:
                                                 OI["qk2_mine"] + 1])
                                g.reg_add(rg, rg, (rowbase + 512 * half) * QKW)
                                g.dma_start(
                                    out=_ap(XQK2, rg,
                                            [[QKW, 128], [128 * QKW, 4], [1, 2048]]),
                                    in_=_ap(qstage, 0,
                                            [[4 * 2048, 128], [2048, 4], [1, 2048]]),
                                ).then_inc(cx.s(ST), 16)
                                cx.add(ST, 16)
                        g.wait_ge(cx.s("SM7"), sm70 + 32)
                        g.reg_load(rg, offs[0:1, OI["v2_mine"]:OI["v2_mine"] + 1])
                        g.dma_start(
                            out=_ap(XV2, rg, [[1040, 128], [128 * 1040, 16], [1, 1040]]),
                            in_=_ap(vstage, 0, [[16 * 1040, 128], [1040, 16], [1, 1040]]),
                        ).then_inc(cx.s(ST), 16)
                        cx.add(ST, 16)
                        g.wait_ge(cx.s(ST), cx.c(ST))
                        barrier(g)
                cx.add("SP7", 96)
                cx.add("SQ7", 64)
                cx.add("SM7", 32)

    def phase_l1ffn(ph):
        afm = ph.enter_context(nc.sbuf_tensor(un("afm4"), [128, 8 * 2048], BF16))
        abf = ph.enter_context(nc.sbuf_tensor(un("abf4"), [128, 2 * C], BF16))
        sqs = ph.enter_context(nc.sbuf_tensor(un("sqs5"), [128, C], BF16))
        ptr = ph.enter_context(nc.psum_tensor(un("ptr4"), [128, 1024], BF16))
        wg_sb = ph.enter_context(nc.sbuf_tensor(un("wg_sb"), [128, 8 * 8], BF16))
        gbrow = ph.enter_context(nc.sbuf_tensor(un("gbrow"), [1, 8], F32))
        gbrep = ph.enter_context(nc.sbuf_tensor(un("gbrep"), [128, 8], F32))
        rwrow = ph.enter_context(nc.sbuf_tensor(un("rwrow"), [1, 8], F32))
        rwrep = ph.enter_context(nc.sbuf_tensor(un("rwrep"), [128, 8], F32))
        b1T = ph.enter_context(nc.sbuf_tensor(un("b1T"), [128, 128], F32))
        b2row = ph.enter_context(nc.sbuf_tensor(un("b2row"), [1, C], F32))
        b2rep = ph.enter_context(nc.sbuf_tensor(un("b2rep"), [128, C], F32))
        identf = ph.enter_context(nc.sbuf_tensor(un("identf"), [128, 128], F32))
        ones8 = ph.enter_context(nc.sbuf_tensor(un("ones8"), [8, 1], F32))
        lgt2 = ph.enter_context(nc.sbuf_tensor(un("lgt2"), [128, 2 * 8], F32))
        em2 = ph.enter_context(nc.sbuf_tensor(un("em2"), [128, 2 * 8], F32))
        rsc2 = ph.enter_context(nc.sbuf_tensor(un("rsc2"), [128, 8], F32))
        normg = ph.enter_context(nc.sbuf_tensor(un("normg"), [128, NT], F32))

        emit_norm_block("ln", afm=afm, abf=abf, sqs=sqs, ptr=ptr, eps_tile=eps_ln)

        sub_es = contextlib.ExitStack()
        b1_sb = sub_es.enter_context(nc.sbuf_tensor(un("b1_sb"), [8, 2048], F32))
        b2_sb = sub_es.enter_context(nc.sbuf_tensor(un("b2_sb"), [8, 1024], F32))
        ld0 = cx.c(LD)
        with nc.Block() as blk:
            @blk.sync
            def _(s):
                for nm, dst in (("gate_b", gbrow), ("rms_w", rwrow),
                                ("fb1", b1_sb), ("fb2", b2_sb)):
                    s.dma_start(out=dst[:, :], in_=din[nm][:, :]).then_inc(cx.s(LD), 16)
                    cx.add(LD, 16)

            @blk.gpsimd
            def _(g):
                g.dma_start(
                    out=_ap(wg_sb, 0, [[64, 128], [8, 8], [1, 8]]),
                    in_=_ap(din["w_gate"], 0, [[8, 128], [1024, 8], [1, 8]]),
                ).then_inc(cx.s(GD), 16)
                cx.add(GD, 16)

            @blk.vector
            def _(v):
                v.memset(ones8[:, :], 1.0)
                v.tensor_copy(identf[:, :], ident[:, :])

        sb80, sy80 = cx.c("SB8"), cx.c("SY8")
        ld1 = cx.c(LD)
        with nc.psum_tensor(un("pT"), [128, 512], F32) as pT:
            with nc.Block() as blk:
                @blk.tensor
                def _(te):
                    te.wait_ge(cx.s(LD), ld1)
                    ins = None
                    for hc in range(16):
                        ins = te.transpose(pT[:, 8 * hc:8 * (hc + 1)],
                                           b1_sb[0:8, 128 * hc:128 * (hc + 1)],
                                           identf[0:8, 0:8])
                    ins.then_inc(cx.s("SB8"), 1)
                    te.wait_ge(cx.s("SY8"), sy80 + 1)
                    for dh in range(2):
                        te.matmul(pT[0:1, 8 * dh:8 * dh + 512] if False else
                                  pT[0:1, 0:512],
                                  ones8[:, :], b2_sb[0:8, 512 * dh:512 * (dh + 1)],
                                  start=True, stop=True).then_inc(cx.s("SB8"), 1)
                        te.wait_ge(cx.s("SY8"), sy80 + 2 + dh)

                @blk.vector
                def _(v):
                    v.wait_ge(cx.s("SB8"), sb80 + 1)
                    v.tensor_copy(b1T[:, :], pT[:, 0:128]).then_inc(cx.s("SY8"), 1)
                    for dh in range(2):
                        v.wait_ge(cx.s("SB8"), sb80 + 2 + dh)
                        v.tensor_copy(b2row[0:1, 512 * dh:512 * (dh + 1)],
                                      pT[0:1, 0:512]).then_inc(cx.s("SY8"), 1)
            cx.add("SB8", 3)
            cx.add("SY8", 3)
        emit_replicate_rows([("b2", b2row, b2rep, C), ("gb", gbrow, gbrep, 8),
                             ("rw", rwrow, rwrep, 8)], ld_wait=ld0 + 64)
        sub_es.close()
        g_sb = ph.enter_context(nc.sbuf_tensor(un("g1_sb"), [128, 16 * 1024], BF16))
        yacc = ph.enter_context(nc.sbuf_tensor(un("yacc"), [128, 8 * 1024], F32))
        w1_sb = ph.enter_context(nc.sbuf_tensor(un("w1f_sb"), [128, 2 * 8 * 128], BF16))
        w2_sb = ph.enter_context(nc.sbuf_tensor(un("w2f_sb"), [128, 8 * 1024], BF16))
        ysc2 = ph.enter_context(nc.sbuf_tensor(un("ysc2"), [128, C], F32))
        lnap2 = ph.enter_context(nc.sbuf_tensor(un("lnap2"), [128, C], F32))


        gd_g = cx.c(GD)
        spg0, syg0, svg0, srg0, swg0, sxg0, szg0 = (cx.c(x) for x in
            ("SPg", "SYg", "SVg", "SRg", "SWg", "SXg", "SZg"))
        with nc.psum_tensor(un("pG"), [128, 512], F32) as pG:
            with nc.Block() as blk:
                @blk.tensor
                def _(te):
                    te.wait_ge(cx.s(GD), gd_g)
                    for t in range(NT):
                        if t >= 2:
                            te.wait_ge(cx.s("SYg"), syg0 + t - 1)
                        ins = None
                        for r in range(8):
                            ins = te.matmul(pG[:, 8 * (t % 2):8 * (t % 2) + 8],
                                            afm[:, 2048 * r + 128 * t:
                                                2048 * r + 128 * (t + 1)],
                                            wg_sb[:, 8 * r:8 * (r + 1)],
                                            start=(r == 0), stop=(r == 7))
                        ins.then_inc(cx.s("SPg"), 1)

                @blk.scalar
                def _(a):
                    for t in range(NT):
                        pp = t % 2
                        a.wait_ge(cx.s("SVg"), svg0 + t + 1)
                        fence(a, a.activation(rsc2[:, 1:2], rsc2[:, 0:1], AF.Ln,
                                              bias=eps_g[:, :]), "FA")
                        a.activation(rsc2[:, 2:3], rsc2[:, 1:2], AF.Exp,
                                     scale=-0.5).then_inc(cx.s("SRg"), 1)
                        a.wait_ge(cx.s("SWg"), swg0 + t + 1)
                        if t >= 2:
                            a.wait_ge(cx.s("SZg"), szg0 + t - 1)
                        a.activation(em2[:, 8 * pp:8 * (pp + 1)],
                                     lgt2[:, 8 * pp:8 * (pp + 1)], AF.Exp,
                                     scale=2.0, bias=rsc2[:, 4:5]).then_inc(
                            cx.s("SXg"), 1)

                @blk.vector
                def _(v):
                    for t in range(NT):
                        pp = t % 2
                        lg = lgt2[:, 8 * pp:8 * (pp + 1)]
                        v.wait_ge(cx.s("SPg"), spg0 + t + 1)
                        fence(v, v.tensor_add(lg, pG[:, 8 * pp:8 * pp + 8],
                                              gbrep[:, :]), "FV")
                        v.engine_nop().then_inc(cx.s("SYg"), 1)
                        fence(v, v.tensor_tensor(rsc2[:, 6:7] if False else rsc2[:, 5:6],
                                                 lg[:, 0:1], lg[:, 0:1], ALU.mult), "FV")
                        fence(v, v.tensor_tensor(em2[:, 8 * pp:8 * (pp + 1)], lg, lg,
                                                 ALU.mult), "FV")
                        fence(v, v.reduce_sum(rsc2[:, 0:1], em2[:, 8 * pp:8 * (pp + 1)],
                                              AX), "FV")
                        fence(v, v.tensor_scalar_mul(rsc2[:, 0:1], rsc2[:, 0:1],
                                                     1.0 / 8), "FV")
                        v.engine_nop().then_inc(cx.s("SVg"), 1)
                        v.wait_ge(cx.s("SRg"), srg0 + t + 1)
                        fence(v, v.tensor_scalar_mul(lg, lg, rsc2[:, 2:3]), "FV")
                        fence(v, v.tensor_tensor(lg, lg, rwrep[:, :], ALU.mult), "FV")
                        fence(v, v.reduce_max(rsc2[:, 3:4], lg, AX), "FV")
                        fence(v, v.tensor_scalar_mul(rsc2[:, 4:5], rsc2[:, 3:4],
                                                     -2.0), "FV")
                        v.engine_nop().then_inc(cx.s("SWg"), 1)
                        v.wait_ge(cx.s("SXg"), sxg0 + t + 1)
                        fence(v, v.reduce_sum(rsc2[:, 5:6],
                                              em2[:, 8 * pp:8 * (pp + 1)], AX), "FV")
                        fence(v, v.tensor_scalar(rsc2[:, 6:7], rsc2[:, 5:6], 1e-6, 1.0,
                                                 ALU.mult, ALU.add), "FV")
                        fence(v, v.reciprocal(normg[:, t:t + 1], rsc2[:, 6:7]), "FV")
                        v.engine_nop().then_inc(cx.s("SZg"), 1)
            for nm in ("SPg", "SYg", "SVg", "SRg", "SWg", "SXg", "SZg"):
                cx.add(nm, NT)

        for hf in range(2):
            gw0 = cx.c("GW")
            sp60 = cx.c("SP6")
            se60 = cx.c("SE6")
            sw20 = cx.c("SW2")
            sy20 = cx.c("SY2")
            with (nc.psum_tensor(un("pH"), [128, 2 * 512], F32) as pH,
                  nc.psum_tensor(un("pY2"), [128, 2 * 512], F32) as pY2):
                with nc.Block() as blk:
                    @blk.vector
                    def _(v):
                        v.memset(yacc[:, :], 0.0)
                        for ei in range(8):
                            for wh in range(2):
                                for i4 in range(16):
                                    tt, dh = i4 // 2, i4 % 2
                                    pp = i4 % 2
                                    idx = 32 * ei + 16 * wh + i4
                                    v.wait_ge(cx.s("SW2"), sw20 + idx + 1)
                                    dst = yacc[:, 1024 * tt + 512 * dh:
                                               1024 * tt + 512 * (dh + 1)]
                                    v.tensor_add(dst, dst,
                                                 pY2[:, 512 * pp:512 * (pp + 1)]
                                                 ).then_inc(cx.s("SY2"), 1)

                    @blk.gpsimd
                    def _(g):
                        for ei in range(8):
                            for hc in range(16):
                                g.wait_ge(cx.s("SP6"),
                                          sp60 + max(0, (ei * 16 + hc - 1) * 2))
                                g.dma_start(
                                    out=_ap(w1_sb, 1024 * (hc % 2),
                                            [[2 * 1024, 128], [128, 8], [1, 128]]),
                                    in_=_ap(din["fw1"], ei * C * 2048 + 128 * hc,
                                            [[2048, 128], [128 * 2048, 8], [1, 128]]),
                                ).then_inc(cx.s("GW"), 16)
                                cx.add("GW", 16)
                            for wh in range(2):
                                g.wait_ge(cx.s("SW2"), sw20 + 32 * ei + 16 * wh)
                                g.dma_start(
                                    out=_ap(w2_sb, 0,
                                            [[8 * 1024, 128], [1024, 8], [1, 1024]]),
                                    in_=_ap(din["fw2"],
                                            ei * 2048 * C + wh * 1024 * C,
                                            [[1024, 128], [128 * 1024, 8], [1, 1024]]),
                                ).then_inc(cx.s("GW"), 16)
                                cx.add("GW", 16)

                    @blk.tensor
                    def _(te):
                        for ei in range(8):
                            for hc in range(16):
                                te.wait_ge(cx.s("GW"), gw0 + 288 * ei + 16 * (hc + 1))
                                for s2 in range(2):
                                    i3 = 2 * hc + s2
                                    pp = i3 % 2
                                    gidx = 32 * ei + i3
                                    if gidx >= 2:
                                        te.wait_ge(cx.s("SE6"), se60 + gidx - 1)
                                    s = 2 * hf + s2
                                    ins = None
                                    for r in range(8):
                                        ins = te.matmul(
                                            pH[:, 512 * pp:512 * (pp + 1)],
                                            w1_sb[:, 1024 * (hc % 2) + 128 * r:
                                                  1024 * (hc % 2) + 128 * (r + 1)],
                                            afm[:, 2048 * r + 512 * s:
                                                2048 * r + 512 * (s + 1)],
                                            start=(r == 0), stop=(r == 7))
                                    ins.then_inc(cx.s("SP6"), 1)
                            te.wait_ge(cx.s("SE6"), se60 + 32 * (ei + 1))
                            for wh in range(2):
                                te.wait_ge(cx.s("GW"), gw0 + 288 * ei + 256
                                           + 16 * (wh + 1))
                                for i4 in range(16):
                                    tt, dh = i4 // 2, i4 % 2
                                    pp = i4 % 2
                                    idx = 32 * ei + 16 * wh + i4
                                    if idx >= 2:
                                        te.wait_ge(cx.s("SY2"), sy20 + idx - 1)
                                    t = 8 * hf + tt
                                    ins = None
                                    for h8 in range(8):
                                        hc = 8 * wh + h8
                                        ins = te.matmul(
                                            pY2[:, 512 * pp:512 * (pp + 1)],
                                            g_sb[:, 1024 * hc + 128 * tt:
                                                 1024 * hc + 128 * (tt + 1)],
                                            w2_sb[:, 1024 * h8 + 512 * dh:
                                                  1024 * h8 + 512 * (dh + 1)],
                                            start=(h8 == 0), stop=(h8 == 7))
                                    ins.then_inc(cx.s("SW2"), 1)

                    @blk.scalar
                    def _(a):
                        for ei in range(8):
                            for i3 in range(32):
                                hc, s2 = i3 // 2, i3 % 2
                                pp = i3 % 2
                                gidx = 32 * ei + i3
                                a.wait_ge(cx.s("SP6"), sp60 + gidx + 1)
                                a.activation(
                                    g_sb[:, 1024 * hc + 512 * s2:
                                         1024 * hc + 512 * (s2 + 1)],
                                    pH[:, 512 * pp:512 * (pp + 1)], AF.Gelu,
                                    bias=b1T[:, 8 * hc + ei:8 * hc + ei + 1]
                                ).then_inc(cx.s("SE6"), 1)
                cx.add("SP6", 32 * 8)
                cx.add("SE6", 32 * 8)
                cx.add("SW2", 32 * 8)
                cx.add("SY2", 32 * 8)
                cx.add("GW", 0)

            sa0, sv0, sr0 = cx.c("SA"), cx.c("SV"), cx.c("SR")
            with nc.Block() as blk:
                @blk.scalar
                def _(a):
                    for tt in range(8):
                        a.wait_ge(cx.s("SFN"), cx.c("SFN") + tt + 1)
                        ln_act_tile(a, ysc2[:, :], tt, sqs[:, :], sv0, eps_ln[:, :])

                @blk.vector
                def _(v):
                    for tt in range(8):
                        t = 8 * hf + tt
                        b = 8 * (tt % 2)
                        fence(v, v.tensor_add(ysc2[:, :],
                                              yacc[:, 1024 * tt:1024 * (tt + 1)],
                                              b2rep[:, :]), "FV")
                        fence(v, v.tensor_scalar_mul(ysc2[:, :], ysc2[:, :],
                                                     normg[:, t:t + 1]), "FV")
                        v.engine_nop().then_inc(cx.s("SFN"), 1)
                        ln_dve_stats(v, ysc2[:, :], tt, sa0)
                        v.wait_ge(cx.s("SR"), sr0 + tt + 1)
                        fence(v, v.tensor_scalar(lnap2[:, :], ysc2[:, :],
                                                 stat[:, b + 2:b + 3],
                                                 stat[:, b + 4:b + 5],
                                                 ALU.subtract, ALU.mult), "FV")
                        fence(v, v.tensor_add(resid[:, C * t:C * (t + 1)],
                                              resid[:, C * t:C * (t + 1)],
                                              lnap2[:, :]), "FV")
            cx.add("SA", 8)
            cx.add("SV", 8)
            cx.add("SR", 8)
            cx.add("SFN", 8)

    def emit_replicate_rows(rows, ld_wait=None):
        with nc.psum_tensor(un("prep"), [128, 512], F32) as prep:
            w0 = cx.c("SW")
            y0 = cx.c("SY")
            with nc.Block() as blk:
                @blk.tensor
                def _(te):
                    if ld_wait is not None:
                        te.wait_ge(cx.s(LD), ld_wait)
                    i = 0
                    for (_, srow, drep, wid) in rows:
                        for dh in range((wid + 511) // 512):
                            w = min(512, wid - 512 * dh)
                            if i >= 1:
                                te.wait_ge(cx.s("SY"), y0 + i)
                            te.matmul(prep[:, 0:w], ones_f[0:1, 0:128],
                                      srow[0:1, 512 * dh:512 * dh + w],
                                      start=True, stop=True).then_inc(cx.s("SW"), 1)
                            i += 1

                @blk.vector
                def _(v):
                    i = 0
                    for (_, srow, drep, wid) in rows:
                        for dh in range((wid + 511) // 512):
                            w = min(512, wid - 512 * dh)
                            v.wait_ge(cx.s("SW"), w0 + i + 1)
                            v.tensor_copy(drep[:, 512 * dh:512 * dh + w],
                                          prep[:, 0:w]).then_inc(cx.s("SY"), 1)
                            i += 1
            n = sum((wid + 511) // 512 for (_, _, _, wid) in rows)
            cx.add("SW", n)
            cx.add("SY", n)

    def emit_norm_block(kind, afm=None, abf=None, sqs=None, ptr=None,
                        gain_rep=None, write_stream=False, eps_tile=None,
                        pre_wait=None):
        sa0, sv0, sr0, sb0, sc0, sd0 = (cx.c(x) for x in
                                        ("SA", "SV", "SR", "SB", "SC", "SD"))
        with nc.Block() as blk:
            @blk.scalar
            def _(a):
                for t in range(NT):
                    b = 8 * (t % 2)
                    a.activation(sqs[:, :], resid[:, C * t:C * (t + 1)], AF.Square,
                                 accum_out=stat[:, b + 1:b + 2]).then_inc(cx.s("SA"), 1)
                    a.wait_ge(cx.s("SV"), sv0 + t + 1)
                    fence(a, a.activation(stat[:, b + 5:b + 6], stat[:, b + 3:b + 4],
                                          AF.Ln, bias=eps_tile[:, :]), "FA")
                    a.activation(stat[:, b + 4:b + 5], stat[:, b + 5:b + 6], AF.Exp,
                                 scale=-0.5).then_inc(cx.s("SR"), 1)

            @blk.vector
            def _(v):
                if pre_wait is not None:
                    v.wait_ge(cx.s(pre_wait[0]), pre_wait[1])
                for t in range(NT):
                    b = 8 * (t % 2)
                    xt = resid[:, C * t:C * (t + 1)]
                    v.wait_ge(cx.s("SA"), sa0 + t + 1)
                    if kind == "ln":
                        fence(v, v.reduce_sum(stat[:, b:b + 1], xt, AX), "FV")
                        fence(v, v.tensor_scalar_mul(stat[:, b + 2:b + 3],
                                                     stat[:, b:b + 1], 1.0 / C), "FV")
                        fence(v, v.tensor_scalar_mul(stat[:, b + 5:b + 6],
                                                     stat[:, b + 1:b + 2], 1.0 / C), "FV")
                        fence(v, v.tensor_tensor(stat[:, b + 3:b + 4],
                                                 stat[:, b + 2:b + 3],
                                                 stat[:, b + 2:b + 3], ALU.mult), "FV")
                        v.tensor_sub(stat[:, b + 3:b + 4], stat[:, b + 5:b + 6],
                                     stat[:, b + 3:b + 4]).then_inc(cx.s("SV"), 1)
                    else:
                        fence(v, v.tensor_scalar_mul(stat[:, b + 3:b + 4],
                                                     stat[:, b + 1:b + 2], 1.0 / C), "FV")
                        v.engine_nop().then_inc(cx.s("SV"), 1)
                    v.wait_ge(cx.s("SR"), sr0 + t + 1)
                    if write_stream:
                        fence(v, v.tensor_scalar(xt, xt, stat[:, b + 2:b + 3],
                                                 stat[:, b + 4:b + 5],
                                                 ALU.subtract, ALU.mult), "FV")
                    if afm is not None:
                        dst = abf[:, C * (t % 2):C * (t % 2 + 1)]
                        if kind == "ln" and not write_stream:
                            v.tensor_scalar(dst, xt, stat[:, b + 2:b + 3],
                                            stat[:, b + 4:b + 5], ALU.subtract,
                                            ALU.mult).then_inc(cx.s("SB"), 1)
                        elif kind == "rms":
                            v.scalar_tensor_tensor(dst, xt, stat[:, b + 4:b + 5],
                                                   gain_rep[:, :], ALU.mult,
                                                   ALU.mult).then_inc(cx.s("SB"), 1)
                        else:
                            v.tensor_copy(dst, xt).then_inc(cx.s("SB"), 1)
                        v.wait_ge(cx.s("SC"), sc0 + t + 1)
                        v.tensor_copy(
                            _ap(afm, 128 * t, [[8 * 2048, 128], [2048, 8], [1, 128]]),
                            _ap(ptr, 0, [[1024, 128], [128, 8], [1, 128]]),
                        ).then_inc(cx.s("SD"), 1)

            if afm is not None:
                @blk.tensor
                def _(te):
                    for t in range(NT):
                        pp = t % 2
                        te.wait_ge(cx.s("SB"), sb0 + t + 1)
                        if t >= 1:
                            te.wait_ge(cx.s("SD"), sd0 + t)
                        ins = None
                        for r in range(8):
                            ins = te.transpose(
                                ptr[:, 128 * r:128 * (r + 1)],
                                abf[:, C * pp + 128 * r:C * pp + 128 * (r + 1)],
                                ident[:, :])
                        ins.then_inc(cx.s("SC"), 1)
        for nm in ("SA", "SV", "SR"):
            cx.add(nm, NT)
        if afm is not None:
            for nm in ("SB", "SC", "SD"):
                cx.add(nm, NT)

    def phase_out():
        ld_f = cx.c(LD)
        with nc.Block() as blk:
            @blk.sync
            def _(s):
                s.wait_ge(cx.s(LD), ld_f)
                s.dma_start(
                    out=_ap(out_t, 0, [[C, 128], [128 * C, NT], [1, C]]),
                    in_=_ap(resid, 0, [[NT * C, 128], [C, NT], [1, C]]),
                ).then_inc(cx.s(ST), 16)
                s.wait_ge(cx.s(ST), cx.add(ST, 16))

    stage = os.environ.get("KSTAGE", "full")
    start = int(os.environ.get("KSTART", "0"))

    def chain_l0attn():
        with contextlib.ExitStack() as ph:
            phase_l0attn_pre(ph)
        if stage == "pre":
            return
        with contextlib.ExitStack() as ph:
            phase_attn_core(ph, rk=32, causal=True, scale=float(32 ** -0.5),
                            xqk=XQK0, xv=XV0, xatt=XATT0,
                            k_qq=(OI["qq0_ev"], OI["qq0_od"]),
                            k_kk=(OI["kk0_ev"], OI["kk0_od"]),
                            k_vv=(OI["v0_ev"], OI["v0_od"]),
                            k_att_mine=OI["att0_mine"])
        if stage == "attn":
            return
        with contextlib.ExitStack() as ph:
            phase_oproj_sandwich(ph, XATT0, "w_o0", "bo0", "ln_add",
                                 OI["att0_mine_myhalf"], OI["att0_part_myhalf"])

    def chain_l0ffn():
        with contextlib.ExitStack() as ph:
            phase_l0ffn_spill_u(ph)
        with contextlib.ExitStack() as ph:
            phase_l0ffn_pre(ph)
        with contextlib.ExitStack() as ph:
            phase_attn_core(ph, rk=64, causal=True, scale=0.125,
                            xqk=XQK1, xv=XV1, xatt=XATT1,
                            k_qq=(OI["qq1_ev"], OI["qq1_od"]),
                            k_kk=(OI["kk1_ev"], OI["kk1_od"]),
                            k_vv=(OI["v1_ev"], OI["v1_od"]),
                            k_att_mine=OI["att1_mine"])
        with contextlib.ExitStack() as ph:
            phase_oproj_sandwich(ph, XATT1, "w_o2", None, "add",
                                 OI["att1_mine_myhalf"], OI["att1_part_myhalf"])
        with contextlib.ExitStack() as ph:
            phase_l0ffn_moe(ph)
        with contextlib.ExitStack() as ph:
            phase_l0ffn_tail(ph)

    def chain_l1attn():
        with contextlib.ExitStack() as ph:
            phase_l1attn_pre(ph)
        with contextlib.ExitStack() as ph:
            phase_attn_core(ph, rk=64, causal=False, scale=0.125,
                            xqk=XQK2, xv=XV2, xatt=XATT2,
                            k_qq=(OI["qq2_ev"], OI["qq2_od"]),
                            k_kk=(OI["kk2_ev"], OI["kk2_od"]),
                            k_vv=(OI["v2_ev"], OI["v2_od"]),
                            k_att_mine=OI["att2_mine"])
        with contextlib.ExitStack() as ph:
            phase_oproj_sandwich(ph, XATT2, "w_o3", "bo3", "ln_add",
                                 OI["att2_mine_myhalf"], OI["att2_part_myhalf"])

    def chain_l1ffn():
        with contextlib.ExitStack() as ph:
            phase_l1ffn(ph)

    chains = [chain_l0attn, chain_l0ffn, chain_l1attn, chain_l1ffn]
    for i in range(start, min(upto, 4)):
        chains[i]()
    phase_out()

    return nc



def _perm_idx(par, per):
    mine = list(range(8 * par, 8 * par + 8))
    other = list(range(8 * (1 - par), 8 * (1 - par) + 8))
    return [g * per + i for g in (mine + other) for i in range(per)]


def host_inputs(inputs):
    f32 = np.float32
    x = np.asarray(inputs["L0_attn_x"], f32)
    ident = np.eye(128, dtype=ml_dtypes.bfloat16)
    masks = np.zeros((128, 4 * 512), dtype=ml_dtypes.bfloat16)
    dk = np.arange(128)[:, None]
    dq = np.arange(512)[None, :]
    for j in range(4):
        masks[:, 512 * j:512 * (j + 1)] = (dq >= 128 * j + dk).astype(ml_dtypes.bfloat16)

    per_core = []
    for c in range(8):
        par = c % 2
        p4 = _perm_idx(par, 4)
        p8 = _perm_idx(par, 8)
        p64 = _perm_idx(par, 64)
        d = {}
        d["x_own"] = np.ascontiguousarray(x[c // 2, 2048 * par:2048 * (par + 1), :])
        d["ident"] = ident
        d["masks01"] = masks
        wq1t = np.asarray(inputs["L0_attn_Wq1"], f32).T[:, p4]
        wk1t = np.asarray(inputs["L0_attn_Wk1"], f32).T[:, p4]
        d["w_qk1"] = np.ascontiguousarray(np.concatenate([wq1t, wk1t], 1))
        wq2t = np.asarray(inputs["L0_attn_Wq2"], f32).T[:, p8]
        wk2t = np.asarray(inputs["L0_attn_Wk2"], f32).T[:, p8]
        d["w_qk2"] = np.ascontiguousarray(np.concatenate([wq2t, wk2t], 1))
        d["w_v0"] = np.ascontiguousarray(np.asarray(inputs["L0_attn_Wv"], f32).T[:, p64])
        d["w_o0"] = np.ascontiguousarray(np.asarray(inputs["L0_attn_Wo"], f32).T[p64, :])
        d["bo0"] = np.ascontiguousarray(np.asarray(inputs["L0_attn_bo"], f32)[None, :])
        order = [g for g in (list(range(8 * par, 8 * par + 8))
                             + list(range(8 * (1 - par), 8 * (1 - par) + 8)))]
        d["ln1_w"] = np.ascontiguousarray(np.asarray(inputs["L0_ffn_ln1_w"], f32)[None, :])
        d["ln2_w"] = np.ascontiguousarray(np.asarray(inputs["L0_ffn_ln2_w"], f32)[None, :])
        d["w_q1f"] = np.ascontiguousarray(np.asarray(inputs["L0_ffn_Wq"], f32)[:, p64])
        d["w_kv"] = np.ascontiguousarray(np.asarray(inputs["L0_ffn_Wkv"], f32))
        ku = np.asarray(inputs["L0_ffn_Wk_up"], f32)
        vu = np.asarray(inputs["L0_ffn_Wv_up"], f32)
        d["w_kup"] = np.ascontiguousarray(np.concatenate([ku[g] for g in order], 1))
        d["w_vup"] = np.ascontiguousarray(np.concatenate([vu[g] for g in order], 1))
        d["w_o2"] = np.ascontiguousarray(np.asarray(inputs["L0_ffn_Wo"], f32)[p64, :])
        d["w_r"] = np.ascontiguousarray(np.asarray(inputs["L0_ffn_Wr"], f32))
        d["r_bias"] = np.ascontiguousarray(np.asarray(inputs["L0_ffn_r_bias"], f32)[None, :])
        for nm, src_ in (("sw1", "L0_ffn_sw1"), ("sw2", "L0_ffn_sw2"), ("sw3", "L0_ffn_sw3"),
                         ("ew1", "L0_ffn_ew1"), ("ew2", "L0_ffn_ew2"), ("ew3", "L0_ffn_ew3")):
            d[nm] = np.ascontiguousarray(np.asarray(inputs[src_], f32))
        pos = np.arange(2048 * par, 2048 * (par + 1), dtype=np.float64)
        inv = 1.0 / (100000.0 ** (np.arange(0, 64, 2, dtype=np.float64) / 64))
        fr = inv[:, None] * pos[None, :]
        co = np.cos(fr).astype(f32)
        si = np.sin(fr).astype(f32)
        CCb = np.concatenate([co, co, co, co], 0)
        SSb = np.concatenate([-si, si, -si, si], 0)
        d["ccss"] = np.ascontiguousarray(
            np.concatenate([CCb, SSb], 1).astype(ml_dtypes.bfloat16))
        psw = np.zeros((128, 128), dtype=ml_dtypes.bfloat16)
        for m in range(128):
            k_ = m + 32 if (m % 64) < 32 else m - 32
            psw[k_, m] = 1.0
        d["pswap"] = psw
        wkv2 = np.asarray(inputs["L1_attn_Wkv"], f32)
        d["w_q2a"] = np.ascontiguousarray(np.asarray(inputs["L1_attn_Wq"], f32)[:, p64])
        d["w_k2a"] = np.ascontiguousarray(wkv2[:, :C][:, p64])
        d["w_v2a"] = np.ascontiguousarray(wkv2[:, C:][:, p64])
        d["w_o3"] = np.ascontiguousarray(np.asarray(inputs["L1_attn_Wo"], f32)[p64, :])
        d["bo3"] = np.ascontiguousarray(np.asarray(inputs["L1_attn_bo"], f32)[None, :])
        d["w_gate"] = np.ascontiguousarray(np.asarray(inputs["L1_ffn_gate_w"], f32).T)
        d["gate_b"] = np.ascontiguousarray(np.asarray(inputs["L1_ffn_gate_b"], f32)[None, :])
        d["rms_w"] = np.ascontiguousarray(np.asarray(inputs["L1_ffn_rms_w"], f32)[None, :])
        d["fw1"] = np.ascontiguousarray(np.asarray(inputs["L1_ffn_w1"], f32))
        d["fb1"] = np.ascontiguousarray(np.asarray(inputs["L1_ffn_b1"], f32))
        d["fw2"] = np.ascontiguousarray(np.asarray(inputs["L1_ffn_w2"], f32))
        d["fb2"] = np.ascontiguousarray(np.asarray(inputs["L1_ffn_b2"], f32))
        off = np.zeros((1, 64), np.int32)
        ev, od = (c // 2) * 2, (c // 2) * 2 + 1
        for l, rk, R in (("0", 32, 1024), ("1", 64, 2048), ("2", 64, 2048)):
            o_ev = 0 if par == 0 else 8 * rk
            o_od = 0 if par == 1 else 8 * rk
            off[0, OI[f"qk{l}_mine"]] = c * R * QKW
            off[0, OI[f"v{l}_mine"]] = c * 2048 * 1040
            off[0, OI[f"qq{l}_ev"]] = (ev * R + o_ev) * QKW
            off[0, OI[f"qq{l}_od"]] = (od * R + o_od) * QKW
            off[0, OI[f"kk{l}_ev"]] = (ev * R + 16 * rk + o_ev) * QKW
            off[0, OI[f"kk{l}_od"]] = (od * R + 16 * rk + o_od) * QKW
            off[0, OI[f"v{l}_ev"]] = ev * 2048 * 1040 + (0 if par == 0 else 520)
            off[0, OI[f"v{l}_od"]] = od * 2048 * 1040 + (0 if par == 1 else 520)
            off[0, OI[f"att{l}_mine"]] = c * 512 * 4096
            off[0, OI[f"att{l}_mine_myhalf"]] = c * 512 * 4096 + par * 2048
            off[0, OI[f"att{l}_part_myhalf"]] = (c ^ 1) * 512 * 4096 + par * 2048
        d["offs_in"] = off
        per_core.append(d)
    return per_core


LAST_RES = None


def kernel(**inputs):
    upto = int(inputs.pop("_upto", 4))
    in_maps = host_inputs(inputs)
    nc = build_program(upto)
    import os as _os
    trace = bool(int(_os.environ.get("KTRACE", "0")))
    res = run_bass_kernel_spmd(nc, in_maps, core_ids=list(range(8)), trace=trace)
    global LAST_RES
    LAST_RES = res
    out = np.empty((B, S, C), np.float32)
    for c in range(8):
        par = c % 2
        out[c // 2, 2048 * par:2048 * (par + 1), :] = res.results[c]["out"]
    return out
